# version 01
# baseline (speedup 1.0000x reference)
import numpy as np
from contextlib import ExitStack
import concourse.bass as bass
import concourse.mybir as mybir

F32 = mybir.dt.float32
F32R = mybir.dt.float32r
BF16 = mybir.dt.bfloat16
ALU = mybir.AluOpType
ACT = mybir.ActivationFunctionType
AXL = mybir.AxisListType

D = 1024
N = 16
RK = 64
NDT = 8
E = 8
R = 4
P = 128

ENGS = ("V", "S", "T", "q0", "q2")
SEMS = ("V", "S", "T", "q0", "q2", "G")
SEM2ENG = {"V": "V", "S": "S", "T": "T", "q0": "q0", "q2": "q2", "G": "q2"}


class Prog:
    def __init__(self):
        self.streams = {e: [] for e in ENGS}
        self.cnt = {s: 0 for s in SEMS}
        self.waited = {c: {p: 0 for p in SEMS} for c in ENGS}

    def emit(self, eng, fn, waits=(), inc=None, sem=None):
        sem = sem or eng
        if inc is None:
            inc = 16 if eng in ("q0", "q2") else 1
        if eng in ("q0", "q2") and sem == eng and self.cnt[eng] > 0:
            self.selfwait(eng)
        if eng == "T" and self.cnt["T"] >= 2:
            self.selfwait("T", ("T", self.cnt["T"] - 2))
        wl = []
        for tok in waits:
            if tok is None:
                continue
            pe, v = tok
            if SEM2ENG[pe] == eng:
                continue
            if v > self.waited[eng][pe]:
                self.waited[eng][pe] = v
                wl.append((pe, v))
        self.cnt[sem] += inc
        self.streams[eng].append((wl, fn, inc, sem))
        return (sem, self.cnt[sem])

    def selfwait(self, eng, tok=None):
        if tok is None:
            tok = (eng, self.cnt[eng])
        pe, v = tok
        if v > self.waited[eng][pe]:
            self.waited[eng][pe] = v
            self.streams[eng].append(([(pe, v)], None, 0, eng))

    def tok(self, sem):
        return (sem, self.cnt[sem])


def build_nc(L=4096, SEG=1024, name="mk"):
    dbg = False
    skip_post = False
    skip_scan = False
    skip_ln = False
    post_level = 4
    assert L % SEG == 0 and SEG % 512 == 0
    NH = L // SEG
    NSUB = SEG // 512
    NLT = SEG // 128
    DGRPS = ((0, 1, 2, 3), (4, 5, 6, 7))

    nc = bass.Bass("TRN2", target_bir_lowering=False, name=name,
                   detect_race_conditions=False)

    u_d = nc.dram_tensor("u", [L, D], F32, kind="ExternalInput").ap()
    alog_d = nc.dram_tensor("A_log", [D, N], F32, kind="ExternalInput").ap()
    wx_d = nc.dram_tensor("W_x", [96, D], F32, kind="ExternalInput").ap()
    wdt_d = nc.dram_tensor("W_dt", [D, RK], F32, kind="ExternalInput").ap()
    bdt_d = nc.dram_tensor("b_dt", [1, D], F32, kind="ExternalInput").ap()
    dpar_d = nc.dram_tensor("D_param", [1, D], F32, kind="ExternalInput").ap()
    wlin_d = nc.dram_tensor("W_lin", [D, D], F32, kind="ExternalInput").ap()
    blin_d = nc.dram_tensor("b_lin", [1, D], F32, kind="ExternalInput").ap()
    gw_d = nc.dram_tensor("gate_w", [E, D], F32, kind="ExternalInput").ap()
    left_d = nc.dram_tensor("left", [E, D, R], F32, kind="ExternalInput").ap()
    right_d = nc.dram_tensor("right", [E * R, D], F32, kind="ExternalInput").ap()
    out_d = nc.dram_tensor("out", [L, D], F32, kind="ExternalOutput").ap()
    knd = "ExternalOutput" if dbg else "Internal"
    dbgscal_d = nc.dram_tensor("dbgscal", [P, 64], F32, kind=knd).ap()
    dbgsq_d = nc.dram_tensor("dbgsq", [P, D], F32, kind=knd).ap()
    bc_d = nc.dram_tensor("bc_scr", [2, 16, L], F32, kind=knd).ap()
    xnt_d = nc.dram_tensor("xnt_scr", [NDT, P, L], F32, kind=knd).ap()
    ylp_d = nc.dram_tensor("ylp_scr", [L, D], F32, kind=knd).ap()

    pr = Prog()
    st = ExitStack()

    def sb(nm, shape, dt):
        return st.enter_context(nc.sbuf_tensor(nm, shape, dt))

    def psum(nm, shape):
        return st.enter_context(nc.psum_tensor(nm, shape, F32))

    with st:
        ident_r = sb("ident_r", [P, P], F32R)
        ident_b = sb("ident_b", [P, P], BF16)
        ident_f = sb("ident_f", [P, P], F32)
        wx_lhsT = sb("wx_lhsT", [P, NDT, 96], F32R)
        wdtT_aug = sb("wdtT_aug", [96, NDT * P], F32R)
        wlinT = sb("wlinT", [P, NDT, D], BF16)
        glT = sb("glT", [P, NDT, 40], BF16)
        right_aug = sb("right_aug", [64, D], F32R)
        acol = sb("acol", [P, NDT * N], F32)
        dcol = sb("dcol", [P, NDT], F32)
        dt_rawT_aug = sb("dt_rawT_aug", [96, SEG], F32R)
        carry = sb("carry", [P, N * NDT], F32)
        dtT = sb("dtT", [P, 4, SEG], F32)
        wT = sb("wT", [P, 4, SEG], F32)
        ybuf = sb("ybuf", [P, 4, SEG], F32)
        abuf = sb("abuf", [P, 4, SEG], F32)
        bybuf = sb("bybuf", [P, 4, SEG], F32)
        bcbuf = [sb(f"bcbuf{i}", [P, 2, SEG], F32) for i in range(2)]
        xstrip = [sb(f"xstrip{i}", [P, SEG], F32) for i in range(2)]
        ubuf = [sb(f"ubuf{i}", [P, D], F32) for i in range(2)]
        xnbuf = [sb(f"xnbuf{i}", [P, D], F32R) for i in range(2)]
        mini = sb("mini", [P, NDT, 512], F32R)
        bcstage = sb("bcstage", [32, 512], F32)
        statbuf = sb("statbuf", [P, 12], F32)
        mvbuf = sb("mvbuf", [P, 2], F32)
        rcol = sb("rcol", [P, 2], F32)
        ystage = sb("ystage", [P, SEG], F32)
        ylt = [sb(f"ylt{i}", [P, D], F32) for i in range(2)]
        u2buf = [sb(f"u2buf{i}", [P, D], F32) for i in range(2)]
        x2buf = sb("x2buf", [P, D], F32)
        yln = sb("yln", [P, D], F32)
        xn3 = sb("xn3", [P, D], BF16)
        xn3T = sb("xn3T", [P, NDT * P], BF16)
        e8 = sb("e8", [P, 8], F32)
        g8 = sb("g8", [P, 8], F32)
        top8 = sb("top8", [P, 8], F32)
        sm_sc = sb("sm_sc", [P, 4], F32)
        tzb = sb("tzb", [P, 32], F32R)
        loraT = [sb(f"loraT{i}", [64, P], F32R) for i in range(2)]
        st4 = sb("st4", [P, 12], F32)
        mv4 = sb("mv4", [P, 2], F32)
        epscol = sb("epscol", [P, 1], F32)
        neghalf = sb("neghalf", [P, 1], F32)
        invDc = sb("invDc", [P, 1], F32)
        vdmy = sb("vdmy", [P, 4], F32)
        sdmy = sb("sdmy", [P, 4], F32)
        if SEG >= 1024:
            stg_a = ybuf[:].rearrange("p a b -> p (a b)")[:, 0:NDT * RK]
            stg_c = ybuf[:].rearrange("p a b -> p (a b)")[:, NDT * RK:NDT * RK + NDT * 32]
            stg_b = ybuf[0:96, 1, :]
            zr = ybuf[0:64, 2, :]
            aug32 = ybuf[0:32, 3, :]
            onesrow = bcbuf[0][0:1, 0, 0:max(SEG, D)]
            stg_row = bcbuf[0][0:1, 1, 0:D]
        else:
            stg_a = sb("stg_a", [P, NDT * RK], F32)[:]
            stg_b = sb("stg_b", [96, D], F32)[:]
            stg_c = sb("stg_c", [P, NDT * 32], F32)[:]
            stg_row = sb("stg_row", [1, D], F32)[:]
            onesrow = sb("onesrow", [1, max(SEG, D)], F32)[:]
            zr = sb("zr", [64, D], F32)[:]
            aug32 = sb("aug32", [32, D], F32)[:]

        px = psum("px", [96, 512])
        pt0 = psum("pt0", [P, 512])
        pt1 = psum("pt1", [P, 512])
        pdtz = psum("pdtz", [P, 512])
        pf0 = psum("pf0", [P, 512])
        pf1 = psum("pf1", [P, 512])
        psm = psum("psm", [P, 64])

        sems = {e: st.enter_context(nc.semaphore(f"sem_{e}")) for e in SEMS}
        V, S, T, Q0, Q2 = "V", "S", "T", "q0", "q2"

        if SEG >= 1024:
            lntmp = ystage[:][:, 0:D]
        else:
            lntmp = sb("lntmp", [P, D], F32)[:]
        last_use = {}
        ltok = {}

        def guard(key):
            t = last_use.get(key)
            if t is None:
                return []
            return list(t) if isinstance(t, (list, tuple)) else [t]

        def setg(key, *toks):
            last_use[key] = list(toks)

        pr.emit(Q2, lambda g: g.memset(ident_f[:], 0.0), inc=1, sem="G")
        t_id = pr.emit(Q2, lambda g: g.affine_select(
            out=ident_f[:], in_=ident_f[:], compare_op=ALU.not_equal,
            fill=1.0, base=0, pattern=[[-1, P]], channel_multiplier=1), inc=1, sem="G")
        t_idr = pr.emit(V, lambda v: v.tensor_copy(out=ident_r[:], in_=ident_f[:]),
                        waits=[t_id])
        t_idb = pr.emit(V, lambda v: v.tensor_copy(out=ident_b[:], in_=ident_f[:]),
                        waits=[t_id])

        t_wx = pr.emit(Q0, lambda s: s.dma_start(out=stg_b, in_=wx_d))
        t_wdt = pr.emit(Q0, lambda s: s.dma_start(
            out=stg_a.rearrange("p (c r) -> p c r", c=NDT),
            in_=wdt_d.rearrange("(c p) r -> p c r", p=P)))
        t_bdt = pr.emit(Q0, lambda s: s.dma_start(out=stg_row, in_=bdt_d))
        t_alog = pr.emit(Q0, lambda s: s.dma_start(
            out=acol[:].rearrange("p (c n) -> p c n", c=NDT),
            in_=alog_d.rearrange("(c p) n -> p c n", p=P)))
        t_dpar = pr.emit(Q0, lambda s: s.dma_start(
            out=dcol[:].rearrange("p (o c) -> p o c", o=1),
            in_=dpar_d.rearrange("o (c p) -> p o c", p=P),
            allow_slow_non_contiguous=True))

        t_zrm = pr.emit(V, lambda v: v.memset(zr, 0.0))
        pr.emit(V, lambda v: v.memset(epscol[:], 1e-5))
        pr.emit(V, lambda v: v.memset(neghalf[:], -0.5))
        pr.emit(V, lambda v: v.memset(invDc[:], 1.0 / D))
        t_zr = pr.emit(V, lambda v: v.memset(onesrow, 1.0))
        t_carry0 = pr.emit(V, lambda v: v.memset(carry[:], 0.0))
        t_ae = pr.emit(S, lambda s: s.activation(out=acol[:], in_=acol[:],
                                                 func=ACT.Exp), waits=[t_alog])
        t_acol = pr.emit(S, lambda s: s.activation(out=acol[:], in_=acol[:],
                                                  func=ACT.Identity, scale=-1.0))
        pr.emit(V, lambda v: v.memset(aug32, 0.0))
        t_aug1 = pr.emit(V, lambda v: v.memset(aug32[0:1, :], 1.0))
        pr.emit(S, lambda s: s.copy(out=dt_rawT_aug[64:96, :], in_=aug32[:, 0:SEG]),
                waits=[t_aug1])
        t_aug2 = pr.emit(S, lambda s: s.copy(out=aug32[0:1, :], in_=stg_row),
                         waits=[t_bdt])
        pr.emit(S, lambda s: s.copy(out=wdtT_aug[64:96, :], in_=aug32[:, 0:NDT * P]),
                waits=[t_aug2])
        for dt in range(NDT):
            tt = pr.emit(T, lambda t, dt=dt: t.transpose(
                pt0[0:RK, 0:P], in_=stg_a[:, dt * RK:(dt + 1) * RK],
                identity=ident_f[:]), waits=[t_wdt, t_id] + guard(("pt0",)))
            te = pr.emit(S, lambda s, dt=dt: s.copy(
                out=wdtT_aug[0:RK, dt * P:(dt + 1) * P], in_=pt0[0:RK, 0:P]), waits=[tt])
            setg(("pt0",), te)
        wx_tts = []
        for c in range(NDT):
            tt = pr.emit(T, lambda t, c=c: t.transpose(
                pt1[:, 0:96], in_=stg_b[0:96, c * P:(c + 1) * P],
                identity=ident_f[0:96, 0:96]), waits=[t_wx, t_id] + guard(("pt1",)))
            te = pr.emit(S, lambda s, c=c: s.copy(out=wx_lhsT[:, c, :],
                                                  in_=pt1[:, 0:96]), waits=[tt])
            setg(("pt1",), te)
            wx_tts.append(tt)
        t_wxT = pr.tok(S)
        setg(("stg_b",), *wx_tts)
        t_wdtT = t_wxT
        for c in range(NDT):
            tl = pr.emit(Q0, lambda s, c=c: s.dma_start(
                out=stg_b[0:96, :], in_=wlin_d[c * P:c * P + 96, :]),
                waits=guard(("stg_b",)))
            tl2 = pr.emit(Q0, lambda s, c=c: s.dma_start(
                out=zr[0:32, :], in_=wlin_d[c * P + 96:(c + 1) * P, :]),
                waits=[t_zrm] + guard(("zrw",)))
            es = []
            for b in range(NDT):
                tt = pr.emit(T, lambda t, b=b: t.transpose(
                    pt0[:, 0:96], in_=stg_b[0:96, b * P:(b + 1) * P],
                    identity=ident_f[0:96, 0:96]), waits=[tl, t_id] + guard(("pt0",)))
                tt2 = pr.emit(T, lambda t, b=b: t.transpose(
                    pt1[:, 0:32], in_=zr[0:32, b * P:(b + 1) * P],
                    identity=ident_f[0:32, 0:32]), waits=[tl2, t_id] + guard(("pt1",)))
                e1 = pr.emit(S, lambda s, c=c, b=b: s.copy(
                    out=wlinT[:, b, c * P:c * P + 96], in_=pt0[:, 0:96]), waits=[tt])
                e2 = pr.emit(S, lambda s, c=c, b=b: s.copy(
                    out=wlinT[:, b, c * P + 96:(c + 1) * P], in_=pt1[:, 0:32]),
                    waits=[tt2])
                setg(("pt0",), e1)
                setg(("pt1",), e2)
                es += [e1, e2]
            setg(("stg_b",), *es)
            setg(("zrw",), *es)
        t_gw = pr.emit(Q0, lambda s: s.dma_start(out=stg_b[0:8, :], in_=gw_d),
                       waits=guard(("stg_b",)))
        for c in range(NDT):
            tt = pr.emit(T, lambda t, c=c: t.transpose(
                pt0[:, 0:8], in_=stg_b[0:8, c * P:(c + 1) * P],
                identity=ident_f[0:8, 0:8]), waits=[t_gw, t_id] + guard(("pt0",)))
            te = pr.emit(S, lambda s, c=c: s.copy(out=glT[:, c, 0:8],
                                                  in_=pt0[:, 0:8]), waits=[tt])
            setg(("pt0",), te)
        t_lfs = []
        for e_ in range(E):
            t_lfs.append(pr.emit(Q0, lambda s, e_=e_: s.dma_start(
                out=stg_c.rearrange("p (c e r) -> p c e r", c=NDT, e=E)[:, :, e_, :],
                in_=left_d[e_].rearrange("(c p) r -> p c r", p=P))))
        t_lf = t_lfs[-1]
        t_glT = pr.emit(V, lambda v: v.tensor_copy(
            out=glT[:, :, 8:40],
            in_=stg_c.rearrange("p (c k) -> p c k", c=NDT)), waits=t_lfs)
        t_rg = pr.emit(Q0, lambda s: s.dma_start(out=stg_b[0:32, :], in_=right_d),
                       waits=[t_gw, pr.tok(T)])
        t_bl = pr.emit(Q0, lambda s: s.dma_start(out=stg_row, in_=blin_d),
                       waits=[pr.tok(S)])
        pr.emit(S, lambda s: s.copy(out=right_aug[0:32, :], in_=stg_b[0:32, :]),
                waits=[t_rg])
        t_aug3 = pr.emit(S, lambda s: s.copy(out=aug32[0:1, :], in_=stg_row),
                         waits=[t_bl])
        pr.emit(S, lambda s: s.copy(out=right_aug[32:64, :], in_=aug32[:, 0:D]),
                waits=[t_aug3])
        t_raug = pr.tok(S)
        t_aug4 = pr.emit(S, lambda s: s.copy(out=aug32[0:1, :], in_=onesrow[0:1, 0:D]),
                         waits=[t_zr])
        for i in range(2):
            pr.emit(S, lambda s, i=i: s.copy(out=loraT[i][32:64, :],
                                             in_=aug32[:, 0:P]), waits=[t_aug4])
        t_lorainit = pr.tok(S)

        t_initdone = [pr.tok("S"), pr.tok("T"), pr.tok("V"), pr.tok("q0"), pr.tok("q2"), pr.tok("G")]
        for k in range(4):
            setg(("ybuf", k), *t_initdone)
        setg(("bc", 0), *t_initdone)
        setg(("bc", 1), *t_initdone)
        prev_carry_S = None

        for h in range(NH):
            lbase = h * SEG
            for sub in range(NSUB):
                for i in range(4):
                    lt = sub * 4 + i
                    sl = lbase + lt * P
                    ub = ubuf[lt % 2]
                    xb = xnbuf[lt % 2]
                    tu = pr.emit(Q0, lambda s, sl=sl, ub=ub: s.dma_start(
                        out=ub[:], in_=u_d[sl:sl + P, :]), waits=guard(("ub", lt % 2)))
                    pr.emit(V, lambda v, ub=ub: v.reduce_sum(
                        out=statbuf[:, 0:1], in_=ub[:], axis=AXL.X),
                        waits=[tu] + guard(("stat",)))
                    tsq = pr.emit(S, lambda s, ub=ub: s.activation(
                        out=lntmp, in_=ub[:], func=ACT.Square),
                        waits=[tu] + guard(("lntmp",)) + guard(("lntmpR",)) + guard(("yst",)))
                    if dbg and h == 0 and lt == 0:
                        pr.emit(Q0, lambda s: s.dma_start(out=dbgsq_d, in_=lntmp),
                                waits=[tsq])
                    tr2 = pr.emit(V, lambda v: v.reduce_sum(
                        out=statbuf[:, 1:2], in_=lntmp, axis=AXL.X), waits=[tsq])
                    for _pi in range(3):
                        pr.emit(V, lambda v, _pi=_pi: v.memset(vdmy[:, _pi:_pi+1], 0.0))
                    tmvf = pr.emit(V, lambda v: v.tensor_copy(
                        out=vdmy[:, 0:2], in_=statbuf[:, 0:2]))
                    ltok[("red2", 0)] = tr2
                    setg(("lntmpR",), tr2)
                    pr.emit(S, lambda s: s.activation(
                        out=statbuf[:, 2:3], in_=statbuf[:, 0:1], func=ACT.Square,
                        scale=invDc[:, 0:1]), waits=[tmvf])
                    pr.emit(S, lambda s: s.activation(
                        out=statbuf[:, 3:4], in_=statbuf[:, 2:3], func=ACT.Identity,
                        scale=-1.0))
                    pr.emit(S, lambda s: s.activation(
                        out=mvbuf[:, 0:1], in_=statbuf[:, 0:1], func=ACT.Identity,
                        scale=invDc[:, 0:1]))
                    tsf0 = pr.emit(S, lambda s: s.copy(out=sdmy[:, 2:4], in_=statbuf[:, 2:4]))
                    pr.selfwait(S, tsf0)
                    tvar = pr.emit(S, lambda s: s.activation(
                        out=mvbuf[:, 1:2], in_=statbuf[:, 1:2], func=ACT.Identity,
                        scale=invDc[:, 0:1], bias=statbuf[:, 3:4]))
                    tln1 = pr.emit(S, lambda s: s.activation(
                        out=rcol[:, 0:1], in_=mvbuf[:, 1:2], func=ACT.Ln, bias=epscol[:, 0:1]))
                    pr.emit(S, lambda s: s.activation(
                        out=rcol[:, 0:1], in_=rcol[:, 0:1], func=ACT.Exp, scale=neghalf[:, 0:1]))
                    tng = pr.emit(S, lambda s: s.activation(
                        out=rcol[:, 1:2], in_=mvbuf[:, 0:1], func=ACT.Identity, scale=-1.0))
                    setg(("mv",), tng)
                    tfl = pr.emit(S, lambda s: s.copy(out=sdmy[:, 0:2], in_=rcol[:, 0:2]))
                    pr.selfwait(S, tfl)
                    if dbg and h == 0:
                        pr.emit(S, lambda s, lt=lt: s.copy(
                            out=statbuf[:, 4:6], in_=rcol[:, 0:2]))
                        pr.emit(S, lambda s, lt=lt: s.copy(
                            out=statbuf[:, 6:8], in_=mvbuf[:, 0:2]))
                        pr.emit(S, lambda s, lt=lt: s.copy(
                            out=statbuf[:, 8:10], in_=statbuf[:, 1:3]))
                        pr.emit(Q0, lambda s, lt=lt: s.dma_start(
                            out=dbgscal_d[:, lt * 6:lt * 6 + 6], in_=statbuf[:, 4:10]),
                            waits=[pr.tok("S")])
                    if skip_ln:
                        txn = pr.emit(S, lambda s, ub=ub, xb=xb: s.copy(
                            out=xb[:], in_=ub[:]),
                            waits=[ltok[("red2", 0)]] + guard(("xb", lt % 2)))
                    else:
                        ta1 = pr.emit(S, lambda s, ub=ub: s.activation(
                            out=lntmp, in_=ub[:], func=ACT.Identity, bias=rcol[:, 1:2]),
                            waits=[ltok[("red2", 0)]] + guard(("lntmp",)) + guard(("lntmpR",)) + guard(("yst",)))
                        txn = pr.emit(S, lambda s, xb=xb: s.activation(
                            out=xb[:], in_=lntmp, func=ACT.Identity, scale=rcol[:, 0:1]),
                            waits=guard(("xb", lt % 2)))
                        setg(("lntmp",), txn)
                    setg(("ub", lt % 2), txn)
                    setg(("stat",), tvar)
                    tts = []
                    for c in range(NDT):
                        pb, pk = ((pt0, "pt0"), (pt1, "pt1"))[c % 2]
                        tt = pr.emit(T, lambda t, xb=xb, c=c, pb=pb: t.transpose(
                            pb[:].bitcast(F32R)[:, 0:P], in_=xb[:, c * P:(c + 1) * P],
                            identity=ident_r[:]), waits=[txn, t_idr] + guard((pk,)))
                        te = pr.emit(S, lambda s, c=c, i=i, pb=pb: s.copy(
                            out=mini[:, c, i * P:(i + 1) * P],
                            in_=pb[:].bitcast(F32R)[:, 0:P]),
                            waits=[tt] + guard(("mini", c)))
                        setg((pk,), te)
                        ltok[("minie", c, i)] = te
                        tts.append(tt)
                    setg(("xb", lt % 2), *tts)
                for c in range(NDT):
                    wts = [ltok[("minie", c, i)] for i in range(4)] + [t_wxT]
                    if c == 0:
                        wts += guard(("px",))
                    tm = pr.emit(T, lambda t, c=c: t.matmul(
                        px[:], lhsT=wx_lhsT[:, c, :], rhs=mini[:, c, :],
                        start=(c == 0), stop=(c == NDT - 1)), waits=wts)
                    tsp = pr.emit(Q2, lambda s, c=c, sub=sub, lbase=lbase: s.dma_start(
                        out=xnt_d[c, :, lbase + sub * 512:lbase + (sub + 1) * 512],
                        in_=mini[:, c, :].bitcast(F32)),
                        waits=[ltok[("minie", c, i)] for i in range(4)])
                    setg(("mini", c), tsp, tm)
                t_px = pr.tok(T)
                te1 = pr.emit(S, lambda s, sub=sub: s.copy(
                    out=dt_rawT_aug[0:64, sub * 512:(sub + 1) * 512],
                    in_=px[0:64, :]), waits=[t_px] + guard(("draw", sub)))
                te2 = pr.emit(S, lambda s: s.copy(out=bcstage[:], in_=px[64:96, :]),
                              waits=[t_px] + guard(("bcstage",)))
                pr.emit(Q2, lambda s, sub=sub, lbase=lbase: s.dma_start(
                    out=bc_d[0, :, lbase + sub * 512:lbase + (sub + 1) * 512],
                    in_=bcstage[0:16, :]), waits=[te2])
                tbc = pr.emit(Q2, lambda s, sub=sub, lbase=lbase: s.dma_start(
                    out=bc_d[1, :, lbase + sub * 512:lbase + (sub + 1) * 512],
                    in_=bcstage[16:32, :]))
                setg(("bcstage",), tbc)
                setg(("px",), te1, te2)
                ltok[("draw", sub)] = te1
                ltok[("bcd", sub)] = tbc

            for gi, dg in enumerate(DGRPS if not skip_scan else ()):
                for k, dt in enumerate(dg):
                    xs = xstrip[k % 2]
                    tx = pr.emit(Q0, lambda s, dt=dt, xs=xs, lbase=lbase: s.dma_start(
                        out=xs[:], in_=xnt_d[dt, :, lbase:lbase + SEG]),
                        waits=[ltok[("bcd", sub)] for sub in range(NSUB)]
                              + guard(("xs", k % 2)))
                    for sub in range(NSUB):
                        tm = pr.emit(T, lambda t, dt=dt, sub=sub: t.matmul(
                            pdtz[:], lhsT=wdtT_aug[:, dt * P:(dt + 1) * P],
                            rhs=dt_rawT_aug[:, sub * 512:(sub + 1) * 512],
                            start=True, stop=True),
                            waits=[t_wdtT, ltok[("draw", sub)]] + guard(("pdtz",)))
                        ts1 = pr.emit(S, lambda s, k=k, sub=sub: s.activation(
                            out=dtT[:, k, sub * 512:(sub + 1) * 512], in_=pdtz[:],
                            func=ACT.Exp), waits=[tm] + guard(("dtTr", k)))
                        pr.emit(S, lambda s, k=k, sub=sub: s.activation(
                            out=dtT[:, k, sub * 512:(sub + 1) * 512],
                            in_=dtT[:, k, sub * 512:(sub + 1) * 512],
                            func=ACT.Ln, bias=1.0))
                        setg(("pdtz",), ts1)
                        ltok[("drawuse", sub)] = tm
                    t_dtT = pr.tok(S)
                    tw = pr.emit(V, lambda v, k=k, xs=xs: v.tensor_tensor(
                        out=wT[:, k, :], in0=dtT[:, k, :], in1=xs[:], op=ALU.mult),
                        waits=[tx, t_dtT])
                    ty0 = pr.emit(V, lambda v, k=k, dt=dt, xs=xs: v.tensor_scalar(
                        out=ybuf[:, k, :], in0=xs[:], scalar1=dcol[:, dt:dt + 1],
                        scalar2=None, op0=ALU.mult),
                        waits=[t_dpar, tx] + guard(("ybuf", k)))
                    setg(("xs", k % 2), ty0)
                    setg(("dtTr", k), tw)
                    ltok[("w", k)] = tw
                    ltok[("y0", k)] = ty0
                setg(("draw", 0), *[ltok[("drawuse", s_)] for s_ in range(NSUB)])
                if NSUB > 1:
                    setg(("draw", 1), *[ltok[("drawuse", s_)] for s_ in range(NSUB)])
                prev_carry_local = None
                t_abar_last = None
                for n in range(N):
                    bb = bcbuf[n % 2]
                    tb1 = pr.emit(Q2, lambda s, n=n, bb=bb, lbase=lbase: s.dma_start(
                        out=bb[:], in_=bc_d[:, n:n + 1, lbase:lbase + SEG]
                        .rearrange("bc o c -> o bc c").to_broadcast([P, 2, SEG])),
                        waits=[ltok[("bcd", sub)] for sub in range(NSUB)]
                              + guard(("bc", n % 2)))
                    tb2 = tb1
                    ta = []
                    for k, dt in enumerate(dg):
                        ta.append(pr.emit(S, lambda s, k=k, dt=dt, n=n: s.activation(
                            out=abuf[:, k, :], in_=dtT[:, k, :], func=ACT.Exp,
                            scale=acol[:, dt * N + n:dt * N + n + 1]),
                            waits=[t_acol] + guard(("abuf", k))))
                    tbv = pr.emit(V, lambda v, bb=bb: v.tensor_tensor(
                        out=bybuf[:], in0=wT[:],
                        in1=bb[:, 0:1, :].to_broadcast([P, 4, SEG]), op=ALU.mult),
                        waits=[tb1, ltok[("w", 0)], ltok[("w", 3)]])
                    if n == 0 and h == 0 and gi == 0:
                        pr.selfwait(V, t_carry0)
                    if prev_carry_S is not None:
                        pr.selfwait(V, prev_carry_S)
                    tscan = []
                    for k, dt in enumerate(dg):
                        cix = n * NDT + dt
                        tscan.append(pr.emit(V, lambda v, k=k, cix=cix: v.tensor_tensor_scan(
                            out=abuf[:, k, :], data0=abuf[:, k, :], data1=bybuf[:, k, :],
                            initial=carry[:, cix:cix + 1], op0=ALU.mult, op1=ALU.add),
                            waits=[ta[k]]))
                    t_scans = pr.tok(V)
                    c0 = n * NDT + dg[0]
                    tcs = pr.emit(S, lambda s, c0=c0: s.copy(
                        out=carry[:, c0:c0 + 4],
                        in_=abuf[:, :, SEG - 1:SEG].rearrange("p k o -> p (k o)")),
                        waits=tscan)
                    prev_carry_local = tcs
                    tyk = pr.emit(V, lambda v, bb=bb: v.tensor_tensor(
                        out=bybuf[:], in0=abuf[:],
                        in1=bb[:, 1:2, :].to_broadcast([P, 4, SEG]), op=ALU.mult),
                        waits=[tb2])
                    setg(("bc", n % 2), tyk)
                    for k in range(4):
                        setg(("abuf", k), tyk, tcs)
                    tya = pr.emit(V, lambda v: v.tensor_tensor(
                        out=ybuf[:], in0=ybuf[:], in1=bybuf[:], op=ALU.add),
                        waits=[ltok[("y0", 0)], ltok[("y0", 3)]])
                prev_carry_S = prev_carry_local
                for k in range(4):
                    setg(("dtTr", k), pr.tok(S), pr.tok(V))
                t_y_done = pr.tok(V)
                for k, dt in enumerate(dg):
                    yes = []
                    for lt in range(NLT):
                        pb, pk = ((pt0, "pt0"), (pt1, "pt1"))[lt % 2]
                        tt = pr.emit(T, lambda t, k=k, lt=lt, pb=pb: t.transpose(
                            pb[:, 0:P], in_=ybuf[:, k, lt * P:(lt + 1) * P],
                            identity=ident_f[:]), waits=[t_y_done, t_id] + guard((pk,)))
                        te = pr.emit(S, lambda s, lt=lt, pb=pb: s.copy(
                            out=ystage[:, lt * P:(lt + 1) * P], in_=pb[:, 0:P]),
                            waits=[tt] + guard(("yst",)) + guard(("lntmp",)))
                        setg((pk,), te)
                        yes.append(te)
                    ty = pr.emit(Q2, lambda s, dt=dt, lbase=lbase: s.dma_start(
                        out=ylp_d[lbase:lbase + SEG, dt * P:(dt + 1) * P]
                        .rearrange("(lt p) c -> p lt c", p=P),
                        in_=ystage[:].rearrange("p (lt c) -> p lt c", c=P)),
                        waits=yes)
                    setg(("yst",), ty)
                    ltok[("ydma", h, dt)] = ty
                for k in range(4):
                    setg(("ybuf", k), ltok[("ydma", h, dg[-1])])

            if skip_scan:
                continue
            if skip_post:
                for lt in range(NLT):
                    sl = lbase + lt * P
                    yb = ylt[lt % 2]
                    tyl = pr.emit(Q0, lambda s, sl=sl, yb=yb: s.dma_start(
                        out=yb[:], in_=ylp_d[sl:sl + P, :]),
                        waits=[ltok[("ydma", h, dt)] for dt in range(NDT)]
                              + guard(("ylt", lt % 2)))
                    tcp = pr.emit(V, lambda v, yb=yb: v.tensor_copy(out=x2buf[:], in_=yb[:]),
                                  waits=[tyl] + guard(("x2",)))
                    setg(("ylt", lt % 2), tcp)
                    tod = pr.emit(Q0, lambda s, sl=sl: s.dma_start(
                        out=out_d[sl:sl + P, :], in_=x2buf[:]), waits=[tcp])
                    setg(("x2",), tod)
                continue
            for lt in range(NLT):
                sl = lbase + lt * P
                yb = ylt[lt % 2]
                ub2 = u2buf[lt % 2]
                tyl = pr.emit(Q0, lambda s, sl=sl, yb=yb: s.dma_start(
                    out=yb[:], in_=ylp_d[sl:sl + P, :]),
                    waits=[ltok[("ydma", h, dt)] for dt in range(NDT)]
                          + guard(("ylt", lt % 2)))
                tu2 = pr.emit(Q0, lambda s, sl=sl, ub2=ub2: s.dma_start(
                    out=ub2[:], in_=u_d[sl:sl + P, :]), waits=guard(("u2", lt % 2)))
                t1 = pr.emit(V, lambda v, yb=yb: v.reduce_sum(
                    out=statbuf[:, 0:1], in_=yb[:], axis=AXL.X),
                    waits=[tyl] + guard(("stat",)))
                tsq2 = pr.emit(S, lambda s, yb=yb: s.activation(
                    out=lntmp, in_=yb[:], func=ACT.Square),
                    waits=[tyl] + guard(("lntmp",)) + guard(("lntmpR",)) + guard(("yst",)))
                tr2b = pr.emit(V, lambda v: v.reduce_sum(
                    out=statbuf[:, 1:2], in_=lntmp, axis=AXL.X), waits=[tsq2])
                for _pi in range(3):
                    pr.emit(V, lambda v, _pi=_pi: v.memset(vdmy[:, _pi:_pi+1], 0.0))
                t2f = pr.emit(V, lambda v: v.tensor_copy(
                    out=vdmy[:, 0:2], in_=statbuf[:, 0:2]))
                ltok[("red2", 1)] = tr2b
                setg(("lntmpR",), tr2b)
                pr.emit(S, lambda s: s.activation(
                    out=statbuf[:, 2:3], in_=statbuf[:, 0:1], func=ACT.Square,
                    scale=invDc[:, 0:1]), waits=[t2f])
                pr.emit(S, lambda s: s.activation(
                    out=statbuf[:, 3:4], in_=statbuf[:, 2:3], func=ACT.Identity,
                    scale=-1.0))
                pr.emit(S, lambda s: s.activation(
                    out=mvbuf[:, 0:1], in_=statbuf[:, 0:1], func=ACT.Identity,
                    scale=invDc[:, 0:1]))
                tsf2 = pr.emit(S, lambda s: s.copy(out=sdmy[:, 2:4], in_=statbuf[:, 2:4]))
                pr.selfwait(S, tsf2)
                tvar2 = pr.emit(S, lambda s: s.activation(
                    out=mvbuf[:, 1:2], in_=statbuf[:, 1:2], func=ACT.Identity,
                    scale=invDc[:, 0:1], bias=statbuf[:, 3:4]))
                pr.emit(S, lambda s: s.activation(out=rcol[:, 0:1], in_=mvbuf[:, 1:2],
                                                  func=ACT.Ln, bias=epscol[:, 0:1]))
                pr.emit(S, lambda s: s.activation(out=rcol[:, 0:1], in_=rcol[:, 0:1],
                                                  func=ACT.Exp, scale=neghalf[:, 0:1]))
                tng2 = pr.emit(S, lambda s: s.activation(
                    out=rcol[:, 1:2], in_=mvbuf[:, 0:1], func=ACT.Identity, scale=-1.0))
                setg(("mv",), tng2)
                tfl2 = pr.emit(S, lambda s: s.copy(out=sdmy[:, 0:2], in_=rcol[:, 0:2]))
                pr.selfwait(S, tfl2)
                ta2 = pr.emit(S, lambda s, yb=yb: s.activation(
                    out=lntmp, in_=yb[:], func=ACT.Identity, bias=rcol[:, 1:2]),
                    waits=[ltok[("red2", 1)]] + guard(("lntmp",)) + guard(("lntmpR",)) + guard(("yst",)))
                t6 = pr.emit(S, lambda s: s.activation(
                    out=yln[:], in_=lntmp, func=ACT.Identity, scale=rcol[:, 0:1]),
                    waits=guard(("yln",)))
                setg(("lntmp",), t6)
                setg(("ylt", lt % 2), t6)
                t7 = pr.emit(V, lambda v, ub2=ub2: v.tensor_tensor(
                    out=x2buf[:], in0=ub2[:], in1=yln[:], op=ALU.add),
                    waits=[tu2, t6] + guard(("x2",)))
                setg(("u2", lt % 2), t7)
                setg(("yln",), t7)
                if post_level <= 1:
                    tod = pr.emit(Q0, lambda s, sl=sl: s.dma_start(
                        out=out_d[sl:sl + P, :], in_=x2buf[:]), waits=[t7] + guard(("x2o",)))
                    setg(("x2",), tod)
                    setg(("x2o",), tod)
                    continue
                t8 = pr.emit(V, lambda v: v.reduce_sum(
                    out=statbuf[:, 0:1], in_=x2buf[:], axis=AXL.X),
                    waits=[t7, tvar2])
                tsq3 = pr.emit(S, lambda s: s.activation(
                    out=lntmp, in_=x2buf[:], func=ACT.Square),
                    waits=[t7] + guard(("lntmp",)) + guard(("lntmpR",)) + guard(("yst",)))
                tr2c = pr.emit(V, lambda v: v.reduce_sum(
                    out=statbuf[:, 1:2], in_=lntmp, axis=AXL.X), waits=[tsq3])
                for _pi in range(3):
                    pr.emit(V, lambda v, _pi=_pi: v.memset(vdmy[:, _pi:_pi+1], 0.0))
                t9f = pr.emit(V, lambda v: v.tensor_copy(
                    out=vdmy[:, 0:2], in_=statbuf[:, 0:2]))
                setg(("lntmpR",), tr2c)
                pr.emit(S, lambda s: s.activation(
                    out=statbuf[:, 2:3], in_=statbuf[:, 0:1], func=ACT.Square,
                    scale=invDc[:, 0:1]), waits=[t9f])
                pr.emit(S, lambda s: s.activation(
                    out=statbuf[:, 3:4], in_=statbuf[:, 2:3], func=ACT.Identity,
                    scale=-1.0))
                pr.emit(S, lambda s: s.activation(
                    out=mvbuf[:, 0:1], in_=statbuf[:, 0:1], func=ACT.Identity,
                    scale=invDc[:, 0:1]))
                tsf3 = pr.emit(S, lambda s: s.copy(out=sdmy[:, 2:4], in_=statbuf[:, 2:4]))
                pr.selfwait(S, tsf3)
                tvar3 = pr.emit(S, lambda s: s.activation(
                    out=mvbuf[:, 1:2], in_=statbuf[:, 1:2], func=ACT.Identity,
                    scale=invDc[:, 0:1], bias=statbuf[:, 3:4]))
                setg(("stat",), tvar3)
                pr.emit(S, lambda s: s.activation(out=rcol[:, 0:1], in_=mvbuf[:, 1:2],
                                                  func=ACT.Ln, bias=epscol[:, 0:1]))
                pr.emit(S, lambda s: s.activation(out=rcol[:, 0:1], in_=rcol[:, 0:1],
                                                  func=ACT.Exp, scale=neghalf[:, 0:1]))
                tng3 = pr.emit(S, lambda s: s.activation(
                    out=rcol[:, 1:2], in_=mvbuf[:, 0:1], func=ACT.Identity, scale=-1.0))
                setg(("mv",), tng3)
                tfl3 = pr.emit(S, lambda s: s.copy(out=sdmy[:, 0:2], in_=rcol[:, 0:2]))
                pr.selfwait(S, tfl3)
                ta3 = pr.emit(S, lambda s: s.activation(
                    out=lntmp, in_=x2buf[:], func=ACT.Identity, bias=rcol[:, 1:2]),
                    waits=[tr2c] + guard(("lntmp",)) + guard(("lntmpR",)) + guard(("yst",)))
                td_ = pr.emit(S, lambda s: s.activation(
                    out=xn3[:], in_=lntmp, func=ACT.Identity, scale=rcol[:, 0:1]),
                    waits=guard(("xn3",)))
                setg(("lntmp",), td_)
                if post_level <= 2:
                    tcp2 = pr.emit(V, lambda v: v.tensor_copy(out=x2buf[:], in_=xn3[:].bitcast(BF16) if False else x2buf[:]), waits=[td_])
                    tod = pr.emit(Q0, lambda s, sl=sl: s.dma_start(
                        out=out_d[sl:sl + P, :], in_=x2buf[:]), waits=[tcp2] + guard(("x2o",)))
                    setg(("x2",), tod)
                    setg(("x2o",), tod)
                    continue
                _stage = 9
                tes = []
                tts = []
                for c in range(NDT):
                    pb, pk = ((pt0, "pt0"), (pt1, "pt1"))[c % 2]
                    tt = pr.emit(T, lambda t, c=c, pb=pb: t.transpose(
                        pb[:].bitcast(BF16)[:, 0:P], in_=xn3[:, c * P:(c + 1) * P],
                        identity=ident_b[:]), waits=[td_, t_idb] + guard((pk,)))
                    te = pr.emit(S, lambda s, c=c, pb=pb: s.copy(
                        out=xn3T[:, c * P:(c + 1) * P], in_=pb[:].bitcast(BF16)[:, 0:P]),
                        waits=[tt] + guard(("xn3T",)))
                    setg((pk,), te)
                    tes.append(te)
                    tts.append(tt)
                setg(("xn3",), *tts)
                if _stage <= 4:
                    tcp3 = pr.emit(V, lambda v: v.tensor_copy(out=x2buf[:], in_=x2buf[:]),
                                   waits=tes)
                    tod = pr.emit(Q0, lambda s, sl=sl: s.dma_start(
                        out=out_d[sl:sl + P, :], in_=x2buf[:]), waits=[tcp3] + guard(("x2o",)))
                    setg(("x2",), tod)
                    setg(("x2o",), tod)
                    continue
                for c in range(NDT):
                    w0 = [tes[c], t_wxT]
                    if c == 0:
                        w0 += guard(("pf",)) + [t_glT]
                    stp = (post_level <= 3 and c == NDT - 1)
                    tm1 = pr.emit(T, lambda t, c=c, stp=stp: t.matmul(
                        pf0[:], lhsT=xn3T[:, c * P:(c + 1) * P], rhs=wlinT[:, c, 0:512],
                        start=(c == 0), stop=stp), waits=w0)
                    pr.emit(T, lambda t, c=c, stp=stp: t.matmul(
                        pf1[:], lhsT=xn3T[:, c * P:(c + 1) * P],
                        rhs=wlinT[:, c, 512:1024],
                        start=(c == 0), stop=stp))
                    if post_level >= 4:
                        w2 = guard(("psm",)) if c == 0 else []
                        pr.emit(T, lambda t, c=c: t.matmul(
                            psm[:, 0:40], lhsT=xn3T[:, c * P:(c + 1) * P], rhs=glT[:, c, :],
                            start=(c == 0), stop=(c == NDT - 1)), waits=w2)
                setg(("xn3T",), pr.tok(T))
                t_mm = pr.tok(T)
                if post_level >= 4:
                    tsm1 = pr.emit(V, lambda v: v.reduce_max(
                        out=sm_sc[:, 0:1], in_=psm[:, 0:8], axis=AXL.X, negate=True),
                        waits=[t_mm])
                    for _pi in range(3):
                        pr.emit(V, lambda v, _pi=_pi: v.memset(vdmy[:, _pi:_pi+1], 0.0))
                    tsm1f = pr.emit(V, lambda v: v.tensor_copy(out=vdmy[:, 0:1],
                                                               in_=sm_sc[:, 0:1]))
                    tsm2 = pr.emit(S, lambda s: s.activation(
                        out=e8[:], in_=psm[:, 0:8], func=ACT.Exp, bias=sm_sc[:, 0:1]),
                        waits=[tsm1f] + guard(("e8",)))
                    tsm3 = pr.emit(V, lambda v: v.reduce_sum(
                        out=sm_sc[:, 1:2], in_=e8[:], axis=AXL.X), waits=[tsm2])
                    tsm5 = pr.emit(V, lambda v: v.max(out=top8[:], in_=e8[:]))
                    for _pi in range(3):
                        pr.emit(V, lambda v, _pi=_pi: v.memset(vdmy[:, _pi:_pi+1], 0.0))
                    pr.emit(V, lambda v: v.reciprocal(out=sm_sc[:, 2:3], in_=sm_sc[:, 1:2]))
                    pr.emit(V, lambda v: v.tensor_copy(out=vdmy[:, 0:2], in_=sm_sc[:, 1:3]))
                    pr.emit(V, lambda v: v.tensor_copy(out=vdmy[:, 2:3], in_=top8[:, 1:2]))
                    pr.selfwait(V)
                    tsm6 = pr.emit(V, lambda v: v.scalar_tensor_tensor(
                        out=g8[:], in0=e8[:], scalar=top8[:, 1:2], in1=e8[:],
                        op0=ALU.is_ge, op1=ALU.mult))
                    tz_t = pr.emit(V, lambda v: v.scalar_tensor_tensor(
                        out=tzb[:], in0=psm[:, 8:40], scalar=sm_sc[:, 2:3],
                        in1=g8[:].rearrange("p (e o) -> p e o", o=1).to_broadcast([P, 8, 4]),
                        op0=ALU.mult, op1=ALU.mult), waits=guard(("tzb",)))
                    setg(("e8",), tz_t)
                    setg(("psm",), tz_t, tsm2)
                    lb = loraT[lt % 2]
                    tt = pr.emit(T, lambda t: t.transpose(
                        pt0[:].bitcast(F32R)[0:32, 0:P], in_=tzb[:], identity=ident_r[:]),
                        waits=[tz_t, t_idr] + guard(("pt0",)))
                    te = pr.emit(S, lambda s, lb=lb: s.copy(
                        out=lb[0:32, :], in_=pt0[:].bitcast(F32R)[0:32, 0:P]),
                        waits=[tt, t_lorainit] + guard(("lora", lt % 2)))
                    setg(("pt0",), te)
                    setg(("tzb",), tt)
                    tl1 = pr.emit(T, lambda t, lb=lb: t.matmul(
                        pf0[:], lhsT=lb[:], rhs=right_aug[:, 0:512], start=False, stop=True),
                        waits=[te, t_raug])
                    tl2 = pr.emit(T, lambda t, lb=lb: t.matmul(
                        pf1[:], lhsT=lb[:], rhs=right_aug[:, 512:1024], start=False,
                        stop=True))
                    setg(("lora", lt % 2), tl2)
                t_ffn = pr.tok(T)
                if _stage <= 5:
                    tcp4 = pr.emit(S, lambda s: s.copy(out=x2buf[:, 0:512], in_=pf0[:]),
                                   waits=[t_ffn] + guard(("x2o",)))
                    tcp5 = pr.emit(S, lambda s: s.copy(out=x2buf[:, 512:1024], in_=pf1[:]))
                    setg(("pf",), tcp5)
                    tod = pr.emit(Q0, lambda s, sl=sl: s.dma_start(
                        out=out_d[sl:sl + P, :], in_=x2buf[:]), waits=[tcp5])
                    setg(("x2",), tod)
                    setg(("x2o",), tod)
                    continue
                tsq4a = pr.emit(S, lambda s: s.activation(
                    out=lntmp[:, 0:512], in_=pf0[:], func=ACT.Square),
                    waits=[t_ffn] + guard(("lntmp",)) + guard(("lntmpR",)) + guard(("yst",)) + [t7])
                tsq4b = pr.emit(S, lambda s: s.activation(
                    out=lntmp[:, 512:1024], in_=pf1[:], func=ACT.Square))
                tn1 = pr.emit(V, lambda v: v.reduce_sum(
                    out=st4[:, 0:1], in_=pf0[:], axis=AXL.X), waits=[tsq4b])
                pr.emit(V, lambda v: v.reduce_sum(
                    out=st4[:, 1:2], in_=pf1[:], axis=AXL.X))
                tr2d = pr.emit(V, lambda v: v.reduce_sum(
                    out=st4[:, 2:3], in_=lntmp, axis=AXL.X), waits=[tsq4b])
                for _pi in range(3):
                    pr.emit(V, lambda v, _pi=_pi: v.memset(vdmy[:, _pi:_pi+1], 0.0))
                tn3f = pr.emit(V, lambda v: v.tensor_copy(
                    out=vdmy[:, 2:4], in_=st4[:, 1:3]))
                setg(("lntmpR",), tr2d)
                pr.emit(S, lambda s: s.activation(
                    out=st4[:, 3:4], in_=st4[:, 0:1], func=ACT.Identity,
                    bias=st4[:, 1:2]), waits=[tn3f])
                pr.emit(S, lambda s: s.activation(
                    out=st4[:, 4:5], in_=st4[:, 3:4], func=ACT.Square,
                    scale=invDc[:, 0:1]))
                pr.emit(S, lambda s: s.activation(
                    out=st4[:, 6:7], in_=st4[:, 4:5], func=ACT.Identity,
                    scale=-1.0))
                pr.emit(S, lambda s: s.activation(
                    out=mv4[:, 0:1], in_=st4[:, 3:4], func=ACT.Identity,
                    scale=invDc[:, 0:1]))
                tsf4 = pr.emit(S, lambda s: s.copy(out=sdmy[:, 2:3], in_=st4[:, 6:7]))
                pr.selfwait(S, tsf4)
                tvar4 = pr.emit(S, lambda s: s.activation(
                    out=mv4[:, 1:2], in_=st4[:, 2:3], func=ACT.Identity,
                    scale=invDc[:, 0:1], bias=st4[:, 6:7]))
                pr.emit(S, lambda s: s.activation(out=rcol[:, 0:1], in_=mv4[:, 1:2],
                                                  func=ACT.Ln, bias=epscol[:, 0:1]))
                pr.emit(S, lambda s: s.activation(out=rcol[:, 0:1], in_=rcol[:, 0:1],
                                                  func=ACT.Exp, scale=neghalf[:, 0:1]))
                tng4 = pr.emit(S, lambda s: s.activation(
                    out=rcol[:, 1:2], in_=mv4[:, 0:1], func=ACT.Identity, scale=-1.0))
                setg(("mv4",), tng4)
                tfl4 = pr.emit(S, lambda s: s.copy(out=sdmy[:, 0:2], in_=rcol[:, 0:2]))
                pr.selfwait(S, tfl4)
                ta4a = pr.emit(S, lambda s: s.activation(
                    out=lntmp[:, 0:512], in_=pf0[:], func=ACT.Identity, bias=rcol[:, 1:2]),
                    waits=[tr2d] + guard(("lntmp",)) + guard(("lntmpR",)) + guard(("yst",)))
                ta4b = pr.emit(S, lambda s: s.activation(
                    out=lntmp[:, 512:1024], in_=pf1[:], func=ACT.Identity, bias=rcol[:, 1:2]))
                to1 = pr.emit(S, lambda s: s.activation(
                    out=yln[:, 0:512], in_=lntmp[:, 0:512], func=ACT.Identity,
                    scale=rcol[:, 0:1]), waits=guard(("obuf",)))
                to2 = pr.emit(S, lambda s: s.activation(
                    out=yln[:, 512:1024], in_=lntmp[:, 512:1024], func=ACT.Identity,
                    scale=rcol[:, 0:1]))
                setg(("lntmp",), to2)
                setg(("pf",), ta4b)
                tfin = pr.emit(V, lambda v: v.tensor_tensor(
                    out=yln[:], in0=yln[:], in1=x2buf[:], op=ALU.add), waits=[to2])
                setg(("x2",), tfin)
                tod = pr.emit(Q0, lambda s, sl=sl: s.dma_start(
                    out=out_d[sl:sl + P, :], in_=yln[:]), waits=[tfin])
                setg(("yln",), tod)
                setg(("obuf",), tod)

        pr.selfwait(Q0)
        pr.selfwait(Q2)

        with nc.Block() as blk:
            def run_stream(name):
                def body(eng):
                    for wl, fn, inc, semn in pr.streams[name]:
                        for (pe, v) in wl:
                            eng.wait_ge(sems[pe], v)
                        if fn is not None:
                            inst = fn(eng)
                            if inc:
                                inst.then_inc(sems[semn], inc)
                return body
            blk.vector(run_stream("V"))
            blk.scalar(run_stream("S"))
            blk.tensor(run_stream("T"))
            blk.sync(run_stream("q0"))
            blk.gpsimd(run_stream("q2"))

    return nc


def prep_inputs(inp, L):
    right = np.ascontiguousarray(np.asarray(inp["L0_ffn_right"], np.float32)
                                 .reshape(E * R, D))
    base = {
        "A_log": np.ascontiguousarray(inp["L0_attn_A_log"], dtype=np.float32),
        "W_x": np.ascontiguousarray(inp["L0_attn_W_x"], dtype=np.float32),
        "W_dt": np.ascontiguousarray(inp["L0_attn_W_dt"], dtype=np.float32),
        "b_dt": np.ascontiguousarray(np.asarray(inp["L0_attn_b_dt"], np.float32)
                                     .reshape(1, D)),
        "D_param": np.ascontiguousarray(np.asarray(inp["L0_attn_D_param"], np.float32)
                                        .reshape(1, D)),
        "W_lin": np.ascontiguousarray(inp["L0_ffn_W_lin"], dtype=np.float32),
        "b_lin": np.ascontiguousarray(np.asarray(inp["L0_ffn_b_lin"], np.float32)
                                      .reshape(1, D)),
        "gate_w": np.ascontiguousarray(inp["L0_ffn_gate_w"], dtype=np.float32),
        "left": np.ascontiguousarray(inp["L0_ffn_left"], dtype=np.float32),
        "right": right,
    }
    u = np.asarray(inp["L0_attn_u"], np.float32)
    maps = []
    for b in range(u.shape[0]):
        m = dict(base)
        m["u"] = np.ascontiguousarray(u[b, :L, :])
        maps.append(m)
    return maps


def kernel(**inputs):
    from concourse.bass_utils import run_bass_kernel_spmd
    L = 4096
    nc = build_nc(L=L, SEG=1024)
    maps = prep_inputs(inputs, L)
    res = run_bass_kernel_spmd(nc, maps, core_ids=list(range(8)), trace=False)
    return np.stack([np.asarray(r["out"], np.float32) for r in res.results])


# version 02
# speedup vs baseline: 1.0619x; 1.0619x over previous
import numpy as np
from contextlib import ExitStack
import concourse.bass as bass
import concourse.mybir as mybir

F32 = mybir.dt.float32
F32R = mybir.dt.float32r
BF16 = mybir.dt.bfloat16
ALU = mybir.AluOpType
ACT = mybir.ActivationFunctionType
AXL = mybir.AxisListType

D = 1024
N = 16
RK = 64
NDT = 8
E = 8
R = 4
P = 128

ENGS = ("V", "S", "T", "q0", "q2")
SEMS = ("V", "S", "T", "q0", "q2", "G")
SEM2ENG = {"V": "V", "S": "S", "T": "T", "q0": "q0", "q2": "q2", "G": "q2"}


class Prog:
    def __init__(self):
        self.streams = {e: [] for e in ENGS}
        self.cnt = {s: 0 for s in SEMS}
        self.waited = {c: {p: 0 for p in SEMS} for c in ENGS}

    def emit(self, eng, fn, waits=(), inc=None, sem=None):
        sem = sem or eng
        if inc is None:
            inc = 16 if eng in ("q0", "q2") else 1
        if eng in ("q0", "q2") and sem == eng and self.cnt[eng] > 0:
            self.selfwait(eng)
        if eng == "T" and self.cnt["T"] >= 2:
            self.selfwait("T", ("T", self.cnt["T"] - 2))
        wl = []
        for tok in waits:
            if tok is None:
                continue
            pe, v = tok
            if SEM2ENG[pe] == eng:
                continue
            if v > self.waited[eng][pe]:
                self.waited[eng][pe] = v
                wl.append((pe, v))
        self.cnt[sem] += inc
        self.streams[eng].append((wl, fn, inc, sem))
        return (sem, self.cnt[sem])

    def selfwait(self, eng, tok=None):
        if tok is None:
            tok = (eng, self.cnt[eng])
        pe, v = tok
        if v > self.waited[eng][pe]:
            self.waited[eng][pe] = v
            self.streams[eng].append(([(pe, v)], None, 0, eng))

    def tok(self, sem):
        return (sem, self.cnt[sem])


def build_nc(L=4096, SEG=1024, name="mk"):
    dbg = False
    skip_post = False
    skip_scan = False
    skip_ln = False
    post_level = 4
    assert L % SEG == 0 and SEG % 512 == 0
    NH = L // SEG
    NSUB = SEG // 512
    NLT = SEG // 128
    DGRPS = ((0, 1, 2, 3), (4, 5, 6, 7))

    nc = bass.Bass("TRN2", target_bir_lowering=False, name=name,
                   detect_race_conditions=False)

    u_d = nc.dram_tensor("u", [L, D], F32, kind="ExternalInput").ap()
    alog_d = nc.dram_tensor("A_log", [D, N], F32, kind="ExternalInput").ap()
    wx_d = nc.dram_tensor("W_x", [96, D], F32, kind="ExternalInput").ap()
    wdt_d = nc.dram_tensor("W_dt", [D, RK], F32, kind="ExternalInput").ap()
    bdt_d = nc.dram_tensor("b_dt", [1, D], F32, kind="ExternalInput").ap()
    dpar_d = nc.dram_tensor("D_param", [1, D], F32, kind="ExternalInput").ap()
    wlin_d = nc.dram_tensor("W_lin", [D, D], F32, kind="ExternalInput").ap()
    blin_d = nc.dram_tensor("b_lin", [1, D], F32, kind="ExternalInput").ap()
    gw_d = nc.dram_tensor("gate_w", [E, D], F32, kind="ExternalInput").ap()
    left_d = nc.dram_tensor("left", [E, D, R], F32, kind="ExternalInput").ap()
    right_d = nc.dram_tensor("right", [E * R, D], F32, kind="ExternalInput").ap()
    out_d = nc.dram_tensor("out", [L, D], F32, kind="ExternalOutput").ap()
    knd = "ExternalOutput" if dbg else "Internal"
    dbgscal_d = nc.dram_tensor("dbgscal", [P, 64], F32, kind=knd).ap()
    dbgsq_d = nc.dram_tensor("dbgsq", [P, D], F32, kind=knd).ap()
    bc_d = nc.dram_tensor("bc_scr", [2, 16, L], F32, kind=knd).ap()
    xnt_d = nc.dram_tensor("xnt_scr", [NDT, P, L], F32, kind=knd).ap()
    ylp_d = nc.dram_tensor("ylp_scr", [L, D], F32, kind=knd).ap()

    pr = Prog()
    st = ExitStack()

    def sb(nm, shape, dt):
        return st.enter_context(nc.sbuf_tensor(nm, shape, dt))

    def psum(nm, shape):
        return st.enter_context(nc.psum_tensor(nm, shape, F32))

    with st:
        ident_r = sb("ident_r", [P, P], F32R)
        ident_b = sb("ident_b", [P, P], BF16)
        ident_f = sb("ident_f", [P, P], F32)
        wx_lhsT = sb("wx_lhsT", [P, NDT, 96], F32R)
        wdtT_aug = sb("wdtT_aug", [96, NDT * P], F32R)
        wlinT = sb("wlinT", [P, NDT, D], BF16)
        glT = sb("glT", [P, NDT, 40], BF16)
        right_aug = sb("right_aug", [64, D], F32R)
        acol = sb("acol", [P, NDT * N], F32)
        dcol = sb("dcol", [P, NDT], F32)
        dt_rawT_aug = sb("dt_rawT_aug", [96, SEG], F32R)
        carry = sb("carry", [P, N * NDT], F32)
        dtT = sb("dtT", [P, 4, SEG], F32)
        wT = sb("wT", [P, 4, SEG], F32)
        ybuf = sb("ybuf", [P, 4, SEG], F32)
        abuf = sb("abuf", [P, 4, SEG], F32)
        bybuf = sb("bybuf", [P, 4, SEG], F32)
        bcbuf = [sb(f"bcbuf{i}", [P, 2, SEG], F32) for i in range(2)]
        xstrip = [sb(f"xstrip{i}", [P, SEG], F32) for i in range(2)]
        ubuf = [sb(f"ubuf{i}", [P, D], F32) for i in range(2)]
        xnbuf = [sb(f"xnbuf{i}", [P, D], F32R) for i in range(2)]
        mini = sb("mini", [P, NDT, 512], F32R)
        bcstage = sb("bcstage", [32, 512], F32)
        statbuf = sb("statbuf", [P, 12], F32)
        mvbuf = sb("mvbuf", [P, 2], F32)
        rcol = sb("rcol", [P, 2], F32)
        ystage = sb("ystage", [P, SEG], F32)
        ylt = [sb(f"ylt{i}", [P, D], F32) for i in range(2)]
        u2buf = [sb(f"u2buf{i}", [P, D], F32) for i in range(2)]
        x2buf = sb("x2buf", [P, D], F32)
        yln = sb("yln", [P, D], F32)
        xn3 = sb("xn3", [P, D], BF16)
        xn3T = sb("xn3T", [P, NDT * P], BF16)
        e8 = sb("e8", [P, 8], F32)
        g8 = sb("g8", [P, 8], F32)
        top8 = sb("top8", [P, 8], F32)
        sm_sc = sb("sm_sc", [P, 4], F32)
        tzb = sb("tzb", [P, 32], F32R)
        loraT = [sb(f"loraT{i}", [64, P], F32R) for i in range(2)]
        st4 = sb("st4", [P, 12], F32)
        mv4 = sb("mv4", [P, 2], F32)
        epscol = sb("epscol", [P, 1], F32)
        neghalf = sb("neghalf", [P, 1], F32)
        invDc = sb("invDc", [P, 1], F32)
        vdmy = sb("vdmy", [P, 4], F32)
        sdmy = sb("sdmy", [P, 4], F32)
        if SEG >= 1024:
            stg_a = ybuf[:].rearrange("p a b -> p (a b)")[:, 0:NDT * RK]
            stg_c = ybuf[:].rearrange("p a b -> p (a b)")[:, NDT * RK:NDT * RK + NDT * 32]
            stg_b = ybuf[0:96, 1, :]
            zr = ybuf[0:64, 2, :]
            aug32 = ybuf[0:32, 3, :]
            onesrow = bcbuf[0][0:1, 0, 0:max(SEG, D)]
            stg_row = bcbuf[0][0:1, 1, 0:D]
        else:
            stg_a = sb("stg_a", [P, NDT * RK], F32)[:]
            stg_b = sb("stg_b", [96, D], F32)[:]
            stg_c = sb("stg_c", [P, NDT * 32], F32)[:]
            stg_row = sb("stg_row", [1, D], F32)[:]
            onesrow = sb("onesrow", [1, max(SEG, D)], F32)[:]
            zr = sb("zr", [64, D], F32)[:]
            aug32 = sb("aug32", [32, D], F32)[:]

        px = psum("px", [96, 512])
        pt0 = psum("pt0", [P, 512])
        pt1 = psum("pt1", [P, 512])
        pdtz = psum("pdtz", [P, 512])
        pdtz2 = psum("pdtz2", [P, 512])
        pf0 = psum("pf0", [P, 512])
        pf1 = psum("pf1", [P, 512])
        psm = psum("psm", [P, 64])

        sems = {e: st.enter_context(nc.semaphore(f"sem_{e}")) for e in SEMS}
        V, S, T, Q0, Q2 = "V", "S", "T", "q0", "q2"

        if SEG >= 1024:
            lntmp = ystage[:][:, 0:D]
        else:
            lntmp = sb("lntmp", [P, D], F32)[:]
        last_use = {}
        ltok = {}

        def guard(key):
            t = last_use.get(key)
            if t is None:
                return []
            return list(t) if isinstance(t, (list, tuple)) else [t]

        def setg(key, *toks):
            last_use[key] = list(toks)

        pr.emit(Q2, lambda g: g.memset(ident_f[:], 0.0), inc=1, sem="G")
        t_id = pr.emit(Q2, lambda g: g.affine_select(
            out=ident_f[:], in_=ident_f[:], compare_op=ALU.not_equal,
            fill=1.0, base=0, pattern=[[-1, P]], channel_multiplier=1), inc=1, sem="G")
        t_idr = pr.emit(V, lambda v: v.tensor_copy(out=ident_r[:], in_=ident_f[:]),
                        waits=[t_id])
        t_idb = pr.emit(V, lambda v: v.tensor_copy(out=ident_b[:], in_=ident_f[:]),
                        waits=[t_id])

        t_wx = pr.emit(Q0, lambda s: s.dma_start(out=stg_b, in_=wx_d))
        t_wdt = pr.emit(Q0, lambda s: s.dma_start(
            out=stg_a.rearrange("p (c r) -> p c r", c=NDT),
            in_=wdt_d.rearrange("(c p) r -> p c r", p=P)))
        t_bdt = pr.emit(Q0, lambda s: s.dma_start(out=stg_row, in_=bdt_d))
        t_alog = pr.emit(Q0, lambda s: s.dma_start(
            out=acol[:].rearrange("p (c n) -> p c n", c=NDT),
            in_=alog_d.rearrange("(c p) n -> p c n", p=P)))
        t_dpar = pr.emit(Q0, lambda s: s.dma_start(
            out=dcol[:].rearrange("p (o c) -> p o c", o=1),
            in_=dpar_d.rearrange("o (c p) -> p o c", p=P),
            allow_slow_non_contiguous=True))

        t_zrm = pr.emit(V, lambda v: v.memset(zr, 0.0))
        pr.emit(V, lambda v: v.memset(epscol[:], 1e-5))
        pr.emit(V, lambda v: v.memset(neghalf[:], -0.5))
        pr.emit(V, lambda v: v.memset(invDc[:], 1.0 / D))
        t_zr = pr.emit(V, lambda v: v.memset(onesrow, 1.0))
        t_carry0 = pr.emit(V, lambda v: v.memset(carry[:], 0.0))
        t_ae = pr.emit(S, lambda s: s.activation(out=acol[:], in_=acol[:],
                                                 func=ACT.Exp), waits=[t_alog])
        t_acol = pr.emit(S, lambda s: s.activation(out=acol[:], in_=acol[:],
                                                  func=ACT.Identity, scale=-1.0))
        pr.emit(V, lambda v: v.memset(aug32, 0.0))
        t_aug1 = pr.emit(V, lambda v: v.memset(aug32[0:1, :], 1.0))
        pr.emit(S, lambda s: s.copy(out=dt_rawT_aug[64:96, :], in_=aug32[:, 0:SEG]),
                waits=[t_aug1])
        t_aug2 = pr.emit(S, lambda s: s.copy(out=aug32[0:1, :], in_=stg_row),
                         waits=[t_bdt])
        pr.emit(S, lambda s: s.copy(out=wdtT_aug[64:96, :], in_=aug32[:, 0:NDT * P]),
                waits=[t_aug2])
        for dt in range(NDT):
            tt = pr.emit(T, lambda t, dt=dt: t.transpose(
                pt0[0:RK, 0:P], in_=stg_a[:, dt * RK:(dt + 1) * RK],
                identity=ident_f[:]), waits=[t_wdt, t_id] + guard(("pt0",)))
            te = pr.emit(S, lambda s, dt=dt: s.copy(
                out=wdtT_aug[0:RK, dt * P:(dt + 1) * P], in_=pt0[0:RK, 0:P]), waits=[tt])
            setg(("pt0",), te)
        wx_tts = []
        for c in range(NDT):
            tt = pr.emit(T, lambda t, c=c: t.transpose(
                pt1[:, 0:96], in_=stg_b[0:96, c * P:(c + 1) * P],
                identity=ident_f[0:96, 0:96]), waits=[t_wx, t_id] + guard(("pt1",)))
            te = pr.emit(S, lambda s, c=c: s.copy(out=wx_lhsT[:, c, :],
                                                  in_=pt1[:, 0:96]), waits=[tt])
            setg(("pt1",), te)
            wx_tts.append(tt)
        t_wxT = pr.tok(S)
        setg(("stg_b",), *wx_tts)
        t_wdtT = t_wxT
        for c in range(NDT):
            tl = pr.emit(Q0, lambda s, c=c: s.dma_start(
                out=stg_b[0:96, :], in_=wlin_d[c * P:c * P + 96, :]),
                waits=guard(("stg_b",)))
            tl2 = pr.emit(Q0, lambda s, c=c: s.dma_start(
                out=zr[0:32, :], in_=wlin_d[c * P + 96:(c + 1) * P, :]),
                waits=[t_zrm] + guard(("zrw",)))
            es = []
            for b in range(NDT):
                tt = pr.emit(T, lambda t, b=b: t.transpose(
                    pt0[:, 0:96], in_=stg_b[0:96, b * P:(b + 1) * P],
                    identity=ident_f[0:96, 0:96]), waits=[tl, t_id] + guard(("pt0",)))
                tt2 = pr.emit(T, lambda t, b=b: t.transpose(
                    pt1[:, 0:32], in_=zr[0:32, b * P:(b + 1) * P],
                    identity=ident_f[0:32, 0:32]), waits=[tl2, t_id] + guard(("pt1",)))
                e1 = pr.emit(S, lambda s, c=c, b=b: s.copy(
                    out=wlinT[:, b, c * P:c * P + 96], in_=pt0[:, 0:96]), waits=[tt])
                e2 = pr.emit(S, lambda s, c=c, b=b: s.copy(
                    out=wlinT[:, b, c * P + 96:(c + 1) * P], in_=pt1[:, 0:32]),
                    waits=[tt2])
                setg(("pt0",), e1)
                setg(("pt1",), e2)
                es += [e1, e2]
            setg(("stg_b",), *es)
            setg(("zrw",), *es)
        t_gw = pr.emit(Q0, lambda s: s.dma_start(out=stg_b[0:8, :], in_=gw_d),
                       waits=guard(("stg_b",)))
        for c in range(NDT):
            tt = pr.emit(T, lambda t, c=c: t.transpose(
                pt0[:, 0:8], in_=stg_b[0:8, c * P:(c + 1) * P],
                identity=ident_f[0:8, 0:8]), waits=[t_gw, t_id] + guard(("pt0",)))
            te = pr.emit(S, lambda s, c=c: s.copy(out=glT[:, c, 0:8],
                                                  in_=pt0[:, 0:8]), waits=[tt])
            setg(("pt0",), te)
        t_lfs = []
        for e_ in range(E):
            t_lfs.append(pr.emit(Q0, lambda s, e_=e_: s.dma_start(
                out=stg_c.rearrange("p (c e r) -> p c e r", c=NDT, e=E)[:, :, e_, :],
                in_=left_d[e_].rearrange("(c p) r -> p c r", p=P))))
        t_lf = t_lfs[-1]
        t_glT = pr.emit(V, lambda v: v.tensor_copy(
            out=glT[:, :, 8:40],
            in_=stg_c.rearrange("p (c k) -> p c k", c=NDT)), waits=t_lfs)
        t_rg = pr.emit(Q0, lambda s: s.dma_start(out=stg_b[0:32, :], in_=right_d),
                       waits=[t_gw, pr.tok(T)])
        t_bl = pr.emit(Q0, lambda s: s.dma_start(out=stg_row, in_=blin_d),
                       waits=[pr.tok(S)])
        pr.emit(S, lambda s: s.copy(out=right_aug[0:32, :], in_=stg_b[0:32, :]),
                waits=[t_rg])
        t_aug3 = pr.emit(S, lambda s: s.copy(out=aug32[0:1, :], in_=stg_row),
                         waits=[t_bl])
        pr.emit(S, lambda s: s.copy(out=right_aug[32:64, :], in_=aug32[:, 0:D]),
                waits=[t_aug3])
        t_raug = pr.tok(S)
        t_aug4 = pr.emit(S, lambda s: s.copy(out=aug32[0:1, :], in_=onesrow[0:1, 0:D]),
                         waits=[t_zr])
        for i in range(2):
            pr.emit(S, lambda s, i=i: s.copy(out=loraT[i][32:64, :],
                                             in_=aug32[:, 0:P]), waits=[t_aug4])
        t_lorainit = pr.tok(S)

        t_initdone = [pr.tok("S"), pr.tok("T"), pr.tok("V"), pr.tok("q0"), pr.tok("q2"), pr.tok("G")]
        for k in range(4):
            setg(("ybuf", k), *t_initdone)
        setg(("bc", 0), *t_initdone)
        setg(("bc", 1), *t_initdone)
        prev_carry_S = None
        post_state = [None]

        def emit_post_A(hp, lt):
            sl = hp * SEG + lt * P
            yb = ylt[lt % 2]
            ub2 = u2buf[lt % 2]
            tyl = pr.emit(Q0, lambda s, sl=sl, yb=yb: s.dma_start(
                out=yb[:], in_=ylp_d[sl:sl + P, :]),
                waits=[ltok[("ydma", hp, dt)] for dt in range(NDT)]
                      + guard(("ylt", lt % 2)))
            tu2 = pr.emit(Q0, lambda s, sl=sl, ub2=ub2: s.dma_start(
                out=ub2[:], in_=u_d[sl:sl + P, :]), waits=guard(("u2", lt % 2)))
            t1 = pr.emit(V, lambda v, yb=yb: v.reduce_sum(
                out=statbuf[:, 0:1], in_=yb[:], axis=AXL.X),
                waits=[tyl] + guard(("stat",)))
            tsq2 = pr.emit(S, lambda s, yb=yb: s.activation(
                out=lntmp, in_=yb[:], func=ACT.Square),
                waits=[tyl] + guard(("lntmp",)) + guard(("lntmpR",)) + guard(("yst",)))
            tr2b = pr.emit(V, lambda v: v.reduce_sum(
                out=statbuf[:, 1:2], in_=lntmp, axis=AXL.X), waits=[tsq2])
            for _pi in range(3):
                pr.emit(V, lambda v, _pi=_pi: v.memset(vdmy[:, _pi:_pi+1], 0.0))
            t2f = pr.emit(V, lambda v: v.tensor_copy(
                out=vdmy[:, 0:2], in_=statbuf[:, 0:2]))
            ltok[("red2", 1)] = tr2b
            setg(("lntmpR",), tr2b)
            pr.emit(S, lambda s: s.activation(
                out=statbuf[:, 2:3], in_=statbuf[:, 0:1], func=ACT.Square,
                scale=invDc[:, 0:1]), waits=[t2f])
            pr.emit(S, lambda s: s.activation(
                out=statbuf[:, 3:4], in_=statbuf[:, 2:3], func=ACT.Identity,
                scale=-1.0))
            pr.emit(S, lambda s: s.activation(
                out=mvbuf[:, 0:1], in_=statbuf[:, 0:1], func=ACT.Identity,
                scale=invDc[:, 0:1]))
            tsf2 = pr.emit(S, lambda s: s.copy(out=sdmy[:, 2:4], in_=statbuf[:, 2:4]))
            pr.selfwait(S, tsf2)
            tvar2 = pr.emit(S, lambda s: s.activation(
                out=mvbuf[:, 1:2], in_=statbuf[:, 1:2], func=ACT.Identity,
                scale=invDc[:, 0:1], bias=statbuf[:, 3:4]))
            pr.emit(S, lambda s: s.activation(out=rcol[:, 0:1], in_=mvbuf[:, 1:2],
                                              func=ACT.Ln, bias=epscol[:, 0:1]))
            pr.emit(S, lambda s: s.activation(out=rcol[:, 0:1], in_=rcol[:, 0:1],
                                              func=ACT.Exp, scale=neghalf[:, 0:1]))
            tng2 = pr.emit(S, lambda s: s.activation(
                out=rcol[:, 1:2], in_=mvbuf[:, 0:1], func=ACT.Identity, scale=-1.0))
            setg(("mv",), tng2)
            tfl2 = pr.emit(S, lambda s: s.copy(out=sdmy[:, 0:2], in_=rcol[:, 0:2]))
            pr.selfwait(S, tfl2)
            ta2 = pr.emit(S, lambda s, yb=yb: s.activation(
                out=lntmp, in_=yb[:], func=ACT.Identity, bias=rcol[:, 1:2]),
                waits=[ltok[("red2", 1)]] + guard(("lntmp",)) + guard(("lntmpR",)) + guard(("yst",)))
            t6 = pr.emit(S, lambda s: s.activation(
                out=yln[:], in_=lntmp, func=ACT.Identity, scale=rcol[:, 0:1]),
                waits=guard(("yln",)))
            setg(("lntmp",), t6)
            setg(("ylt", lt % 2), t6)
            t7 = pr.emit(V, lambda v, ub2=ub2: v.tensor_tensor(
                out=x2buf[:], in0=ub2[:], in1=yln[:], op=ALU.add),
                waits=[tu2, t6] + guard(("x2",)))
            setg(("u2", lt % 2), t7)
            setg(("yln",), t7)
            t8 = pr.emit(V, lambda v: v.reduce_sum(
                out=statbuf[:, 0:1], in_=x2buf[:], axis=AXL.X),
                waits=[t7, tvar2])
            tsq3 = pr.emit(S, lambda s: s.activation(
                out=lntmp, in_=x2buf[:], func=ACT.Square),
                waits=[t7] + guard(("lntmp",)) + guard(("lntmpR",)) + guard(("yst",)))
            tr2c = pr.emit(V, lambda v: v.reduce_sum(
                out=statbuf[:, 1:2], in_=lntmp, axis=AXL.X), waits=[tsq3])
            for _pi in range(3):
                pr.emit(V, lambda v, _pi=_pi: v.memset(vdmy[:, _pi:_pi+1], 0.0))
            t9f = pr.emit(V, lambda v: v.tensor_copy(
                out=vdmy[:, 0:2], in_=statbuf[:, 0:2]))
            setg(("lntmpR",), tr2c)
            pr.emit(S, lambda s: s.activation(
                out=statbuf[:, 2:3], in_=statbuf[:, 0:1], func=ACT.Square,
                scale=invDc[:, 0:1]), waits=[t9f])
            pr.emit(S, lambda s: s.activation(
                out=statbuf[:, 3:4], in_=statbuf[:, 2:3], func=ACT.Identity,
                scale=-1.0))
            pr.emit(S, lambda s: s.activation(
                out=mvbuf[:, 0:1], in_=statbuf[:, 0:1], func=ACT.Identity,
                scale=invDc[:, 0:1]))
            tsf3 = pr.emit(S, lambda s: s.copy(out=sdmy[:, 2:4], in_=statbuf[:, 2:4]))
            pr.selfwait(S, tsf3)
            tvar3 = pr.emit(S, lambda s: s.activation(
                out=mvbuf[:, 1:2], in_=statbuf[:, 1:2], func=ACT.Identity,
                scale=invDc[:, 0:1], bias=statbuf[:, 3:4]))
            setg(("stat",), tvar3)
            pr.emit(S, lambda s: s.activation(out=rcol[:, 0:1], in_=mvbuf[:, 1:2],
                                              func=ACT.Ln, bias=epscol[:, 0:1]))
            pr.emit(S, lambda s: s.activation(out=rcol[:, 0:1], in_=rcol[:, 0:1],
                                              func=ACT.Exp, scale=neghalf[:, 0:1]))
            tng3 = pr.emit(S, lambda s: s.activation(
                out=rcol[:, 1:2], in_=mvbuf[:, 0:1], func=ACT.Identity, scale=-1.0))
            setg(("mv",), tng3)
            tfl3 = pr.emit(S, lambda s: s.copy(out=sdmy[:, 0:2], in_=rcol[:, 0:2]))
            pr.selfwait(S, tfl3)
            ta3 = pr.emit(S, lambda s: s.activation(
                out=lntmp, in_=x2buf[:], func=ACT.Identity, bias=rcol[:, 1:2]),
                waits=[tr2c] + guard(("lntmp",)) + guard(("lntmpR",)) + guard(("yst",)))
            td_ = pr.emit(S, lambda s: s.activation(
                out=xn3[:], in_=lntmp, func=ACT.Identity, scale=rcol[:, 0:1]),
                waits=guard(("xn3",)))
            setg(("lntmp",), td_)
            tes = []
            tts = []
            for c in range(NDT):
                pb, pk = ((pt0, "pt0"), (pt1, "pt1"))[c % 2]
                tt = pr.emit(T, lambda t, c=c, pb=pb: t.transpose(
                    pb[:].bitcast(BF16)[:, 0:P], in_=xn3[:, c * P:(c + 1) * P],
                    identity=ident_b[:]), waits=[td_, t_idb] + guard((pk,)))
                te = pr.emit(S, lambda s, c=c, pb=pb: s.copy(
                    out=xn3T[:, c * P:(c + 1) * P], in_=pb[:].bitcast(BF16)[:, 0:P]),
                    waits=[tt] + guard(("xn3T",)))
                setg((pk,), te)
                tes.append(te)
                tts.append(tt)
            setg(("xn3",), *tts)
            for c in range(NDT):
                w0 = [tes[c], t_wxT]
                if c == 0:
                    w0 += guard(("pf",)) + [t_glT]
                stp = False
                tm1 = pr.emit(T, lambda t, c=c, stp=stp: t.matmul(
                    pf0[:], lhsT=xn3T[:, c * P:(c + 1) * P], rhs=wlinT[:, c, 0:512],
                    start=(c == 0), stop=stp), waits=w0)
                pr.emit(T, lambda t, c=c, stp=stp: t.matmul(
                    pf1[:], lhsT=xn3T[:, c * P:(c + 1) * P],
                    rhs=wlinT[:, c, 512:1024],
                    start=(c == 0), stop=stp))
                if True:
                    w2 = guard(("psm",)) if c == 0 else []
                    pr.emit(T, lambda t, c=c: t.matmul(
                        psm[:, 0:40], lhsT=xn3T[:, c * P:(c + 1) * P], rhs=glT[:, c, :],
                        start=(c == 0), stop=(c == NDT - 1)), waits=w2)
            setg(("xn3T",), pr.tok(T))
            t_mm = pr.tok(T)
            return t7, t_mm

        def emit_post_B(hp, lt, t7, t_mm):
            sl = hp * SEG + lt * P
            tsm1 = pr.emit(V, lambda v: v.reduce_max(
                out=sm_sc[:, 0:1], in_=psm[:, 0:8], axis=AXL.X, negate=True),
                waits=[t_mm])
            for _pi in range(3):
                pr.emit(V, lambda v, _pi=_pi: v.memset(vdmy[:, _pi:_pi+1], 0.0))
            tsm1f = pr.emit(V, lambda v: v.tensor_copy(out=vdmy[:, 0:1],
                                                       in_=sm_sc[:, 0:1]))
            tsm2 = pr.emit(S, lambda s: s.activation(
                out=e8[:], in_=psm[:, 0:8], func=ACT.Exp, bias=sm_sc[:, 0:1]),
                waits=[tsm1f] + guard(("e8",)))
            tsm3 = pr.emit(V, lambda v: v.reduce_sum(
                out=sm_sc[:, 1:2], in_=e8[:], axis=AXL.X), waits=[tsm2])
            tsm5 = pr.emit(V, lambda v: v.max(out=top8[:], in_=e8[:]))
            for _pi in range(3):
                pr.emit(V, lambda v, _pi=_pi: v.memset(vdmy[:, _pi:_pi+1], 0.0))
            pr.emit(V, lambda v: v.reciprocal(out=sm_sc[:, 2:3], in_=sm_sc[:, 1:2]))
            pr.emit(V, lambda v: v.tensor_copy(out=vdmy[:, 0:2], in_=sm_sc[:, 1:3]))
            pr.emit(V, lambda v: v.tensor_copy(out=vdmy[:, 2:3], in_=top8[:, 1:2]))
            pr.selfwait(V)
            tsm6 = pr.emit(V, lambda v: v.scalar_tensor_tensor(
                out=g8[:], in0=e8[:], scalar=top8[:, 1:2], in1=e8[:],
                op0=ALU.is_ge, op1=ALU.mult))
            tz_t = pr.emit(V, lambda v: v.scalar_tensor_tensor(
                out=tzb[:], in0=psm[:, 8:40], scalar=sm_sc[:, 2:3],
                in1=g8[:].rearrange("p (e o) -> p e o", o=1).to_broadcast([P, 8, 4]),
                op0=ALU.mult, op1=ALU.mult), waits=guard(("tzb",)))
            setg(("e8",), tz_t)
            setg(("psm",), tz_t, tsm2)
            lb = loraT[lt % 2]
            tt = pr.emit(T, lambda t: t.transpose(
                pt0[:].bitcast(F32R)[0:32, 0:P], in_=tzb[:], identity=ident_r[:]),
                waits=[tz_t, t_idr] + guard(("pt0",)))
            te = pr.emit(S, lambda s, lb=lb: s.copy(
                out=lb[0:32, :], in_=pt0[:].bitcast(F32R)[0:32, 0:P]),
                waits=[tt, t_lorainit] + guard(("lora", lt % 2)))
            setg(("pt0",), te)
            setg(("tzb",), tt)
            tl1 = pr.emit(T, lambda t, lb=lb: t.matmul(
                pf0[:], lhsT=lb[:], rhs=right_aug[:, 0:512], start=False, stop=True),
                waits=[te, t_raug])
            tl2 = pr.emit(T, lambda t, lb=lb: t.matmul(
                pf1[:], lhsT=lb[:], rhs=right_aug[:, 512:1024], start=False,
                stop=True))
            setg(("lora", lt % 2), tl2)
            t_ffn = pr.tok(T)
            tsq4a = pr.emit(S, lambda s: s.activation(
                out=lntmp[:, 0:512], in_=pf0[:], func=ACT.Square),
                waits=[t_ffn] + guard(("lntmp",)) + guard(("lntmpR",)) + guard(("yst",)) + [t7])
            tsq4b = pr.emit(S, lambda s: s.activation(
                out=lntmp[:, 512:1024], in_=pf1[:], func=ACT.Square))
            tn1 = pr.emit(V, lambda v: v.reduce_sum(
                out=st4[:, 0:1], in_=pf0[:], axis=AXL.X), waits=[tsq4b])
            pr.emit(V, lambda v: v.reduce_sum(
                out=st4[:, 1:2], in_=pf1[:], axis=AXL.X))
            tr2d = pr.emit(V, lambda v: v.reduce_sum(
                out=st4[:, 2:3], in_=lntmp, axis=AXL.X), waits=[tsq4b])
            for _pi in range(3):
                pr.emit(V, lambda v, _pi=_pi: v.memset(vdmy[:, _pi:_pi+1], 0.0))
            tn3f = pr.emit(V, lambda v: v.tensor_copy(
                out=vdmy[:, 2:4], in_=st4[:, 1:3]))
            setg(("lntmpR",), tr2d)
            pr.emit(S, lambda s: s.activation(
                out=st4[:, 3:4], in_=st4[:, 0:1], func=ACT.Identity,
                bias=st4[:, 1:2]), waits=[tn3f])
            pr.emit(S, lambda s: s.activation(
                out=st4[:, 4:5], in_=st4[:, 3:4], func=ACT.Square,
                scale=invDc[:, 0:1]))
            pr.emit(S, lambda s: s.activation(
                out=st4[:, 6:7], in_=st4[:, 4:5], func=ACT.Identity,
                scale=-1.0))
            pr.emit(S, lambda s: s.activation(
                out=mv4[:, 0:1], in_=st4[:, 3:4], func=ACT.Identity,
                scale=invDc[:, 0:1]))
            tsf4 = pr.emit(S, lambda s: s.copy(out=sdmy[:, 2:3], in_=st4[:, 6:7]))
            pr.selfwait(S, tsf4)
            tvar4 = pr.emit(S, lambda s: s.activation(
                out=mv4[:, 1:2], in_=st4[:, 2:3], func=ACT.Identity,
                scale=invDc[:, 0:1], bias=st4[:, 6:7]))
            pr.emit(S, lambda s: s.activation(out=rcol[:, 0:1], in_=mv4[:, 1:2],
                                              func=ACT.Ln, bias=epscol[:, 0:1]))
            pr.emit(S, lambda s: s.activation(out=rcol[:, 0:1], in_=rcol[:, 0:1],
                                              func=ACT.Exp, scale=neghalf[:, 0:1]))
            tng4 = pr.emit(S, lambda s: s.activation(
                out=rcol[:, 1:2], in_=mv4[:, 0:1], func=ACT.Identity, scale=-1.0))
            setg(("mv4",), tng4)
            tfl4 = pr.emit(S, lambda s: s.copy(out=sdmy[:, 0:2], in_=rcol[:, 0:2]))
            pr.selfwait(S, tfl4)
            ta4a = pr.emit(S, lambda s: s.activation(
                out=lntmp[:, 0:512], in_=pf0[:], func=ACT.Identity, bias=rcol[:, 1:2]),
                waits=[tr2d] + guard(("lntmp",)) + guard(("lntmpR",)) + guard(("yst",)))
            ta4b = pr.emit(S, lambda s: s.activation(
                out=lntmp[:, 512:1024], in_=pf1[:], func=ACT.Identity, bias=rcol[:, 1:2]))
            to1 = pr.emit(S, lambda s: s.activation(
                out=yln[:, 0:512], in_=lntmp[:, 0:512], func=ACT.Identity,
                scale=rcol[:, 0:1]), waits=guard(("obuf",)))
            to2 = pr.emit(S, lambda s: s.activation(
                out=yln[:, 512:1024], in_=lntmp[:, 512:1024], func=ACT.Identity,
                scale=rcol[:, 0:1]))
            setg(("lntmp",), to2)
            setg(("pf",), ta4b)
            tfin = pr.emit(V, lambda v: v.tensor_tensor(
                out=yln[:], in0=yln[:], in1=x2buf[:], op=ALU.add), waits=[to2])
            setg(("x2",), tfin)
            tod = pr.emit(Q0, lambda s, sl=sl: s.dma_start(
                out=out_d[sl:sl + P, :], in_=yln[:]), waits=[tfin])
            setg(("yln",), tod)
            setg(("obuf",), tod)

        for h in range(NH):
            lbase = h * SEG
            for sub in range(NSUB):
                for i in range(4):
                    lt = sub * 4 + i
                    sl = lbase + lt * P
                    ub = ubuf[lt % 2]
                    xb = xnbuf[lt % 2]
                    tu = pr.emit(Q0, lambda s, sl=sl, ub=ub: s.dma_start(
                        out=ub[:], in_=u_d[sl:sl + P, :]), waits=guard(("ub", lt % 2)))
                    pr.emit(V, lambda v, ub=ub: v.reduce_sum(
                        out=statbuf[:, 0:1], in_=ub[:], axis=AXL.X),
                        waits=[tu] + guard(("stat",)))
                    tsq = pr.emit(S, lambda s, ub=ub: s.activation(
                        out=lntmp, in_=ub[:], func=ACT.Square),
                        waits=[tu] + guard(("lntmp",)) + guard(("lntmpR",)) + guard(("yst",)))
                    if dbg and h == 0 and lt == 0:
                        pr.emit(Q0, lambda s: s.dma_start(out=dbgsq_d, in_=lntmp),
                                waits=[tsq])
                    tr2 = pr.emit(V, lambda v: v.reduce_sum(
                        out=statbuf[:, 1:2], in_=lntmp, axis=AXL.X), waits=[tsq])
                    for _pi in range(3):
                        pr.emit(V, lambda v, _pi=_pi: v.memset(vdmy[:, _pi:_pi+1], 0.0))
                    tmvf = pr.emit(V, lambda v: v.tensor_copy(
                        out=vdmy[:, 0:2], in_=statbuf[:, 0:2]))
                    ltok[("red2", 0)] = tr2
                    setg(("lntmpR",), tr2)
                    pr.emit(S, lambda s: s.activation(
                        out=statbuf[:, 2:3], in_=statbuf[:, 0:1], func=ACT.Square,
                        scale=invDc[:, 0:1]), waits=[tmvf])
                    pr.emit(S, lambda s: s.activation(
                        out=statbuf[:, 3:4], in_=statbuf[:, 2:3], func=ACT.Identity,
                        scale=-1.0))
                    pr.emit(S, lambda s: s.activation(
                        out=mvbuf[:, 0:1], in_=statbuf[:, 0:1], func=ACT.Identity,
                        scale=invDc[:, 0:1]))
                    tsf0 = pr.emit(S, lambda s: s.copy(out=sdmy[:, 2:4], in_=statbuf[:, 2:4]))
                    pr.selfwait(S, tsf0)
                    tvar = pr.emit(S, lambda s: s.activation(
                        out=mvbuf[:, 1:2], in_=statbuf[:, 1:2], func=ACT.Identity,
                        scale=invDc[:, 0:1], bias=statbuf[:, 3:4]))
                    tln1 = pr.emit(S, lambda s: s.activation(
                        out=rcol[:, 0:1], in_=mvbuf[:, 1:2], func=ACT.Ln, bias=epscol[:, 0:1]))
                    pr.emit(S, lambda s: s.activation(
                        out=rcol[:, 0:1], in_=rcol[:, 0:1], func=ACT.Exp, scale=neghalf[:, 0:1]))
                    tng = pr.emit(S, lambda s: s.activation(
                        out=rcol[:, 1:2], in_=mvbuf[:, 0:1], func=ACT.Identity, scale=-1.0))
                    setg(("mv",), tng)
                    tfl = pr.emit(S, lambda s: s.copy(out=sdmy[:, 0:2], in_=rcol[:, 0:2]))
                    pr.selfwait(S, tfl)
                    if dbg and h == 0:
                        pr.emit(S, lambda s, lt=lt: s.copy(
                            out=statbuf[:, 4:6], in_=rcol[:, 0:2]))
                        pr.emit(S, lambda s, lt=lt: s.copy(
                            out=statbuf[:, 6:8], in_=mvbuf[:, 0:2]))
                        pr.emit(S, lambda s, lt=lt: s.copy(
                            out=statbuf[:, 8:10], in_=statbuf[:, 1:3]))
                        pr.emit(Q0, lambda s, lt=lt: s.dma_start(
                            out=dbgscal_d[:, lt * 6:lt * 6 + 6], in_=statbuf[:, 4:10]),
                            waits=[pr.tok("S")])
                    if skip_ln:
                        txn = pr.emit(S, lambda s, ub=ub, xb=xb: s.copy(
                            out=xb[:], in_=ub[:]),
                            waits=[ltok[("red2", 0)]] + guard(("xb", lt % 2)))
                    else:
                        ta1 = pr.emit(S, lambda s, ub=ub: s.activation(
                            out=lntmp, in_=ub[:], func=ACT.Identity, bias=rcol[:, 1:2]),
                            waits=[ltok[("red2", 0)]] + guard(("lntmp",)) + guard(("lntmpR",)) + guard(("yst",)))
                        txn = pr.emit(S, lambda s, xb=xb: s.activation(
                            out=xb[:], in_=lntmp, func=ACT.Identity, scale=rcol[:, 0:1]),
                            waits=guard(("xb", lt % 2)))
                        setg(("lntmp",), txn)
                    setg(("ub", lt % 2), txn)
                    setg(("stat",), tvar)
                    tts = []
                    for c in range(NDT):
                        pb, pk = ((pt0, "pt0"), (pt1, "pt1"))[c % 2]
                        tt = pr.emit(T, lambda t, xb=xb, c=c, pb=pb: t.transpose(
                            pb[:].bitcast(F32R)[:, 0:P], in_=xb[:, c * P:(c + 1) * P],
                            identity=ident_r[:]), waits=[txn, t_idr] + guard((pk,)))
                        te = pr.emit(S, lambda s, c=c, i=i, pb=pb: s.copy(
                            out=mini[:, c, i * P:(i + 1) * P],
                            in_=pb[:].bitcast(F32R)[:, 0:P]),
                            waits=[tt] + guard(("mini", c)))
                        setg((pk,), te)
                        ltok[("minie", c, i)] = te
                        tts.append(tt)
                    setg(("xb", lt % 2), *tts)
                for c in range(NDT):
                    wts = [ltok[("minie", c, i)] for i in range(4)] + [t_wxT]
                    if c == 0:
                        wts += guard(("px",))
                    tm = pr.emit(T, lambda t, c=c: t.matmul(
                        px[:], lhsT=wx_lhsT[:, c, :], rhs=mini[:, c, :],
                        start=(c == 0), stop=(c == NDT - 1)), waits=wts)
                    tsp = pr.emit(Q2, lambda s, c=c, sub=sub, lbase=lbase: s.dma_start(
                        out=xnt_d[c, :, lbase + sub * 512:lbase + (sub + 1) * 512],
                        in_=mini[:, c, :].bitcast(F32)),
                        waits=[ltok[("minie", c, i)] for i in range(4)])
                    setg(("mini", c), tsp, tm)
                t_px = pr.tok(T)
                te1 = pr.emit(S, lambda s, sub=sub: s.copy(
                    out=dt_rawT_aug[0:64, sub * 512:(sub + 1) * 512],
                    in_=px[0:64, :]), waits=[t_px] + guard(("draw", sub)))
                te2 = pr.emit(S, lambda s: s.copy(out=bcstage[:], in_=px[64:96, :]),
                              waits=[t_px] + guard(("bcstage",)))
                pr.emit(Q2, lambda s, sub=sub, lbase=lbase: s.dma_start(
                    out=bc_d[0, :, lbase + sub * 512:lbase + (sub + 1) * 512],
                    in_=bcstage[0:16, :]), waits=[te2])
                tbc = pr.emit(Q2, lambda s, sub=sub, lbase=lbase: s.dma_start(
                    out=bc_d[1, :, lbase + sub * 512:lbase + (sub + 1) * 512],
                    in_=bcstage[16:32, :]))
                setg(("bcstage",), tbc)
                setg(("px",), te1, te2)
                ltok[("draw", sub)] = te1
                ltok[("bcd", sub)] = tbc

            for gi, dg in enumerate(DGRPS if not skip_scan else ()):
                for k, dt in enumerate(dg):
                    xs = xstrip[k % 2]
                    tx = pr.emit(Q0, lambda s, dt=dt, xs=xs, lbase=lbase: s.dma_start(
                        out=xs[:], in_=xnt_d[dt, :, lbase:lbase + SEG]),
                        waits=[ltok[("bcd", sub)] for sub in range(NSUB)]
                              + guard(("xs", k % 2)))
                    for sub in range(NSUB):
                        bix = (k * NSUB + sub) % 2
                        pzb = (pdtz, pdtz2)[bix]
                        tm = pr.emit(T, lambda t, dt=dt, sub=sub, pzb=pzb: t.matmul(
                            pzb[:], lhsT=wdtT_aug[:, dt * P:(dt + 1) * P],
                            rhs=dt_rawT_aug[:, sub * 512:(sub + 1) * 512],
                            start=True, stop=True),
                            waits=[t_wdtT, ltok[("draw", sub)]] + guard(("pdtz", bix)))
                        ts1 = pr.emit(S, lambda s, k=k, sub=sub, pzb=pzb: s.activation(
                            out=dtT[:, k, sub * 512:(sub + 1) * 512], in_=pzb[:],
                            func=ACT.Exp), waits=[tm] + guard(("dtTr", k)))
                        pr.emit(S, lambda s, k=k, sub=sub: s.activation(
                            out=dtT[:, k, sub * 512:(sub + 1) * 512],
                            in_=dtT[:, k, sub * 512:(sub + 1) * 512],
                            func=ACT.Ln, bias=1.0))
                        setg(("pdtz", bix), ts1)
                        ltok[("drawuse", sub)] = tm
                    t_dtT = pr.tok(S)
                    tw = pr.emit(V, lambda v, k=k, xs=xs: v.tensor_tensor(
                        out=wT[:, k, :], in0=dtT[:, k, :], in1=xs[:], op=ALU.mult),
                        waits=[tx, t_dtT])
                    ty0 = pr.emit(V, lambda v, k=k, dt=dt, xs=xs: v.tensor_scalar(
                        out=ybuf[:, k, :], in0=xs[:], scalar1=dcol[:, dt:dt + 1],
                        scalar2=None, op0=ALU.mult),
                        waits=[t_dpar, tx] + guard(("ybuf", k)))
                    setg(("xs", k % 2), ty0)
                    setg(("dtTr", k), tw)
                    ltok[("w", k)] = tw
                    ltok[("y0", k)] = ty0
                setg(("draw", 0), *[ltok[("drawuse", s_)] for s_ in range(NSUB)])
                if NSUB > 1:
                    setg(("draw", 1), *[ltok[("drawuse", s_)] for s_ in range(NSUB)])
                prev_carry_local = None
                t_abar_last = None
                for n in range(N):
                    bb = bcbuf[n % 2]
                    tb1 = pr.emit(Q2, lambda s, n=n, bb=bb, lbase=lbase: s.dma_start(
                        out=bb[:], in_=bc_d[:, n:n + 1, lbase:lbase + SEG]
                        .rearrange("bc o c -> o bc c").to_broadcast([P, 2, SEG])),
                        waits=[ltok[("bcd", sub)] for sub in range(NSUB)]
                              + guard(("bc", n % 2)))
                    tb2 = tb1
                    ta = []
                    for k, dt in enumerate(dg):
                        ta.append(pr.emit(S, lambda s, k=k, dt=dt, n=n: s.activation(
                            out=abuf[:, k, :], in_=dtT[:, k, :], func=ACT.Exp,
                            scale=acol[:, dt * N + n:dt * N + n + 1]),
                            waits=[t_acol] + guard(("abuf", k))))
                    tbv = pr.emit(V, lambda v, bb=bb: v.tensor_tensor(
                        out=bybuf[:], in0=wT[:],
                        in1=bb[:, 0:1, :].to_broadcast([P, 4, SEG]), op=ALU.mult),
                        waits=[tb1, ltok[("w", 0)], ltok[("w", 3)]])
                    if n == 0 and h == 0 and gi == 0:
                        pr.selfwait(V, t_carry0)
                    if prev_carry_S is not None:
                        pr.selfwait(V, prev_carry_S)
                    tscan = []
                    for k, dt in enumerate(dg):
                        cix = n * NDT + dt
                        tscan.append(pr.emit(V, lambda v, k=k, cix=cix: v.tensor_tensor_scan(
                            out=abuf[:, k, :], data0=abuf[:, k, :], data1=bybuf[:, k, :],
                            initial=carry[:, cix:cix + 1], op0=ALU.mult, op1=ALU.add),
                            waits=[ta[-1]]))
                    t_scans = pr.tok(V)
                    c0 = n * NDT + dg[0]
                    tcs = pr.emit(S, lambda s, c0=c0: s.copy(
                        out=carry[:, c0:c0 + 4],
                        in_=abuf[:, :, SEG - 1:SEG].rearrange("p k o -> p (k o)")),
                        waits=tscan)
                    prev_carry_local = tcs
                    tyk = pr.emit(V, lambda v, bb=bb: v.tensor_tensor(
                        out=bybuf[:], in0=abuf[:],
                        in1=bb[:, 1:2, :].to_broadcast([P, 4, SEG]), op=ALU.mult),
                        waits=[tb2])
                    setg(("bc", n % 2), tyk)
                    for k in range(4):
                        setg(("abuf", k), tyk, tcs)
                    tya = pr.emit(V, lambda v: v.tensor_tensor(
                        out=ybuf[:], in0=ybuf[:], in1=bybuf[:], op=ALU.add),
                        waits=[ltok[("y0", 0)], ltok[("y0", 3)]])
                    j = gi * N + n
                    if h > 0 and j % 4 == 1:
                        post_state[0] = emit_post_A(h - 1, j // 4)
                    elif h > 0 and j % 4 == 3:
                        emit_post_B(h - 1, j // 4, *post_state[0])
                prev_carry_S = prev_carry_local
                for k in range(4):
                    setg(("dtTr", k), pr.tok(S), pr.tok(V))
                t_y_done = pr.tok(V)
                for k, dt in enumerate(dg):
                    yes = []
                    for lt in range(NLT):
                        pb, pk = ((pt0, "pt0"), (pt1, "pt1"))[lt % 2]
                        tt = pr.emit(T, lambda t, k=k, lt=lt, pb=pb: t.transpose(
                            pb[:, 0:P], in_=ybuf[:, k, lt * P:(lt + 1) * P],
                            identity=ident_f[:]), waits=[t_y_done, t_id] + guard((pk,)))
                        te = pr.emit(S, lambda s, lt=lt, pb=pb: s.copy(
                            out=ystage[:, lt * P:(lt + 1) * P], in_=pb[:, 0:P]),
                            waits=[tt] + guard(("yst",)) + guard(("lntmp",)))
                        setg((pk,), te)
                        yes.append(te)
                    ty = pr.emit(Q2, lambda s, dt=dt, lbase=lbase: s.dma_start(
                        out=ylp_d[lbase:lbase + SEG, dt * P:(dt + 1) * P]
                        .rearrange("(lt p) c -> p lt c", p=P),
                        in_=ystage[:].rearrange("p (lt c) -> p lt c", c=P)),
                        waits=yes)
                    setg(("yst",), ty)
                    ltok[("ydma", h, dt)] = ty
                for k in range(4):
                    setg(("ybuf", k), ltok[("ydma", h, dg[-1])])


        for lt in range(NLT):
            _pa = emit_post_A(NH - 1, lt)
            emit_post_B(NH - 1, lt, *_pa)

        pr.selfwait(Q0)
        pr.selfwait(Q2)

        with nc.Block() as blk:
            def run_stream(name):
                def body(eng):
                    for wl, fn, inc, semn in pr.streams[name]:
                        for (pe, v) in wl:
                            eng.wait_ge(sems[pe], v)
                        if fn is not None:
                            inst = fn(eng)
                            if inc:
                                inst.then_inc(sems[semn], inc)
                return body
            blk.vector(run_stream("V"))
            blk.scalar(run_stream("S"))
            blk.tensor(run_stream("T"))
            blk.sync(run_stream("q0"))
            blk.gpsimd(run_stream("q2"))

    return nc


def prep_inputs(inp, L):
    right = np.ascontiguousarray(np.asarray(inp["L0_ffn_right"], np.float32)
                                 .reshape(E * R, D))
    base = {
        "A_log": np.ascontiguousarray(inp["L0_attn_A_log"], dtype=np.float32),
        "W_x": np.ascontiguousarray(inp["L0_attn_W_x"], dtype=np.float32),
        "W_dt": np.ascontiguousarray(inp["L0_attn_W_dt"], dtype=np.float32),
        "b_dt": np.ascontiguousarray(np.asarray(inp["L0_attn_b_dt"], np.float32)
                                     .reshape(1, D)),
        "D_param": np.ascontiguousarray(np.asarray(inp["L0_attn_D_param"], np.float32)
                                        .reshape(1, D)),
        "W_lin": np.ascontiguousarray(inp["L0_ffn_W_lin"], dtype=np.float32),
        "b_lin": np.ascontiguousarray(np.asarray(inp["L0_ffn_b_lin"], np.float32)
                                      .reshape(1, D)),
        "gate_w": np.ascontiguousarray(inp["L0_ffn_gate_w"], dtype=np.float32),
        "left": np.ascontiguousarray(inp["L0_ffn_left"], dtype=np.float32),
        "right": right,
    }
    u = np.asarray(inp["L0_attn_u"], np.float32)
    maps = []
    for b in range(u.shape[0]):
        m = dict(base)
        m["u"] = np.ascontiguousarray(u[b, :L, :])
        maps.append(m)
    return maps


def kernel(**inputs):
    from concourse.bass_utils import run_bass_kernel_spmd
    L = 4096
    nc = build_nc(L=L, SEG=1024)
    maps = prep_inputs(inputs, L)
    res = run_bass_kernel_spmd(nc, maps, core_ids=list(range(8)), trace=False)
    return np.stack([np.asarray(r["out"], np.float32) for r in res.results])


# version 03
# speedup vs baseline: 1.0992x; 1.0352x over previous
import numpy as np
from contextlib import ExitStack
import concourse.bass as bass
import concourse.mybir as mybir

F32 = mybir.dt.float32
F32R = mybir.dt.float32r
BF16 = mybir.dt.bfloat16
ALU = mybir.AluOpType
ACT = mybir.ActivationFunctionType
AXL = mybir.AxisListType

D = 1024
N = 16
RK = 64
NDT = 8
E = 8
R = 4
P = 128

ENGS = ("V", "S", "T", "q0", "q2")
SEMS = ("V", "S", "T", "q0", "q2", "G")
SEM2ENG = {"V": "V", "S": "S", "T": "T", "q0": "q0", "q2": "q2", "G": "q2"}


class Prog:
    def __init__(self):
        self.streams = {e: [] for e in ENGS}
        self.cnt = {s: 0 for s in SEMS}
        self.waited = {c: {p: 0 for p in SEMS} for c in ENGS}

    def emit(self, eng, fn, waits=(), inc=None, sem=None):
        sem = sem or eng
        if inc is None:
            inc = 16 if eng in ("q0", "q2") else 1
        if eng in ("q0", "q2") and sem == eng and self.cnt[eng] > 0:
            self.selfwait(eng)
        if eng == "T" and self.cnt["T"] >= 2:
            self.selfwait("T", ("T", self.cnt["T"] - 2))
        wl = []
        for tok in waits:
            if tok is None:
                continue
            pe, v = tok
            if SEM2ENG[pe] == eng:
                continue
            if v > self.waited[eng][pe]:
                self.waited[eng][pe] = v
                wl.append((pe, v))
        self.cnt[sem] += inc
        self.streams[eng].append((wl, fn, inc, sem))
        return (sem, self.cnt[sem])

    def selfwait(self, eng, tok=None):
        if tok is None:
            tok = (eng, self.cnt[eng])
        pe, v = tok
        if v > self.waited[eng][pe]:
            self.waited[eng][pe] = v
            self.streams[eng].append(([(pe, v)], None, 0, eng))

    def tok(self, sem):
        return (sem, self.cnt[sem])


def build_nc(L=4096, SEG=1024, name="mk"):
    dbg = False
    skip_post = False
    skip_scan = False
    skip_ln = False
    post_level = 4
    assert L % SEG == 0 and SEG % 512 == 0
    NH = L // SEG
    NSUB = SEG // 512
    NLT = SEG // 128
    DGRPS = ((0, 1, 2, 3), (4, 5, 6, 7))

    nc = bass.Bass("TRN2", target_bir_lowering=False, name=name,
                   detect_race_conditions=False)

    u_d = nc.dram_tensor("u", [L, D], F32, kind="ExternalInput").ap()
    alog_d = nc.dram_tensor("A_log", [D, N], F32, kind="ExternalInput").ap()
    wx_d = nc.dram_tensor("W_x", [96, D], F32, kind="ExternalInput").ap()
    wdt_d = nc.dram_tensor("W_dt", [D, RK], F32, kind="ExternalInput").ap()
    bdt_d = nc.dram_tensor("b_dt", [1, D], F32, kind="ExternalInput").ap()
    dpar_d = nc.dram_tensor("D_param", [1, D], F32, kind="ExternalInput").ap()
    wlin_d = nc.dram_tensor("W_lin", [D, D], F32, kind="ExternalInput").ap()
    blin_d = nc.dram_tensor("b_lin", [1, D], F32, kind="ExternalInput").ap()
    gw_d = nc.dram_tensor("gate_w", [E, D], F32, kind="ExternalInput").ap()
    left_d = nc.dram_tensor("left", [E, D, R], F32, kind="ExternalInput").ap()
    right_d = nc.dram_tensor("right", [E * R, D], F32, kind="ExternalInput").ap()
    out_d = nc.dram_tensor("out", [L, D], F32, kind="ExternalOutput").ap()
    knd = "ExternalOutput" if dbg else "Internal"
    dbgscal_d = nc.dram_tensor("dbgscal", [P, 64], F32, kind=knd).ap()
    dbgsq_d = nc.dram_tensor("dbgsq", [P, D], F32, kind=knd).ap()
    bc_d = nc.dram_tensor("bc_scr", [2, 16, L], F32, kind=knd).ap()
    xnt_d = nc.dram_tensor("xnt_scr", [NDT, P, L], F32, kind=knd).ap()
    ylp_d = nc.dram_tensor("ylp_scr", [L, D], F32, kind=knd).ap()

    pr = Prog()
    st = ExitStack()

    def sb(nm, shape, dt):
        return st.enter_context(nc.sbuf_tensor(nm, shape, dt))

    def psum(nm, shape):
        return st.enter_context(nc.psum_tensor(nm, shape, F32))

    with st:
        ident_r = sb("ident_r", [P, P], F32R)
        ident_b = sb("ident_b", [P, P], BF16)
        ident_f = sb("ident_f", [P, P], F32)
        wx_lhsT = sb("wx_lhsT", [P, NDT, 96], F32R)
        wdtT_aug = sb("wdtT_aug", [96, NDT * P], F32R)
        wlinT = sb("wlinT", [P, NDT, D], BF16)
        glT = sb("glT", [P, NDT, 40], BF16)
        right_aug = sb("right_aug", [64, D], F32R)
        acol = sb("acol", [P, NDT * N], F32)
        dcol = sb("dcol", [P, NDT], F32)
        dt_rawT_aug = sb("dt_rawT_aug", [96, SEG], F32R)
        carry = sb("carry", [P, N * NDT], F32)
        dtT = sb("dtT", [P, 4, SEG], F32)
        wT = sb("wT", [P, 4, SEG], F32)
        ybuf = sb("ybuf", [P, 4, SEG], F32)
        abuf = sb("abuf", [P, 4, SEG], F32)
        bybuf = sb("bybuf", [P, 4, SEG], F32)
        bcbuf = [sb(f"bcbuf{i}", [P, 2, SEG], F32) for i in range(2)]
        xstrip = [sb(f"xstrip{i}", [P, SEG], F32) for i in range(4)]
        ubuf = [sb(f"ubuf{i}", [P, D], F32) for i in range(2)]
        xnbuf = [sb(f"xnbuf{i}", [P, D], F32R) for i in range(2)]
        mini = sb("mini", [P, NDT, 512], F32R)
        bcstage = sb("bcstage", [32, 512], F32)
        statbuf = sb("statbuf", [P, 12], F32)
        mvbuf = sb("mvbuf", [P, 2], F32)
        rcol = sb("rcol", [P, 2], F32)
        ystage = sb("ystage", [P, SEG], F32)
        ylt = [sb("ylt0", [P, D], F32)]
        u2buf = [sb("u2buf0", [P, D], F32)]
        x2buf = sb("x2buf", [P, D], F32)
        yln = sb("yln", [P, D], F32)
        xn3 = sb("xn3", [P, D], BF16)
        xn3T = sb("xn3T", [P, NDT * P], BF16)
        e8 = sb("e8", [P, 8], F32)
        g8 = sb("g8", [P, 8], F32)
        top8 = sb("top8", [P, 8], F32)
        sm_sc = sb("sm_sc", [P, 4], F32)
        tzb = sb("tzb", [P, 32], F32R)
        loraT = [sb(f"loraT{i}", [64, P], F32R) for i in range(2)]
        st4 = sb("st4", [P, 12], F32)
        mv4 = sb("mv4", [P, 2], F32)
        epscol = sb("epscol", [P, 1], F32)
        neghalf = sb("neghalf", [P, 1], F32)
        invDc = sb("invDc", [P, 1], F32)
        vdmy = sb("vdmy", [P, 4], F32)
        sdmy = sb("sdmy", [P, 4], F32)
        if SEG >= 1024:
            stg_a = ybuf[:].rearrange("p a b -> p (a b)")[:, 0:NDT * RK]
            stg_c = ybuf[:].rearrange("p a b -> p (a b)")[:, NDT * RK:NDT * RK + NDT * 32]
            stg_b = ybuf[0:96, 1, :]
            zr = ybuf[0:64, 2, :]
            aug32 = ybuf[0:32, 3, :]
            onesrow = bcbuf[0][0:1, 0, 0:max(SEG, D)]
            stg_row = bcbuf[0][0:1, 1, 0:D]
        else:
            stg_a = sb("stg_a", [P, NDT * RK], F32)[:]
            stg_b = sb("stg_b", [96, D], F32)[:]
            stg_c = sb("stg_c", [P, NDT * 32], F32)[:]
            stg_row = sb("stg_row", [1, D], F32)[:]
            onesrow = sb("onesrow", [1, max(SEG, D)], F32)[:]
            zr = sb("zr", [64, D], F32)[:]
            aug32 = sb("aug32", [32, D], F32)[:]

        px = psum("px", [96, 512])
        pt0 = psum("pt0", [P, 512])
        pt1 = psum("pt1", [P, 512])
        pdtz = psum("pdtz", [P, 512])
        pdtz2 = psum("pdtz2", [P, 512])
        pf0 = psum("pf0", [P, 512])
        pf1 = psum("pf1", [P, 512])
        psm = psum("psm", [P, 64])

        sems = {e: st.enter_context(nc.semaphore(f"sem_{e}")) for e in SEMS}
        V, S, T, Q0, Q2 = "V", "S", "T", "q0", "q2"

        if SEG >= 1024:
            lntmp = ystage[:][:, 0:D]
        else:
            lntmp = sb("lntmp", [P, D], F32)[:]
        last_use = {}
        ltok = {}

        def guard(key):
            t = last_use.get(key)
            if t is None:
                return []
            return list(t) if isinstance(t, (list, tuple)) else [t]

        def setg(key, *toks):
            last_use[key] = list(toks)

        pr.emit(Q2, lambda g: g.memset(ident_f[:], 0.0), inc=1, sem="G")
        t_id = pr.emit(Q2, lambda g: g.affine_select(
            out=ident_f[:], in_=ident_f[:], compare_op=ALU.not_equal,
            fill=1.0, base=0, pattern=[[-1, P]], channel_multiplier=1), inc=1, sem="G")
        t_idr = pr.emit(V, lambda v: v.tensor_copy(out=ident_r[:], in_=ident_f[:]),
                        waits=[t_id])
        t_idb = pr.emit(V, lambda v: v.tensor_copy(out=ident_b[:], in_=ident_f[:]),
                        waits=[t_id])

        t_wx = pr.emit(Q0, lambda s: s.dma_start(out=stg_b, in_=wx_d))
        t_wdt = pr.emit(Q0, lambda s: s.dma_start(
            out=stg_a.rearrange("p (c r) -> p c r", c=NDT),
            in_=wdt_d.rearrange("(c p) r -> p c r", p=P)))
        t_bdt = pr.emit(Q0, lambda s: s.dma_start(out=stg_row, in_=bdt_d))
        t_alog = pr.emit(Q0, lambda s: s.dma_start(
            out=acol[:].rearrange("p (c n) -> p c n", c=NDT),
            in_=alog_d.rearrange("(c p) n -> p c n", p=P)))
        t_dpar = pr.emit(Q0, lambda s: s.dma_start(
            out=dcol[:].rearrange("p (o c) -> p o c", o=1),
            in_=dpar_d.rearrange("o (c p) -> p o c", p=P),
            allow_slow_non_contiguous=True))

        t_zrm = pr.emit(V, lambda v: v.memset(zr, 0.0))
        pr.emit(V, lambda v: v.memset(epscol[:], 1e-5))
        pr.emit(V, lambda v: v.memset(neghalf[:], -0.5))
        pr.emit(V, lambda v: v.memset(invDc[:], 1.0 / D))
        t_zr = pr.emit(V, lambda v: v.memset(onesrow, 1.0))
        t_carry0 = pr.emit(V, lambda v: v.memset(carry[:], 0.0))
        t_ae = pr.emit(S, lambda s: s.activation(out=acol[:], in_=acol[:],
                                                 func=ACT.Exp), waits=[t_alog])
        t_acol = pr.emit(S, lambda s: s.activation(out=acol[:], in_=acol[:],
                                                  func=ACT.Identity, scale=-1.0))
        pr.emit(V, lambda v: v.memset(aug32, 0.0))
        t_aug1 = pr.emit(V, lambda v: v.memset(aug32[0:1, :], 1.0))
        pr.emit(S, lambda s: s.copy(out=dt_rawT_aug[64:96, :], in_=aug32[:, 0:SEG]),
                waits=[t_aug1])
        t_aug2 = pr.emit(S, lambda s: s.copy(out=aug32[0:1, :], in_=stg_row),
                         waits=[t_bdt])
        pr.emit(S, lambda s: s.copy(out=wdtT_aug[64:96, :], in_=aug32[:, 0:NDT * P]),
                waits=[t_aug2])
        for dt in range(NDT):
            tt = pr.emit(T, lambda t, dt=dt: t.transpose(
                pt0[0:RK, 0:P], in_=stg_a[:, dt * RK:(dt + 1) * RK],
                identity=ident_f[:]), waits=[t_wdt, t_id] + guard(("pt0",)))
            te = pr.emit(S, lambda s, dt=dt: s.copy(
                out=wdtT_aug[0:RK, dt * P:(dt + 1) * P], in_=pt0[0:RK, 0:P]), waits=[tt])
            setg(("pt0",), te)
        wx_tts = []
        for c in range(NDT):
            tt = pr.emit(T, lambda t, c=c: t.transpose(
                pt1[:, 0:96], in_=stg_b[0:96, c * P:(c + 1) * P],
                identity=ident_f[0:96, 0:96]), waits=[t_wx, t_id] + guard(("pt1",)))
            te = pr.emit(S, lambda s, c=c: s.copy(out=wx_lhsT[:, c, :],
                                                  in_=pt1[:, 0:96]), waits=[tt])
            setg(("pt1",), te)
            wx_tts.append(tt)
        t_wxT = pr.tok(S)
        setg(("stg_b",), *wx_tts)
        t_wdtT = t_wxT
        for c in range(NDT):
            tl = pr.emit(Q0, lambda s, c=c: s.dma_start(
                out=stg_b[0:96, :], in_=wlin_d[c * P:c * P + 96, :]),
                waits=guard(("stg_b",)))
            tl2 = pr.emit(Q0, lambda s, c=c: s.dma_start(
                out=zr[0:32, :], in_=wlin_d[c * P + 96:(c + 1) * P, :]),
                waits=[t_zrm] + guard(("zrw",)))
            es = []
            for b in range(NDT):
                tt = pr.emit(T, lambda t, b=b: t.transpose(
                    pt0[:, 0:96], in_=stg_b[0:96, b * P:(b + 1) * P],
                    identity=ident_f[0:96, 0:96]), waits=[tl, t_id] + guard(("pt0",)))
                tt2 = pr.emit(T, lambda t, b=b: t.transpose(
                    pt1[:, 0:32], in_=zr[0:32, b * P:(b + 1) * P],
                    identity=ident_f[0:32, 0:32]), waits=[tl2, t_id] + guard(("pt1",)))
                e1 = pr.emit(S, lambda s, c=c, b=b: s.copy(
                    out=wlinT[:, b, c * P:c * P + 96], in_=pt0[:, 0:96]), waits=[tt])
                e2 = pr.emit(S, lambda s, c=c, b=b: s.copy(
                    out=wlinT[:, b, c * P + 96:(c + 1) * P], in_=pt1[:, 0:32]),
                    waits=[tt2])
                setg(("pt0",), e1)
                setg(("pt1",), e2)
                es += [e1, e2]
            setg(("stg_b",), *es)
            setg(("zrw",), *es)
        t_gw = pr.emit(Q0, lambda s: s.dma_start(out=stg_b[0:8, :], in_=gw_d),
                       waits=guard(("stg_b",)))
        for c in range(NDT):
            tt = pr.emit(T, lambda t, c=c: t.transpose(
                pt0[:, 0:8], in_=stg_b[0:8, c * P:(c + 1) * P],
                identity=ident_f[0:8, 0:8]), waits=[t_gw, t_id] + guard(("pt0",)))
            te = pr.emit(S, lambda s, c=c: s.copy(out=glT[:, c, 0:8],
                                                  in_=pt0[:, 0:8]), waits=[tt])
            setg(("pt0",), te)
        t_lfs = []
        for e_ in range(E):
            t_lfs.append(pr.emit(Q0, lambda s, e_=e_: s.dma_start(
                out=stg_c.rearrange("p (c e r) -> p c e r", c=NDT, e=E)[:, :, e_, :],
                in_=left_d[e_].rearrange("(c p) r -> p c r", p=P))))
        t_lf = t_lfs[-1]
        t_glT = pr.emit(V, lambda v: v.tensor_copy(
            out=glT[:, :, 8:40],
            in_=stg_c.rearrange("p (c k) -> p c k", c=NDT)), waits=t_lfs)
        t_rg = pr.emit(Q0, lambda s: s.dma_start(out=stg_b[0:32, :], in_=right_d),
                       waits=[t_gw, pr.tok(T)])
        t_bl = pr.emit(Q0, lambda s: s.dma_start(out=stg_row, in_=blin_d),
                       waits=[pr.tok(S)])
        pr.emit(S, lambda s: s.copy(out=right_aug[0:32, :], in_=stg_b[0:32, :]),
                waits=[t_rg])
        t_aug3 = pr.emit(S, lambda s: s.copy(out=aug32[0:1, :], in_=stg_row),
                         waits=[t_bl])
        pr.emit(S, lambda s: s.copy(out=right_aug[32:64, :], in_=aug32[:, 0:D]),
                waits=[t_aug3])
        t_raug = pr.tok(S)
        t_aug4 = pr.emit(S, lambda s: s.copy(out=aug32[0:1, :], in_=onesrow[0:1, 0:D]),
                         waits=[t_zr])
        for i in range(2):
            pr.emit(S, lambda s, i=i: s.copy(out=loraT[i][32:64, :],
                                             in_=aug32[:, 0:P]), waits=[t_aug4])
        t_lorainit = pr.tok(S)

        t_initdone = [pr.tok("S"), pr.tok("T"), pr.tok("V"), pr.tok("q0"), pr.tok("q2"), pr.tok("G")]
        for k in range(4):
            setg(("ybuf", k), *t_initdone)
        setg(("bc", 0), *t_initdone)
        setg(("bc", 1), *t_initdone)
        prev_carry_S = None
        post_state = [None]

        def emit_post_A(hp, lt):
            sl = hp * SEG + lt * P
            yb = ylt[0]
            ub2 = u2buf[0]
            tyl = pr.emit(Q0, lambda s, sl=sl, yb=yb: s.dma_start(
                out=yb[:], in_=ylp_d[sl:sl + P, :]),
                waits=[ltok[("ydma", hp, dt)] for dt in range(NDT)]
                      + guard(("ylt", 0)))
            tu2 = pr.emit(Q0, lambda s, sl=sl, ub2=ub2: s.dma_start(
                out=ub2[:], in_=u_d[sl:sl + P, :]), waits=guard(("u2", 0)))
            tsq2 = pr.emit(S, lambda s, yb=yb: s.activation(
                out=lntmp, in_=yb[:], func=ACT.Square,
                accum_out=statbuf[:, 1:2]),
                waits=[tyl] + guard(("lntmp",)) + guard(("lntmpR",)) + guard(("yst",)) + guard(("stat",)))
            tid2 = pr.emit(S, lambda s, yb=yb: s.activation(
                out=lntmp, in_=yb[:], func=ACT.Identity,
                accum_out=statbuf[:, 0:1]))
            ltok[("red2", 1)] = tid2
            setg(("lntmpR",), tid2)
            pr.emit(S, lambda s: s.activation(
                out=statbuf[:, 2:3], in_=statbuf[:, 0:1], func=ACT.Square,
                scale=invDc[:, 0:1]))
            pr.emit(S, lambda s: s.activation(
                out=statbuf[:, 3:4], in_=statbuf[:, 2:3], func=ACT.Identity,
                scale=-1.0))
            pr.emit(S, lambda s: s.activation(
                out=mvbuf[:, 0:1], in_=statbuf[:, 0:1], func=ACT.Identity,
                scale=invDc[:, 0:1]))
            tsf2 = pr.emit(S, lambda s: s.copy(out=sdmy[:, 2:4], in_=statbuf[:, 2:4]))
            pr.selfwait(S, tsf2)
            tvar2 = pr.emit(S, lambda s: s.activation(
                out=mvbuf[:, 1:2], in_=statbuf[:, 1:2], func=ACT.Identity,
                scale=invDc[:, 0:1], bias=statbuf[:, 3:4]))
            pr.emit(S, lambda s: s.activation(out=rcol[:, 0:1], in_=mvbuf[:, 1:2],
                                              func=ACT.Ln, bias=epscol[:, 0:1]))
            pr.emit(S, lambda s: s.activation(out=rcol[:, 0:1], in_=rcol[:, 0:1],
                                              func=ACT.Exp, scale=neghalf[:, 0:1]))
            tng2 = pr.emit(S, lambda s: s.activation(
                out=rcol[:, 1:2], in_=mvbuf[:, 0:1], func=ACT.Identity, scale=-1.0))
            setg(("mv",), tng2)
            tfl2 = pr.emit(S, lambda s: s.copy(out=sdmy[:, 0:2], in_=rcol[:, 0:2]))
            pr.selfwait(S, tfl2)
            ta2 = pr.emit(S, lambda s, yb=yb: s.activation(
                out=lntmp, in_=yb[:], func=ACT.Identity, bias=rcol[:, 1:2]),
                waits=[ltok[("red2", 1)]] + guard(("lntmp",)) + guard(("lntmpR",)) + guard(("yst",)))
            t6 = pr.emit(S, lambda s: s.activation(
                out=yln[:], in_=lntmp, func=ACT.Identity, scale=rcol[:, 0:1]),
                waits=guard(("yln",)))
            setg(("lntmp",), t6)
            setg(("ylt", 0), t6)
            t7 = pr.emit(V, lambda v, ub2=ub2: v.tensor_tensor(
                out=x2buf[:], in0=ub2[:], in1=yln[:], op=ALU.add),
                waits=[tu2, t6] + guard(("x2",)))
            setg(("u2", 0), t7)
            setg(("yln",), t7)
            tsq3 = pr.emit(S, lambda s: s.activation(
                out=lntmp, in_=x2buf[:], func=ACT.Square,
                accum_out=statbuf[:, 1:2]),
                waits=[t7] + guard(("lntmp",)) + guard(("lntmpR",)) + guard(("yst",)))
            tid3 = pr.emit(S, lambda s: s.activation(
                out=lntmp, in_=x2buf[:], func=ACT.Identity,
                accum_out=statbuf[:, 0:1]))
            tr2c = tid3
            setg(("lntmpR",), tid3)
            pr.emit(S, lambda s: s.activation(
                out=statbuf[:, 2:3], in_=statbuf[:, 0:1], func=ACT.Square,
                scale=invDc[:, 0:1]))
            pr.emit(S, lambda s: s.activation(
                out=statbuf[:, 3:4], in_=statbuf[:, 2:3], func=ACT.Identity,
                scale=-1.0))
            pr.emit(S, lambda s: s.activation(
                out=mvbuf[:, 0:1], in_=statbuf[:, 0:1], func=ACT.Identity,
                scale=invDc[:, 0:1]))
            tsf3 = pr.emit(S, lambda s: s.copy(out=sdmy[:, 2:4], in_=statbuf[:, 2:4]))
            pr.selfwait(S, tsf3)
            tvar3 = pr.emit(S, lambda s: s.activation(
                out=mvbuf[:, 1:2], in_=statbuf[:, 1:2], func=ACT.Identity,
                scale=invDc[:, 0:1], bias=statbuf[:, 3:4]))
            setg(("stat",), tvar3)
            pr.emit(S, lambda s: s.activation(out=rcol[:, 0:1], in_=mvbuf[:, 1:2],
                                              func=ACT.Ln, bias=epscol[:, 0:1]))
            pr.emit(S, lambda s: s.activation(out=rcol[:, 0:1], in_=rcol[:, 0:1],
                                              func=ACT.Exp, scale=neghalf[:, 0:1]))
            tng3 = pr.emit(S, lambda s: s.activation(
                out=rcol[:, 1:2], in_=mvbuf[:, 0:1], func=ACT.Identity, scale=-1.0))
            setg(("mv",), tng3)
            tfl3 = pr.emit(S, lambda s: s.copy(out=sdmy[:, 0:2], in_=rcol[:, 0:2]))
            pr.selfwait(S, tfl3)
            ta3 = pr.emit(S, lambda s: s.activation(
                out=lntmp, in_=x2buf[:], func=ACT.Identity, bias=rcol[:, 1:2]),
                waits=[tr2c] + guard(("lntmp",)) + guard(("lntmpR",)) + guard(("yst",)))
            td_ = pr.emit(S, lambda s: s.activation(
                out=xn3[:], in_=lntmp, func=ACT.Identity, scale=rcol[:, 0:1]),
                waits=guard(("xn3",)))
            setg(("lntmp",), td_)
            tes = []
            tts = []
            for c in range(NDT):
                pb, pk = ((pt0, "pt0"), (pt1, "pt1"))[c % 2]
                tt = pr.emit(T, lambda t, c=c, pb=pb: t.transpose(
                    pb[:].bitcast(BF16)[:, 0:P], in_=xn3[:, c * P:(c + 1) * P],
                    identity=ident_b[:]), waits=[td_, t_idb] + guard((pk,)))
                te = pr.emit(S, lambda s, c=c, pb=pb: s.copy(
                    out=xn3T[:, c * P:(c + 1) * P], in_=pb[:].bitcast(BF16)[:, 0:P]),
                    waits=[tt] + guard(("xn3T",)))
                setg((pk,), te)
                tes.append(te)
                tts.append(tt)
            setg(("xn3",), *tts)
            for c in range(NDT):
                w0 = [tes[c], t_wxT]
                if c == 0:
                    w0 += guard(("pf",)) + [t_glT]
                stp = False
                tm1 = pr.emit(T, lambda t, c=c, stp=stp: t.matmul(
                    pf0[:], lhsT=xn3T[:, c * P:(c + 1) * P], rhs=wlinT[:, c, 0:512],
                    start=(c == 0), stop=stp), waits=w0)
                pr.emit(T, lambda t, c=c, stp=stp: t.matmul(
                    pf1[:], lhsT=xn3T[:, c * P:(c + 1) * P],
                    rhs=wlinT[:, c, 512:1024],
                    start=(c == 0), stop=stp))
                if True:
                    w2 = guard(("psm",)) if c == 0 else []
                    pr.emit(T, lambda t, c=c: t.matmul(
                        psm[:, 0:40], lhsT=xn3T[:, c * P:(c + 1) * P], rhs=glT[:, c, :],
                        start=(c == 0), stop=(c == NDT - 1)), waits=w2)
            setg(("xn3T",), pr.tok(T))
            t_mm = pr.tok(T)
            return t7, t_mm

        def emit_post_B(hp, lt, t7, t_mm):
            sl = hp * SEG + lt * P
            tsm1 = pr.emit(V, lambda v: v.reduce_max(
                out=sm_sc[:, 0:1], in_=psm[:, 0:8], axis=AXL.X, negate=True),
                waits=[t_mm])
            for _pi in range(3):
                pr.emit(V, lambda v, _pi=_pi: v.memset(vdmy[:, _pi:_pi+1], 0.0))
            tsm1f = pr.emit(V, lambda v: v.tensor_copy(out=vdmy[:, 0:1],
                                                       in_=sm_sc[:, 0:1]))
            tsm2 = pr.emit(S, lambda s: s.activation(
                out=e8[:], in_=psm[:, 0:8], func=ACT.Exp, bias=sm_sc[:, 0:1]),
                waits=[tsm1f] + guard(("e8",)))
            tsm3 = pr.emit(V, lambda v: v.reduce_sum(
                out=sm_sc[:, 1:2], in_=e8[:], axis=AXL.X), waits=[tsm2])
            tsm5 = pr.emit(V, lambda v: v.max(out=top8[:], in_=e8[:]))
            for _pi in range(3):
                pr.emit(V, lambda v, _pi=_pi: v.memset(vdmy[:, _pi:_pi+1], 0.0))
            pr.emit(V, lambda v: v.reciprocal(out=sm_sc[:, 2:3], in_=sm_sc[:, 1:2]))
            pr.emit(V, lambda v: v.tensor_copy(out=vdmy[:, 0:2], in_=sm_sc[:, 1:3]))
            pr.emit(V, lambda v: v.tensor_copy(out=vdmy[:, 2:3], in_=top8[:, 1:2]))
            pr.selfwait(V)
            tsm6 = pr.emit(V, lambda v: v.scalar_tensor_tensor(
                out=g8[:], in0=e8[:], scalar=top8[:, 1:2], in1=e8[:],
                op0=ALU.is_ge, op1=ALU.mult))
            tz_t = pr.emit(V, lambda v: v.scalar_tensor_tensor(
                out=tzb[:], in0=psm[:, 8:40], scalar=sm_sc[:, 2:3],
                in1=g8[:].rearrange("p (e o) -> p e o", o=1).to_broadcast([P, 8, 4]),
                op0=ALU.mult, op1=ALU.mult), waits=guard(("tzb",)))
            setg(("e8",), tz_t)
            setg(("psm",), tz_t, tsm2)
            lb = loraT[lt % 2]
            tt = pr.emit(T, lambda t: t.transpose(
                pt0[:].bitcast(F32R)[0:32, 0:P], in_=tzb[:], identity=ident_r[:]),
                waits=[tz_t, t_idr] + guard(("pt0",)))
            te = pr.emit(S, lambda s, lb=lb: s.copy(
                out=lb[0:32, :], in_=pt0[:].bitcast(F32R)[0:32, 0:P]),
                waits=[tt, t_lorainit] + guard(("lora", lt % 2)))
            setg(("pt0",), te)
            setg(("tzb",), tt)
            tl1 = pr.emit(T, lambda t, lb=lb: t.matmul(
                pf0[:], lhsT=lb[:], rhs=right_aug[:, 0:512], start=False, stop=True),
                waits=[te, t_raug])
            tl2 = pr.emit(T, lambda t, lb=lb: t.matmul(
                pf1[:], lhsT=lb[:], rhs=right_aug[:, 512:1024], start=False,
                stop=True))
            setg(("lora", lt % 2), tl2)
            t_ffn = pr.tok(T)
            tsq4a = pr.emit(S, lambda s: s.activation(
                out=lntmp[:, 0:512], in_=pf0[:], func=ACT.Square,
                accum_out=st4[:, 0:1]),
                waits=[t_ffn] + guard(("lntmp",)) + guard(("lntmpR",)) + guard(("yst",)) + [t7])
            tsq4b = pr.emit(S, lambda s: s.activation(
                out=lntmp[:, 512:1024], in_=pf1[:], func=ACT.Square,
                accum_out=st4[:, 1:2]))
            tid4a = pr.emit(S, lambda s: s.activation(
                out=lntmp[:, 0:512], in_=pf0[:], func=ACT.Identity,
                accum_out=st4[:, 7:8]))
            tid4b = pr.emit(S, lambda s: s.activation(
                out=lntmp[:, 512:1024], in_=pf1[:], func=ACT.Identity,
                accum_out=st4[:, 8:9]))
            tr2d = tid4b
            setg(("lntmpR",), tid4b)
            tsfa = pr.emit(S, lambda s: s.copy(out=sdmy[:, 0:1], in_=st4[:, 8:9]))
            pr.selfwait(S, tsfa)
            pr.emit(S, lambda s: s.activation(
                out=st4[:, 2:3], in_=st4[:, 0:1], func=ACT.Identity,
                bias=st4[:, 1:2]))
            pr.emit(S, lambda s: s.activation(
                out=st4[:, 3:4], in_=st4[:, 7:8], func=ACT.Identity,
                bias=st4[:, 8:9]))
            pr.emit(S, lambda s: s.activation(
                out=st4[:, 4:5], in_=st4[:, 3:4], func=ACT.Square,
                scale=invDc[:, 0:1]))
            pr.emit(S, lambda s: s.activation(
                out=st4[:, 6:7], in_=st4[:, 4:5], func=ACT.Identity,
                scale=-1.0))
            pr.emit(S, lambda s: s.activation(
                out=mv4[:, 0:1], in_=st4[:, 3:4], func=ACT.Identity,
                scale=invDc[:, 0:1]))
            tsf4 = pr.emit(S, lambda s: s.copy(out=sdmy[:, 2:3], in_=st4[:, 6:7]))
            pr.selfwait(S, tsf4)
            tvar4 = pr.emit(S, lambda s: s.activation(
                out=mv4[:, 1:2], in_=st4[:, 2:3], func=ACT.Identity,
                scale=invDc[:, 0:1], bias=st4[:, 6:7]))
            pr.emit(S, lambda s: s.activation(out=rcol[:, 0:1], in_=mv4[:, 1:2],
                                              func=ACT.Ln, bias=epscol[:, 0:1]))
            pr.emit(S, lambda s: s.activation(out=rcol[:, 0:1], in_=rcol[:, 0:1],
                                              func=ACT.Exp, scale=neghalf[:, 0:1]))
            tng4 = pr.emit(S, lambda s: s.activation(
                out=rcol[:, 1:2], in_=mv4[:, 0:1], func=ACT.Identity, scale=-1.0))
            setg(("mv4",), tng4)
            tfl4 = pr.emit(S, lambda s: s.copy(out=sdmy[:, 0:2], in_=rcol[:, 0:2]))
            pr.selfwait(S, tfl4)
            ta4a = pr.emit(S, lambda s: s.activation(
                out=lntmp[:, 0:512], in_=pf0[:], func=ACT.Identity, bias=rcol[:, 1:2]),
                waits=[tr2d] + guard(("lntmp",)) + guard(("lntmpR",)) + guard(("yst",)))
            ta4b = pr.emit(S, lambda s: s.activation(
                out=lntmp[:, 512:1024], in_=pf1[:], func=ACT.Identity, bias=rcol[:, 1:2]))
            to1 = pr.emit(S, lambda s: s.activation(
                out=yln[:, 0:512], in_=lntmp[:, 0:512], func=ACT.Identity,
                scale=rcol[:, 0:1]), waits=guard(("obuf",)))
            to2 = pr.emit(S, lambda s: s.activation(
                out=yln[:, 512:1024], in_=lntmp[:, 512:1024], func=ACT.Identity,
                scale=rcol[:, 0:1]))
            setg(("lntmp",), to2)
            setg(("pf",), ta4b)
            tfin = pr.emit(V, lambda v: v.tensor_tensor(
                out=yln[:], in0=yln[:], in1=x2buf[:], op=ALU.add), waits=[to2])
            setg(("x2",), tfin)
            tod = pr.emit(Q0, lambda s, sl=sl: s.dma_start(
                out=out_d[sl:sl + P, :], in_=yln[:]), waits=[tfin])
            setg(("yln",), tod)
            setg(("obuf",), tod)

        for h in range(NH):
            lbase = h * SEG
            for sub in range(NSUB):
                for i in range(4):
                    lt = sub * 4 + i
                    sl = lbase + lt * P
                    ub = ubuf[lt % 2]
                    xb = xnbuf[lt % 2]
                    tu = pr.emit(Q0, lambda s, sl=sl, ub=ub: s.dma_start(
                        out=ub[:], in_=u_d[sl:sl + P, :]), waits=guard(("ub", lt % 2)))
                    tsq = pr.emit(S, lambda s, ub=ub: s.activation(
                        out=lntmp, in_=ub[:], func=ACT.Square,
                        accum_out=statbuf[:, 1:2]),
                        waits=[tu] + guard(("lntmp",)) + guard(("lntmpR",)) + guard(("yst",)) + guard(("stat",)))
                    tid = pr.emit(S, lambda s, ub=ub: s.activation(
                        out=lntmp, in_=ub[:], func=ACT.Identity,
                        accum_out=statbuf[:, 0:1]))
                    ltok[("red2", 0)] = tid
                    setg(("lntmpR",), tid)
                    pr.emit(S, lambda s: s.activation(
                        out=statbuf[:, 2:3], in_=statbuf[:, 0:1], func=ACT.Square,
                        scale=invDc[:, 0:1]))
                    pr.emit(S, lambda s: s.activation(
                        out=statbuf[:, 3:4], in_=statbuf[:, 2:3], func=ACT.Identity,
                        scale=-1.0))
                    pr.emit(S, lambda s: s.activation(
                        out=mvbuf[:, 0:1], in_=statbuf[:, 0:1], func=ACT.Identity,
                        scale=invDc[:, 0:1]))
                    tsf0 = pr.emit(S, lambda s: s.copy(out=sdmy[:, 2:4], in_=statbuf[:, 2:4]))
                    pr.selfwait(S, tsf0)
                    tvar = pr.emit(S, lambda s: s.activation(
                        out=mvbuf[:, 1:2], in_=statbuf[:, 1:2], func=ACT.Identity,
                        scale=invDc[:, 0:1], bias=statbuf[:, 3:4]))
                    tln1 = pr.emit(S, lambda s: s.activation(
                        out=rcol[:, 0:1], in_=mvbuf[:, 1:2], func=ACT.Ln, bias=epscol[:, 0:1]))
                    pr.emit(S, lambda s: s.activation(
                        out=rcol[:, 0:1], in_=rcol[:, 0:1], func=ACT.Exp, scale=neghalf[:, 0:1]))
                    tng = pr.emit(S, lambda s: s.activation(
                        out=rcol[:, 1:2], in_=mvbuf[:, 0:1], func=ACT.Identity, scale=-1.0))
                    setg(("mv",), tng)
                    tfl = pr.emit(S, lambda s: s.copy(out=sdmy[:, 0:2], in_=rcol[:, 0:2]))
                    pr.selfwait(S, tfl)
                    if dbg and h == 0:
                        pr.emit(S, lambda s, lt=lt: s.copy(
                            out=statbuf[:, 4:6], in_=rcol[:, 0:2]))
                        pr.emit(S, lambda s, lt=lt: s.copy(
                            out=statbuf[:, 6:8], in_=mvbuf[:, 0:2]))
                        pr.emit(S, lambda s, lt=lt: s.copy(
                            out=statbuf[:, 8:10], in_=statbuf[:, 1:3]))
                        pr.emit(Q0, lambda s, lt=lt: s.dma_start(
                            out=dbgscal_d[:, lt * 6:lt * 6 + 6], in_=statbuf[:, 4:10]),
                            waits=[pr.tok("S")])
                    if skip_ln:
                        txn = pr.emit(S, lambda s, ub=ub, xb=xb: s.copy(
                            out=xb[:], in_=ub[:]),
                            waits=[ltok[("red2", 0)]] + guard(("xb", lt % 2)))
                    else:
                        ta1 = pr.emit(S, lambda s, ub=ub: s.activation(
                            out=lntmp, in_=ub[:], func=ACT.Identity, bias=rcol[:, 1:2]),
                            waits=[ltok[("red2", 0)]] + guard(("lntmp",)) + guard(("lntmpR",)) + guard(("yst",)))
                        txn = pr.emit(S, lambda s, xb=xb: s.activation(
                            out=xb[:], in_=lntmp, func=ACT.Identity, scale=rcol[:, 0:1]),
                            waits=guard(("xb", lt % 2)))
                        setg(("lntmp",), txn)
                    setg(("ub", lt % 2), txn)
                    setg(("stat",), tvar)
                    tts = []
                    for c in range(NDT):
                        pb, pk = ((pt0, "pt0"), (pt1, "pt1"))[c % 2]
                        tt = pr.emit(T, lambda t, xb=xb, c=c, pb=pb: t.transpose(
                            pb[:].bitcast(F32R)[:, 0:P], in_=xb[:, c * P:(c + 1) * P],
                            identity=ident_r[:]), waits=[txn, t_idr] + guard((pk,)))
                        te = pr.emit(S, lambda s, c=c, i=i, pb=pb: s.copy(
                            out=mini[:, c, i * P:(i + 1) * P],
                            in_=pb[:].bitcast(F32R)[:, 0:P]),
                            waits=[tt] + guard(("mini", c)))
                        setg((pk,), te)
                        ltok[("minie", c, i)] = te
                        tts.append(tt)
                    setg(("xb", lt % 2), *tts)
                for c in range(NDT):
                    wts = [ltok[("minie", c, i)] for i in range(4)] + [t_wxT]
                    if c == 0:
                        wts += guard(("px",))
                    tm = pr.emit(T, lambda t, c=c: t.matmul(
                        px[:], lhsT=wx_lhsT[:, c, :], rhs=mini[:, c, :],
                        start=(c == 0), stop=(c == NDT - 1)), waits=wts)
                    tsp = pr.emit(Q2, lambda s, c=c, sub=sub, lbase=lbase: s.dma_start(
                        out=xnt_d[c, :, lbase + sub * 512:lbase + (sub + 1) * 512],
                        in_=mini[:, c, :].bitcast(F32)),
                        waits=[ltok[("minie", c, i)] for i in range(4)])
                    setg(("mini", c), tsp, tm)
                t_px = pr.tok(T)
                te1 = pr.emit(S, lambda s, sub=sub: s.copy(
                    out=dt_rawT_aug[0:64, sub * 512:(sub + 1) * 512],
                    in_=px[0:64, :]), waits=[t_px] + guard(("draw", sub)))
                te2 = pr.emit(S, lambda s: s.copy(out=bcstage[:], in_=px[64:96, :]),
                              waits=[t_px] + guard(("bcstage",)))
                pr.emit(Q2, lambda s, sub=sub, lbase=lbase: s.dma_start(
                    out=bc_d[0, :, lbase + sub * 512:lbase + (sub + 1) * 512],
                    in_=bcstage[0:16, :]), waits=[te2])
                tbc = pr.emit(Q2, lambda s, sub=sub, lbase=lbase: s.dma_start(
                    out=bc_d[1, :, lbase + sub * 512:lbase + (sub + 1) * 512],
                    in_=bcstage[16:32, :]))
                setg(("bcstage",), tbc)
                setg(("px",), te1, te2)
                ltok[("draw", sub)] = te1
                ltok[("bcd", sub)] = tbc

            def prepassA(dg):
                outa = []
                for k, dt in enumerate(dg):
                    xs = xstrip[k]
                    tx = pr.emit(Q0, lambda s, dt=dt, xs=xs, lbase=lbase: s.dma_start(
                        out=xs[:], in_=xnt_d[dt, :, lbase:lbase + SEG]),
                        waits=[ltok[("bcd", sub)] for sub in range(NSUB)]
                              + guard(("xs", k)))
                    for sub in range(NSUB):
                        bix = (k * NSUB + sub) % 2
                        pzb = (pdtz, pdtz2)[bix]
                        tm = pr.emit(T, lambda t, dt=dt, sub=sub, pzb=pzb: t.matmul(
                            pzb[:], lhsT=wdtT_aug[:, dt * P:(dt + 1) * P],
                            rhs=dt_rawT_aug[:, sub * 512:(sub + 1) * 512],
                            start=True, stop=True),
                            waits=[t_wdtT, ltok[("draw", sub)]] + guard(("pdtz", bix)))
                        ts1 = pr.emit(S, lambda s, k=k, sub=sub, pzb=pzb: s.activation(
                            out=dtT[:, k, sub * 512:(sub + 1) * 512], in_=pzb[:],
                            func=ACT.Exp), waits=[tm] + guard(("dtTr", k)))
                        pr.emit(S, lambda s, k=k, sub=sub: s.activation(
                            out=dtT[:, k, sub * 512:(sub + 1) * 512],
                            in_=dtT[:, k, sub * 512:(sub + 1) * 512],
                            func=ACT.Ln, bias=1.0))
                        setg(("pdtz", bix), ts1)
                        ltok[("drawuse", sub)] = tm
                    t_dtT = pr.tok(S)
                    tw = pr.emit(V, lambda v, k=k, xs=xs: v.tensor_tensor(
                        out=wT[:, k, :], in0=dtT[:, k, :], in1=xs[:], op=ALU.mult),
                        waits=[tx, t_dtT])
                    setg(("dtTr", k), tw)
                    ltok[("w", k)] = tw
                    outa.append((k, dt, xs, tx))
                setg(("draw", 0), *[ltok[("drawuse", s_)] for s_ in range(NSUB)])
                if NSUB > 1:
                    setg(("draw", 1), *[ltok[("drawuse", s_)] for s_ in range(NSUB)])
                return outa

            def prepassB(outa):
                for k, dt, xs, tx in outa:
                    ty0 = pr.emit(V, lambda v, k=k, dt=dt, xs=xs: v.tensor_scalar(
                        out=ybuf[:, k, :], in0=xs[:], scalar1=dcol[:, dt:dt + 1],
                        scalar2=None, op0=ALU.mult),
                        waits=[t_dpar, tx] + guard(("ybuf", k)))
                    setg(("xs", k), ty0)
                    ltok[("y0", k)] = ty0

            def nloop(gi, dg):
                nonlocal prev_carry_S
                prev_carry_local = None
                for n in range(N):
                    bb = bcbuf[n % 2]
                    tb1 = pr.emit(Q2, lambda s, n=n, bb=bb, lbase=lbase: s.dma_start(
                        out=bb[:], in_=bc_d[:, n:n + 1, lbase:lbase + SEG]
                        .rearrange("bc o c -> o bc c").to_broadcast([P, 2, SEG])),
                        waits=[ltok[("bcd", sub)] for sub in range(NSUB)]
                              + guard(("bc", n % 2)))
                    tb2 = tb1
                    ta = []
                    for k, dt in enumerate(dg):
                        ta.append(pr.emit(S, lambda s, k=k, dt=dt, n=n: s.activation(
                            out=abuf[:, k, :], in_=dtT[:, k, :], func=ACT.Exp,
                            scale=acol[:, dt * N + n:dt * N + n + 1]),
                            waits=[t_acol] + guard(("abuf", k))))
                    tbv = pr.emit(V, lambda v, bb=bb: v.tensor_tensor(
                        out=bybuf[:], in0=wT[:],
                        in1=bb[:, 0:1, :].to_broadcast([P, 4, SEG]), op=ALU.mult),
                        waits=[tb1, ltok[("w", 0)], ltok[("w", 3)]])
                    if n == 0 and h == 0 and gi == 0:
                        pr.selfwait(V, t_carry0)
                    if prev_carry_S is not None:
                        pr.selfwait(V, prev_carry_S)
                    tscan = []
                    for k, dt in enumerate(dg):
                        cix = n * NDT + dt
                        tscan.append(pr.emit(V, lambda v, k=k, cix=cix: v.tensor_tensor_scan(
                            out=abuf[:, k, :], data0=abuf[:, k, :], data1=bybuf[:, k, :],
                            initial=carry[:, cix:cix + 1], op0=ALU.mult, op1=ALU.add),
                            waits=[ta[-1]]))
                    t_scans = pr.tok(V)
                    c0 = n * NDT + dg[0]
                    tcs = pr.emit(S, lambda s, c0=c0: s.copy(
                        out=carry[:, c0:c0 + 4],
                        in_=abuf[:, :, SEG - 1:SEG].rearrange("p k o -> p (k o)")),
                        waits=tscan)
                    prev_carry_local = tcs
                    tyk = pr.emit(V, lambda v, bb=bb: v.tensor_tensor(
                        out=bybuf[:], in0=abuf[:],
                        in1=bb[:, 1:2, :].to_broadcast([P, 4, SEG]), op=ALU.mult),
                        waits=[tb2])
                    setg(("bc", n % 2), tyk)
                    for k in range(4):
                        setg(("abuf", k), tyk, tcs)
                    tya = pr.emit(V, lambda v: v.tensor_tensor(
                        out=ybuf[:], in0=ybuf[:], in1=bybuf[:], op=ALU.add),
                        waits=[ltok[("y0", 0)], ltok[("y0", 3)]])
                    j = gi * N + n
                    if h > 0 and j % 4 == 1:
                        post_state[0] = emit_post_A(h - 1, j // 4)
                    elif h > 0 and j % 4 == 3:
                        emit_post_B(h - 1, j // 4, *post_state[0])
                prev_carry_S = prev_carry_local
                for k in range(4):
                    setg(("dtTr", k), pr.tok(S), pr.tok(V))

            def yout(dg):
                t_y_done = pr.tok(V)
                for k, dt in enumerate(dg):
                    yes = []
                    tts_k = []
                    for lt in range(NLT):
                        pb, pk = ((pt0, "pt0"), (pt1, "pt1"))[lt % 2]
                        tt = pr.emit(T, lambda t, k=k, lt=lt, pb=pb: t.transpose(
                            pb[:, 0:P], in_=ybuf[:, k, lt * P:(lt + 1) * P],
                            identity=ident_f[:]), waits=[t_y_done, t_id] + guard((pk,)))
                        te = pr.emit(S, lambda s, lt=lt, pb=pb: s.copy(
                            out=ystage[:, lt * P:(lt + 1) * P], in_=pb[:, 0:P]),
                            waits=[tt] + guard(("yst",)) + guard(("lntmp",)))
                        setg((pk,), te)
                        yes.append(te)
                        tts_k.append(tt)
                    ty = pr.emit(Q2, lambda s, dt=dt, lbase=lbase: s.dma_start(
                        out=ylp_d[lbase:lbase + SEG, dt * P:(dt + 1) * P]
                        .rearrange("(lt p) c -> p lt c", p=P),
                        in_=ystage[:].rearrange("p (lt c) -> p lt c", c=P)),
                        waits=yes)
                    setg(("yst",), ty)
                    ltok[("ydma", h, dt)] = ty
                    setg(("ybuf", k), *tts_k)

            if not skip_scan:
                pa0 = prepassA(DGRPS[0])
                prepassB(pa0)
                nloop(0, DGRPS[0])
                pa1 = prepassA(DGRPS[1])
                yout(DGRPS[0])
                prepassB(pa1)
                nloop(1, DGRPS[1])
                yout(DGRPS[1])


        for lt in range(NLT):
            _pa = emit_post_A(NH - 1, lt)
            emit_post_B(NH - 1, lt, *_pa)

        pr.selfwait(Q0)
        pr.selfwait(Q2)

        with nc.Block() as blk:
            def run_stream(name):
                def body(eng):
                    for wl, fn, inc, semn in pr.streams[name]:
                        for (pe, v) in wl:
                            eng.wait_ge(sems[pe], v)
                        if fn is not None:
                            inst = fn(eng)
                            if inc:
                                inst.then_inc(sems[semn], inc)
                return body
            blk.vector(run_stream("V"))
            blk.scalar(run_stream("S"))
            blk.tensor(run_stream("T"))
            blk.sync(run_stream("q0"))
            blk.gpsimd(run_stream("q2"))

    return nc


def prep_inputs(inp, L):
    right = np.ascontiguousarray(np.asarray(inp["L0_ffn_right"], np.float32)
                                 .reshape(E * R, D))
    base = {
        "A_log": np.ascontiguousarray(inp["L0_attn_A_log"], dtype=np.float32),
        "W_x": np.ascontiguousarray(inp["L0_attn_W_x"], dtype=np.float32),
        "W_dt": np.ascontiguousarray(inp["L0_attn_W_dt"], dtype=np.float32),
        "b_dt": np.ascontiguousarray(np.asarray(inp["L0_attn_b_dt"], np.float32)
                                     .reshape(1, D)),
        "D_param": np.ascontiguousarray(np.asarray(inp["L0_attn_D_param"], np.float32)
                                        .reshape(1, D)),
        "W_lin": np.ascontiguousarray(inp["L0_ffn_W_lin"], dtype=np.float32),
        "b_lin": np.ascontiguousarray(np.asarray(inp["L0_ffn_b_lin"], np.float32)
                                      .reshape(1, D)),
        "gate_w": np.ascontiguousarray(inp["L0_ffn_gate_w"], dtype=np.float32),
        "left": np.ascontiguousarray(inp["L0_ffn_left"], dtype=np.float32),
        "right": right,
    }
    u = np.asarray(inp["L0_attn_u"], np.float32)
    maps = []
    for b in range(u.shape[0]):
        m = dict(base)
        m["u"] = np.ascontiguousarray(u[b, :L, :])
        maps.append(m)
    return maps


def kernel(**inputs):
    from concourse.bass_utils import run_bass_kernel_spmd
    L = 4096
    nc = build_nc(L=L, SEG=1024)
    maps = prep_inputs(inputs, L)
    res = run_bass_kernel_spmd(nc, maps, core_ids=list(range(8)), trace=False)
    return np.stack([np.asarray(r["out"], np.float32) for r in res.results])


# version 04
# speedup vs baseline: 1.1301x; 1.0281x over previous
import numpy as np
from contextlib import ExitStack
import concourse.bass as bass
import concourse.mybir as mybir

F32 = mybir.dt.float32
F32R = mybir.dt.float32r
BF16 = mybir.dt.bfloat16
ALU = mybir.AluOpType
ACT = mybir.ActivationFunctionType
AXL = mybir.AxisListType

D = 1024
N = 16
RK = 64
NDT = 8
E = 8
R = 4
P = 128

ENGS = ("V", "S", "T", "q0", "q2")
SEMS = ("V", "S", "T", "q0", "q2", "G")
SEM2ENG = {"V": "V", "S": "S", "T": "T", "q0": "q0", "q2": "q2", "G": "q2"}


class Prog:
    def __init__(self):
        self.streams = {e: [] for e in ENGS}
        self.cnt = {s: 0 for s in SEMS}
        self.waited = {c: {p: 0 for p in SEMS} for c in ENGS}

    def emit(self, eng, fn, waits=(), inc=None, sem=None):
        sem = sem or eng
        if inc is None:
            inc = 16 if eng in ("q0", "q2") else 1
        if eng in ("q0", "q2") and sem == eng and self.cnt[eng] > 0:
            self.selfwait(eng)
        if eng == "T" and self.cnt["T"] >= 2:
            self.selfwait("T", ("T", self.cnt["T"] - 2))
        wl = []
        for tok in waits:
            if tok is None:
                continue
            pe, v = tok
            if SEM2ENG[pe] == eng:
                continue
            if v > self.waited[eng][pe]:
                self.waited[eng][pe] = v
                wl.append((pe, v))
        self.cnt[sem] += inc
        self.streams[eng].append((wl, fn, inc, sem))
        return (sem, self.cnt[sem])

    def selfwait(self, eng, tok=None):
        if tok is None:
            tok = (eng, self.cnt[eng])
        pe, v = tok
        if v > self.waited[eng][pe]:
            self.waited[eng][pe] = v
            self.streams[eng].append(([(pe, v)], None, 0, eng))

    def tok(self, sem):
        return (sem, self.cnt[sem])


def build_nc(L=4096, SEG=1024, name="mk"):
    dbg = False
    skip_post = False
    skip_scan = False
    skip_ln = False
    post_level = 4
    assert L % SEG == 0 and SEG % 512 == 0
    NH = L // SEG
    NSUB = SEG // 512
    NLT = SEG // 128
    DGRPS = ((0, 1, 2, 3), (4, 5, 6, 7))

    nc = bass.Bass("TRN2", target_bir_lowering=False, name=name,
                   detect_race_conditions=False)

    u_d = nc.dram_tensor("u", [L, D], F32, kind="ExternalInput").ap()
    alog_d = nc.dram_tensor("A_log", [D, N], F32, kind="ExternalInput").ap()
    wx_d = nc.dram_tensor("W_x", [96, D], F32, kind="ExternalInput").ap()
    wdt_d = nc.dram_tensor("W_dt", [D, RK], F32, kind="ExternalInput").ap()
    bdt_d = nc.dram_tensor("b_dt", [1, D], F32, kind="ExternalInput").ap()
    dpar_d = nc.dram_tensor("D_param", [1, D], F32, kind="ExternalInput").ap()
    wlin_d = nc.dram_tensor("W_lin", [D, D], F32, kind="ExternalInput").ap()
    blin_d = nc.dram_tensor("b_lin", [1, D], F32, kind="ExternalInput").ap()
    gw_d = nc.dram_tensor("gate_w", [E, D], F32, kind="ExternalInput").ap()
    left_d = nc.dram_tensor("left", [E, D, R], F32, kind="ExternalInput").ap()
    right_d = nc.dram_tensor("right", [E * R, D], F32, kind="ExternalInput").ap()
    out_d = nc.dram_tensor("out", [L, D], F32, kind="ExternalOutput").ap()
    knd = "ExternalOutput" if dbg else "Internal"
    dbgscal_d = nc.dram_tensor("dbgscal", [P, 64], F32, kind=knd).ap()
    dbgsq_d = nc.dram_tensor("dbgsq", [P, D], F32, kind=knd).ap()
    bc_d = nc.dram_tensor("bc_scr", [2, 16, L], F32, kind=knd).ap()
    xnt_d = nc.dram_tensor("xnt_scr", [NDT, P, L], F32, kind=knd).ap()
    ylp_d = nc.dram_tensor("ylp_scr", [L, D], F32, kind=knd).ap()

    pr = Prog()
    st = ExitStack()

    def sb(nm, shape, dt):
        return st.enter_context(nc.sbuf_tensor(nm, shape, dt))

    def psum(nm, shape):
        return st.enter_context(nc.psum_tensor(nm, shape, F32))

    with st:
        ident_r = sb("ident_r", [P, P], F32R)
        ident_b = sb("ident_b", [P, P], BF16)
        ident_f = sb("ident_f", [P, P], F32)
        wx_lhsT = sb("wx_lhsT", [P, NDT, 96], F32R)
        wdtT_aug = sb("wdtT_aug", [96, NDT * P], F32R)
        wlinT = sb("wlinT", [P, NDT, D], BF16)
        glT = sb("glT", [P, NDT, 40], BF16)
        right_aug = sb("right_aug", [64, D], F32R)
        acol = sb("acol", [P, NDT * N], F32)
        dcol = sb("dcol", [P, NDT], F32)
        dt_rawT_aug = sb("dt_rawT_aug", [96, SEG], F32R)
        carry = sb("carry", [P, N * NDT], F32)
        dtT = sb("dtT", [P, 4, SEG], F32)
        wT = sb("wT", [P, 4, SEG], F32)
        ybuf = sb("ybuf", [P, 4, SEG], F32)
        abuf = sb("abuf", [P, 4, SEG], F32)
        bybuf = sb("bybuf", [P, 4, SEG], F32)
        bcbuf = [sb(f"bcbuf{i}", [P, 2, SEG], F32) for i in range(2)]
        xstrip = [sb(f"xstrip{i}", [P, SEG], F32) for i in range(4)]
        ubuf = [sb(f"ubuf{i}", [P, D], F32) for i in range(2)]
        xnbuf = [sb(f"xnbuf{i}", [P, D], F32R) for i in range(2)]
        mini = sb("mini", [P, NDT, 512], F32R)
        bcstage = sb("bcstage", [32, 512], F32)
        statbuf = sb("statbuf", [P, 12], F32)
        mvbuf = sb("mvbuf", [P, 2], F32)
        rcol = sb("rcol", [P, 2], F32)
        ystage = sb("ystage", [P, SEG], F32)
        ylt = [sb("ylt0", [P, D], F32)]
        u2buf = [sb("u2buf0", [P, D], F32)]
        x2buf = sb("x2buf", [P, D], F32)
        yln = sb("yln", [P, D], F32)
        xn3 = sb("xn3", [P, D], BF16)
        xn3T = sb("xn3T", [P, NDT * P], BF16)
        e8 = sb("e8", [P, 8], F32)
        g8 = sb("g8", [P, 8], F32)
        top8 = sb("top8", [P, 8], F32)
        sm_sc = sb("sm_sc", [P, 4], F32)
        tzb = sb("tzb", [P, 32], F32R)
        loraT = [sb(f"loraT{i}", [64, P], F32R) for i in range(2)]
        st4 = sb("st4", [P, 12], F32)
        mv4 = sb("mv4", [P, 2], F32)
        epscol = sb("epscol", [P, 1], F32)
        neghalf = sb("neghalf", [P, 1], F32)
        invDc = sb("invDc", [P, 1], F32)
        vdmy = sb("vdmy", [P, 4], F32)
        sdmy = sb("sdmy", [P, 4], F32)
        if SEG >= 1024:
            stg_a = ybuf[:].rearrange("p a b -> p (a b)")[:, 0:NDT * RK]
            stg_c = ybuf[:].rearrange("p a b -> p (a b)")[:, NDT * RK:NDT * RK + NDT * 32]
            stg_b = ybuf[0:96, 1, :]
            zr = ybuf[0:64, 2, :]
            aug32 = ybuf[0:32, 3, :]
            onesrow = bcbuf[0][0:1, 0, 0:max(SEG, D)]
            stg_row = bcbuf[0][0:1, 1, 0:D]
        else:
            stg_a = sb("stg_a", [P, NDT * RK], F32)[:]
            stg_b = sb("stg_b", [96, D], F32)[:]
            stg_c = sb("stg_c", [P, NDT * 32], F32)[:]
            stg_row = sb("stg_row", [1, D], F32)[:]
            onesrow = sb("onesrow", [1, max(SEG, D)], F32)[:]
            zr = sb("zr", [64, D], F32)[:]
            aug32 = sb("aug32", [32, D], F32)[:]

        px = psum("px", [96, 512])
        pt0 = psum("pt0", [P, 512])
        pt1 = psum("pt1", [P, 512])
        pdtz = psum("pdtz", [P, 512])
        pdtz2 = psum("pdtz2", [P, 512])
        pf0 = psum("pf0", [P, 512])
        pf1 = psum("pf1", [P, 512])
        psm = psum("psm", [P, 64])

        sems = {e: st.enter_context(nc.semaphore(f"sem_{e}")) for e in SEMS}
        V, S, T, Q0, Q2 = "V", "S", "T", "q0", "q2"

        if SEG >= 1024:
            lntmp = ystage[:][:, 0:D]
        else:
            lntmp = sb("lntmp", [P, D], F32)[:]
        last_use = {}
        ltok = {}

        def guard(key):
            t = last_use.get(key)
            if t is None:
                return []
            return list(t) if isinstance(t, (list, tuple)) else [t]

        def setg(key, *toks):
            last_use[key] = list(toks)

        pr.emit(Q2, lambda g: g.memset(ident_f[:], 0.0), inc=1, sem="G")
        t_id = pr.emit(Q2, lambda g: g.affine_select(
            out=ident_f[:], in_=ident_f[:], compare_op=ALU.not_equal,
            fill=1.0, base=0, pattern=[[-1, P]], channel_multiplier=1), inc=1, sem="G")
        t_idr = pr.emit(V, lambda v: v.tensor_copy(out=ident_r[:], in_=ident_f[:]),
                        waits=[t_id])
        t_idb = pr.emit(V, lambda v: v.tensor_copy(out=ident_b[:], in_=ident_f[:]),
                        waits=[t_id])

        t_wx = pr.emit(Q0, lambda s: s.dma_start(out=stg_b, in_=wx_d))
        t_wdt = pr.emit(Q0, lambda s: s.dma_start(
            out=stg_a.rearrange("p (c r) -> p c r", c=NDT),
            in_=wdt_d.rearrange("(c p) r -> p c r", p=P)))
        t_bdt = pr.emit(Q0, lambda s: s.dma_start(out=stg_row, in_=bdt_d))
        t_alog = pr.emit(Q0, lambda s: s.dma_start(
            out=acol[:].rearrange("p (c n) -> p c n", c=NDT),
            in_=alog_d.rearrange("(c p) n -> p c n", p=P)))
        t_dpar = pr.emit(Q0, lambda s: s.dma_start(
            out=dcol[:].rearrange("p (o c) -> p o c", o=1),
            in_=dpar_d.rearrange("o (c p) -> p o c", p=P),
            allow_slow_non_contiguous=True))

        t_zrm = pr.emit(V, lambda v: v.memset(zr, 0.0))
        pr.emit(V, lambda v: v.memset(epscol[:], 1e-5))
        pr.emit(V, lambda v: v.memset(neghalf[:], -0.5))
        pr.emit(V, lambda v: v.memset(invDc[:], 1.0 / D))
        t_zr = pr.emit(V, lambda v: v.memset(onesrow, 1.0))
        t_carry0 = pr.emit(V, lambda v: v.memset(carry[:], 0.0))
        t_ae = pr.emit(S, lambda s: s.activation(out=acol[:], in_=acol[:],
                                                 func=ACT.Exp), waits=[t_alog])
        t_acol = pr.emit(S, lambda s: s.activation(out=acol[:], in_=acol[:],
                                                  func=ACT.Identity, scale=-1.0))
        pr.emit(V, lambda v: v.memset(aug32, 0.0))
        t_aug1 = pr.emit(V, lambda v: v.memset(aug32[0:1, :], 1.0))
        pr.emit(S, lambda s: s.copy(out=dt_rawT_aug[64:96, :], in_=aug32[:, 0:SEG]),
                waits=[t_aug1])
        t_aug2 = pr.emit(S, lambda s: s.copy(out=aug32[0:1, :], in_=stg_row),
                         waits=[t_bdt])
        pr.emit(S, lambda s: s.copy(out=wdtT_aug[64:96, :], in_=aug32[:, 0:NDT * P]),
                waits=[t_aug2])
        for dt in range(NDT):
            tt = pr.emit(T, lambda t, dt=dt: t.transpose(
                pt0[0:RK, 0:P], in_=stg_a[:, dt * RK:(dt + 1) * RK],
                identity=ident_f[:]), waits=[t_wdt, t_id] + guard(("pt0",)))
            te = pr.emit(S, lambda s, dt=dt: s.copy(
                out=wdtT_aug[0:RK, dt * P:(dt + 1) * P], in_=pt0[0:RK, 0:P]), waits=[tt])
            setg(("pt0",), te)
        wx_tts = []
        for c in range(NDT):
            tt = pr.emit(T, lambda t, c=c: t.transpose(
                pt1[:, 0:96], in_=stg_b[0:96, c * P:(c + 1) * P],
                identity=ident_f[0:96, 0:96]), waits=[t_wx, t_id] + guard(("pt1",)))
            te = pr.emit(S, lambda s, c=c: s.copy(out=wx_lhsT[:, c, :],
                                                  in_=pt1[:, 0:96]), waits=[tt])
            setg(("pt1",), te)
            wx_tts.append(tt)
        t_wxT = pr.tok(S)
        setg(("stg_b",), *wx_tts)
        t_wdtT = t_wxT
        for c in range(NDT):
            tl = pr.emit(Q0, lambda s, c=c: s.dma_start(
                out=stg_b[0:96, :], in_=wlin_d[c * P:c * P + 96, :]),
                waits=guard(("stg_b",)))
            tl2 = pr.emit(Q0, lambda s, c=c: s.dma_start(
                out=zr[0:32, :], in_=wlin_d[c * P + 96:(c + 1) * P, :]),
                waits=[t_zrm] + guard(("zrw",)))
            es = []
            for b in range(NDT):
                tt = pr.emit(T, lambda t, b=b: t.transpose(
                    pt0[:, 0:96], in_=stg_b[0:96, b * P:(b + 1) * P],
                    identity=ident_f[0:96, 0:96]), waits=[tl, t_id] + guard(("pt0",)))
                tt2 = pr.emit(T, lambda t, b=b: t.transpose(
                    pt1[:, 0:32], in_=zr[0:32, b * P:(b + 1) * P],
                    identity=ident_f[0:32, 0:32]), waits=[tl2, t_id] + guard(("pt1",)))
                e1 = pr.emit(S, lambda s, c=c, b=b: s.copy(
                    out=wlinT[:, b, c * P:c * P + 96], in_=pt0[:, 0:96]), waits=[tt])
                e2 = pr.emit(S, lambda s, c=c, b=b: s.copy(
                    out=wlinT[:, b, c * P + 96:(c + 1) * P], in_=pt1[:, 0:32]),
                    waits=[tt2])
                setg(("pt0",), e1)
                setg(("pt1",), e2)
                es += [e1, e2]
            setg(("stg_b",), *es)
            setg(("zrw",), *es)
        t_gw = pr.emit(Q0, lambda s: s.dma_start(out=stg_b[0:8, :], in_=gw_d),
                       waits=guard(("stg_b",)))
        for c in range(NDT):
            tt = pr.emit(T, lambda t, c=c: t.transpose(
                pt0[:, 0:8], in_=stg_b[0:8, c * P:(c + 1) * P],
                identity=ident_f[0:8, 0:8]), waits=[t_gw, t_id] + guard(("pt0",)))
            te = pr.emit(S, lambda s, c=c: s.copy(out=glT[:, c, 0:8],
                                                  in_=pt0[:, 0:8]), waits=[tt])
            setg(("pt0",), te)
        t_lfs = []
        for e_ in range(E):
            t_lfs.append(pr.emit(Q0, lambda s, e_=e_: s.dma_start(
                out=stg_c.rearrange("p (c e r) -> p c e r", c=NDT, e=E)[:, :, e_, :],
                in_=left_d[e_].rearrange("(c p) r -> p c r", p=P))))
        t_lf = t_lfs[-1]
        t_glT = pr.emit(V, lambda v: v.tensor_copy(
            out=glT[:, :, 8:40],
            in_=stg_c.rearrange("p (c k) -> p c k", c=NDT)), waits=t_lfs)
        t_rg = pr.emit(Q0, lambda s: s.dma_start(out=stg_b[0:32, :], in_=right_d),
                       waits=[t_gw, pr.tok(T)])
        t_bl = pr.emit(Q0, lambda s: s.dma_start(out=stg_row, in_=blin_d),
                       waits=[pr.tok(S)])
        pr.emit(S, lambda s: s.copy(out=right_aug[0:32, :], in_=stg_b[0:32, :]),
                waits=[t_rg])
        t_aug3 = pr.emit(S, lambda s: s.copy(out=aug32[0:1, :], in_=stg_row),
                         waits=[t_bl])
        pr.emit(S, lambda s: s.copy(out=right_aug[32:64, :], in_=aug32[:, 0:D]),
                waits=[t_aug3])
        t_raug = pr.tok(S)
        t_aug4 = pr.emit(S, lambda s: s.copy(out=aug32[0:1, :], in_=onesrow[0:1, 0:D]),
                         waits=[t_zr])
        for i in range(2):
            pr.emit(S, lambda s, i=i: s.copy(out=loraT[i][32:64, :],
                                             in_=aug32[:, 0:P]), waits=[t_aug4])
        t_lorainit = pr.tok(S)

        t_initdone = [pr.tok("S"), pr.tok("T"), pr.tok("V"), pr.tok("q0"), pr.tok("q2"), pr.tok("G")]
        for k in range(4):
            setg(("ybuf", k), *t_initdone)
        setg(("bc", 0), *t_initdone)
        setg(("bc", 1), *t_initdone)
        prev_carry_S = None
        post_state = [None, None]

        def emit_post_A1(hp, lt):
            sl = hp * SEG + lt * P
            yb = ylt[0]
            ub2 = u2buf[0]
            tyl = pr.emit(Q0, lambda s, sl=sl, yb=yb: s.dma_start(
                out=yb[:], in_=ylp_d[sl:sl + P, :]),
                waits=[ltok[("ydma", hp, dt)] for dt in range(NDT)]
                      + guard(("ylt", 0)))
            tu2 = pr.emit(Q0, lambda s, sl=sl, ub2=ub2: s.dma_start(
                out=ub2[:], in_=u_d[sl:sl + P, :]), waits=guard(("u2", 0)))
            tsq2 = pr.emit(S, lambda s, yb=yb: s.activation(
                out=lntmp, in_=yb[:], func=ACT.Square,
                accum_out=statbuf[:, 1:2]),
                waits=[tyl] + guard(("lntmp",)) + guard(("lntmpR",)) + guard(("yst",)) + guard(("stat",)))
            tid2 = pr.emit(S, lambda s, yb=yb: s.activation(
                out=lntmp, in_=yb[:], func=ACT.Identity,
                accum_out=statbuf[:, 0:1]))
            ltok[("red2", 1)] = tid2
            setg(("lntmpR",), tid2)
            pr.emit(S, lambda s: s.activation(
                out=statbuf[:, 2:3], in_=statbuf[:, 0:1], func=ACT.Square,
                scale=invDc[:, 0:1]))
            pr.emit(S, lambda s: s.activation(
                out=statbuf[:, 3:4], in_=statbuf[:, 2:3], func=ACT.Identity,
                scale=-1.0))
            pr.emit(S, lambda s: s.activation(
                out=mvbuf[:, 0:1], in_=statbuf[:, 0:1], func=ACT.Identity,
                scale=invDc[:, 0:1]))
            tsf2 = pr.emit(S, lambda s: s.copy(out=sdmy[:, 2:4], in_=statbuf[:, 2:4]))
            pr.selfwait(S, tsf2)
            tvar2 = pr.emit(S, lambda s: s.activation(
                out=mvbuf[:, 1:2], in_=statbuf[:, 1:2], func=ACT.Identity,
                scale=invDc[:, 0:1], bias=statbuf[:, 3:4]))
            pr.emit(S, lambda s: s.activation(out=rcol[:, 0:1], in_=mvbuf[:, 1:2],
                                              func=ACT.Ln, bias=epscol[:, 0:1]))
            pr.emit(S, lambda s: s.activation(out=rcol[:, 0:1], in_=rcol[:, 0:1],
                                              func=ACT.Exp, scale=neghalf[:, 0:1]))
            tng2 = pr.emit(S, lambda s: s.activation(
                out=rcol[:, 1:2], in_=mvbuf[:, 0:1], func=ACT.Identity, scale=-1.0))
            setg(("mv",), tng2)
            tfl2 = pr.emit(S, lambda s: s.copy(out=sdmy[:, 0:2], in_=rcol[:, 0:2]))
            pr.selfwait(S, tfl2)
            ta2 = pr.emit(S, lambda s, yb=yb: s.activation(
                out=lntmp, in_=yb[:], func=ACT.Identity, bias=rcol[:, 1:2]),
                waits=[ltok[("red2", 1)]] + guard(("lntmp",)) + guard(("lntmpR",)) + guard(("yst",)))
            t6 = pr.emit(S, lambda s: s.activation(
                out=yln[:], in_=lntmp, func=ACT.Identity, scale=rcol[:, 0:1]),
                waits=guard(("yln",)))
            setg(("lntmp",), t6)
            setg(("ylt", 0), t6)
            return tu2, t6

        def emit_post_A2(hp, lt, tu2, t6):
            sl = hp * SEG + lt * P
            ub2 = u2buf[0]
            t7 = pr.emit(V, lambda v, ub2=ub2: v.tensor_tensor(
                out=x2buf[:], in0=ub2[:], in1=yln[:], op=ALU.add),
                waits=[tu2, t6] + guard(("x2",)))
            setg(("u2", 0), t7)
            setg(("yln",), t7)
            tsq3 = pr.emit(S, lambda s: s.activation(
                out=lntmp, in_=x2buf[:], func=ACT.Square,
                accum_out=statbuf[:, 1:2]),
                waits=[t7] + guard(("lntmp",)) + guard(("lntmpR",)) + guard(("yst",)))
            tid3 = pr.emit(S, lambda s: s.activation(
                out=lntmp, in_=x2buf[:], func=ACT.Identity,
                accum_out=statbuf[:, 0:1]))
            tr2c = tid3
            setg(("lntmpR",), tid3)
            pr.emit(S, lambda s: s.activation(
                out=statbuf[:, 2:3], in_=statbuf[:, 0:1], func=ACT.Square,
                scale=invDc[:, 0:1]))
            pr.emit(S, lambda s: s.activation(
                out=statbuf[:, 3:4], in_=statbuf[:, 2:3], func=ACT.Identity,
                scale=-1.0))
            pr.emit(S, lambda s: s.activation(
                out=mvbuf[:, 0:1], in_=statbuf[:, 0:1], func=ACT.Identity,
                scale=invDc[:, 0:1]))
            tsf3 = pr.emit(S, lambda s: s.copy(out=sdmy[:, 2:4], in_=statbuf[:, 2:4]))
            pr.selfwait(S, tsf3)
            tvar3 = pr.emit(S, lambda s: s.activation(
                out=mvbuf[:, 1:2], in_=statbuf[:, 1:2], func=ACT.Identity,
                scale=invDc[:, 0:1], bias=statbuf[:, 3:4]))
            setg(("stat",), tvar3)
            pr.emit(S, lambda s: s.activation(out=rcol[:, 0:1], in_=mvbuf[:, 1:2],
                                              func=ACT.Ln, bias=epscol[:, 0:1]))
            pr.emit(S, lambda s: s.activation(out=rcol[:, 0:1], in_=rcol[:, 0:1],
                                              func=ACT.Exp, scale=neghalf[:, 0:1]))
            tng3 = pr.emit(S, lambda s: s.activation(
                out=rcol[:, 1:2], in_=mvbuf[:, 0:1], func=ACT.Identity, scale=-1.0))
            setg(("mv",), tng3)
            tfl3 = pr.emit(S, lambda s: s.copy(out=sdmy[:, 0:2], in_=rcol[:, 0:2]))
            pr.selfwait(S, tfl3)
            ta3 = pr.emit(S, lambda s: s.activation(
                out=lntmp, in_=x2buf[:], func=ACT.Identity, bias=rcol[:, 1:2]),
                waits=[tr2c] + guard(("lntmp",)) + guard(("lntmpR",)) + guard(("yst",)))
            td_ = pr.emit(S, lambda s: s.activation(
                out=xn3[:], in_=lntmp, func=ACT.Identity, scale=rcol[:, 0:1]),
                waits=guard(("xn3",)))
            setg(("lntmp",), td_)
            tes = []
            tts = []
            for half in range(2):
                pb, pk = ((pt0, "pt0"), (pt1, "pt1"))[half]
                tths = []
                for q in range(4):
                    c = half * 4 + q
                    w = [td_, t_idb] + (guard((pk,)) if q == 0 else [])
                    tt = pr.emit(T, lambda t, c=c, q=q, pb=pb: t.transpose(
                        pb[:].bitcast(BF16)[:, q * P:(q + 1) * P],
                        in_=xn3[:, c * P:(c + 1) * P],
                        identity=ident_b[:]), waits=w)
                    tths.append(tt)
                    tts.append(tt)
                te = pr.emit(S, lambda s, half=half, pb=pb: s.copy(
                    out=xn3T[:, half * 4 * P:(half + 1) * 4 * P],
                    in_=pb[:].bitcast(BF16)[:, 0:4 * P]),
                    waits=[tths[-1]] + guard(("xn3T",)))
                setg((pk,), te)
                tes += [te] * 4
            setg(("xn3",), *tts)
            for c in range(NDT):
                w0 = [tes[c], t_wxT]
                if c == 0:
                    w0 += guard(("pf",)) + [t_glT]
                stp = False
                tm1 = pr.emit(T, lambda t, c=c, stp=stp: t.matmul(
                    pf0[:], lhsT=xn3T[:, c * P:(c + 1) * P], rhs=wlinT[:, c, 0:512],
                    start=(c == 0), stop=stp), waits=w0)
                pr.emit(T, lambda t, c=c, stp=stp: t.matmul(
                    pf1[:], lhsT=xn3T[:, c * P:(c + 1) * P],
                    rhs=wlinT[:, c, 512:1024],
                    start=(c == 0), stop=stp))
                if True:
                    w2 = guard(("psm",)) if c == 0 else []
                    pr.emit(T, lambda t, c=c: t.matmul(
                        psm[:, 0:40], lhsT=xn3T[:, c * P:(c + 1) * P], rhs=glT[:, c, :],
                        start=(c == 0), stop=(c == NDT - 1)), waits=w2)
            setg(("xn3T",), pr.tok(T))
            t_mm = pr.tok(T)
            return t7, t_mm


        def emit_post_B(hp, lt, t7, t_mm):
            sl = hp * SEG + lt * P
            tsm1 = pr.emit(V, lambda v: v.reduce_max(
                out=sm_sc[:, 0:1], in_=psm[:, 0:8], axis=AXL.X, negate=True),
                waits=[t_mm])
            for _pi in range(3):
                pr.emit(V, lambda v, _pi=_pi: v.memset(vdmy[:, _pi:_pi+1], 0.0))
            tsm1f = pr.emit(V, lambda v: v.tensor_copy(out=vdmy[:, 0:1],
                                                       in_=sm_sc[:, 0:1]))
            tsm2 = pr.emit(S, lambda s: s.activation(
                out=e8[:], in_=psm[:, 0:8], func=ACT.Exp, bias=sm_sc[:, 0:1]),
                waits=[tsm1f] + guard(("e8",)))
            tsm3 = pr.emit(V, lambda v: v.reduce_sum(
                out=sm_sc[:, 1:2], in_=e8[:], axis=AXL.X), waits=[tsm2])
            tsm5 = pr.emit(V, lambda v: v.max(out=top8[:], in_=e8[:]))
            for _pi in range(3):
                pr.emit(V, lambda v, _pi=_pi: v.memset(vdmy[:, _pi:_pi+1], 0.0))
            pr.emit(V, lambda v: v.reciprocal(out=sm_sc[:, 2:3], in_=sm_sc[:, 1:2]))
            pr.emit(V, lambda v: v.tensor_copy(out=vdmy[:, 0:2], in_=sm_sc[:, 1:3]))
            pr.emit(V, lambda v: v.tensor_copy(out=vdmy[:, 2:3], in_=top8[:, 1:2]))
            pr.selfwait(V)
            tsm6 = pr.emit(V, lambda v: v.scalar_tensor_tensor(
                out=g8[:], in0=e8[:], scalar=top8[:, 1:2], in1=e8[:],
                op0=ALU.is_ge, op1=ALU.mult))
            tz_t = pr.emit(V, lambda v: v.scalar_tensor_tensor(
                out=tzb[:], in0=psm[:, 8:40], scalar=sm_sc[:, 2:3],
                in1=g8[:].rearrange("p (e o) -> p e o", o=1).to_broadcast([P, 8, 4]),
                op0=ALU.mult, op1=ALU.mult), waits=guard(("tzb",)))
            setg(("e8",), tz_t)
            setg(("psm",), tz_t, tsm2)
            lb = loraT[lt % 2]
            tt = pr.emit(T, lambda t: t.transpose(
                pt0[:].bitcast(F32R)[0:32, 0:P], in_=tzb[:], identity=ident_r[:]),
                waits=[tz_t, t_idr] + guard(("pt0",)))
            te = pr.emit(S, lambda s, lb=lb: s.copy(
                out=lb[0:32, :], in_=pt0[:].bitcast(F32R)[0:32, 0:P]),
                waits=[tt, t_lorainit] + guard(("lora", lt % 2)))
            setg(("pt0",), te)
            setg(("tzb",), tt)
            tl1 = pr.emit(T, lambda t, lb=lb: t.matmul(
                pf0[:], lhsT=lb[:], rhs=right_aug[:, 0:512], start=False, stop=True),
                waits=[te, t_raug])
            tl2 = pr.emit(T, lambda t, lb=lb: t.matmul(
                pf1[:], lhsT=lb[:], rhs=right_aug[:, 512:1024], start=False,
                stop=True))
            setg(("lora", lt % 2), tl2)
            t_ffn = pr.tok(T)
            tsq4a = pr.emit(S, lambda s: s.activation(
                out=lntmp[:, 0:512], in_=pf0[:], func=ACT.Square,
                accum_out=st4[:, 0:1]),
                waits=[t_ffn] + guard(("lntmp",)) + guard(("lntmpR",)) + guard(("yst",)) + [t7])
            tsq4b = pr.emit(S, lambda s: s.activation(
                out=lntmp[:, 512:1024], in_=pf1[:], func=ACT.Square,
                accum_out=st4[:, 1:2]))
            tid4a = pr.emit(S, lambda s: s.activation(
                out=lntmp[:, 0:512], in_=pf0[:], func=ACT.Identity,
                accum_out=st4[:, 7:8]))
            tid4b = pr.emit(S, lambda s: s.activation(
                out=lntmp[:, 512:1024], in_=pf1[:], func=ACT.Identity,
                accum_out=st4[:, 8:9]))
            tr2d = tid4b
            setg(("lntmpR",), tid4b)
            tsfa = pr.emit(S, lambda s: s.copy(out=sdmy[:, 0:1], in_=st4[:, 8:9]))
            pr.selfwait(S, tsfa)
            pr.emit(S, lambda s: s.activation(
                out=st4[:, 2:3], in_=st4[:, 0:1], func=ACT.Identity,
                bias=st4[:, 1:2]))
            pr.emit(S, lambda s: s.activation(
                out=st4[:, 3:4], in_=st4[:, 7:8], func=ACT.Identity,
                bias=st4[:, 8:9]))
            pr.emit(S, lambda s: s.activation(
                out=st4[:, 4:5], in_=st4[:, 3:4], func=ACT.Square,
                scale=invDc[:, 0:1]))
            pr.emit(S, lambda s: s.activation(
                out=st4[:, 6:7], in_=st4[:, 4:5], func=ACT.Identity,
                scale=-1.0))
            pr.emit(S, lambda s: s.activation(
                out=mv4[:, 0:1], in_=st4[:, 3:4], func=ACT.Identity,
                scale=invDc[:, 0:1]))
            tsf4 = pr.emit(S, lambda s: s.copy(out=sdmy[:, 2:3], in_=st4[:, 6:7]))
            pr.selfwait(S, tsf4)
            tvar4 = pr.emit(S, lambda s: s.activation(
                out=mv4[:, 1:2], in_=st4[:, 2:3], func=ACT.Identity,
                scale=invDc[:, 0:1], bias=st4[:, 6:7]))
            pr.emit(S, lambda s: s.activation(out=rcol[:, 0:1], in_=mv4[:, 1:2],
                                              func=ACT.Ln, bias=epscol[:, 0:1]))
            pr.emit(S, lambda s: s.activation(out=rcol[:, 0:1], in_=rcol[:, 0:1],
                                              func=ACT.Exp, scale=neghalf[:, 0:1]))
            tng4 = pr.emit(S, lambda s: s.activation(
                out=rcol[:, 1:2], in_=mv4[:, 0:1], func=ACT.Identity, scale=-1.0))
            setg(("mv4",), tng4)
            tfl4 = pr.emit(S, lambda s: s.copy(out=sdmy[:, 0:2], in_=rcol[:, 0:2]))
            pr.selfwait(S, tfl4)
            ta4a = pr.emit(S, lambda s: s.activation(
                out=lntmp[:, 0:512], in_=pf0[:], func=ACT.Identity, bias=rcol[:, 1:2]),
                waits=[tr2d] + guard(("lntmp",)) + guard(("lntmpR",)) + guard(("yst",)))
            ta4b = pr.emit(S, lambda s: s.activation(
                out=lntmp[:, 512:1024], in_=pf1[:], func=ACT.Identity, bias=rcol[:, 1:2]))
            to1 = pr.emit(S, lambda s: s.activation(
                out=yln[:, 0:512], in_=lntmp[:, 0:512], func=ACT.Identity,
                scale=rcol[:, 0:1]), waits=guard(("obuf",)))
            to2 = pr.emit(S, lambda s: s.activation(
                out=yln[:, 512:1024], in_=lntmp[:, 512:1024], func=ACT.Identity,
                scale=rcol[:, 0:1]))
            setg(("lntmp",), to2)
            setg(("pf",), ta4b)
            tfin = pr.emit(V, lambda v: v.tensor_tensor(
                out=yln[:], in0=yln[:], in1=x2buf[:], op=ALU.add), waits=[to2])
            setg(("x2",), tfin)
            tod = pr.emit(Q0, lambda s, sl=sl: s.dma_start(
                out=out_d[sl:sl + P, :], in_=yln[:]), waits=[tfin])
            setg(("yln",), tod)
            setg(("obuf",), tod)

        for h in range(NH):
            lbase = h * SEG
            for sub in range(NSUB):
                for i in range(4):
                    lt = sub * 4 + i
                    sl = lbase + lt * P
                    ub = ubuf[lt % 2]
                    xb = xnbuf[lt % 2]
                    tu = pr.emit(Q0, lambda s, sl=sl, ub=ub: s.dma_start(
                        out=ub[:], in_=u_d[sl:sl + P, :]), waits=guard(("ub", lt % 2)))
                    tsq = pr.emit(S, lambda s, ub=ub: s.activation(
                        out=lntmp, in_=ub[:], func=ACT.Square,
                        accum_out=statbuf[:, 1:2]),
                        waits=[tu] + guard(("lntmp",)) + guard(("lntmpR",)) + guard(("yst",)) + guard(("stat",)))
                    tid = pr.emit(S, lambda s, ub=ub: s.activation(
                        out=lntmp, in_=ub[:], func=ACT.Identity,
                        accum_out=statbuf[:, 0:1]))
                    ltok[("red2", 0)] = tid
                    setg(("lntmpR",), tid)
                    pr.emit(S, lambda s: s.activation(
                        out=statbuf[:, 2:3], in_=statbuf[:, 0:1], func=ACT.Square,
                        scale=invDc[:, 0:1]))
                    pr.emit(S, lambda s: s.activation(
                        out=statbuf[:, 3:4], in_=statbuf[:, 2:3], func=ACT.Identity,
                        scale=-1.0))
                    pr.emit(S, lambda s: s.activation(
                        out=mvbuf[:, 0:1], in_=statbuf[:, 0:1], func=ACT.Identity,
                        scale=invDc[:, 0:1]))
                    tsf0 = pr.emit(S, lambda s: s.copy(out=sdmy[:, 2:4], in_=statbuf[:, 2:4]))
                    pr.selfwait(S, tsf0)
                    tvar = pr.emit(S, lambda s: s.activation(
                        out=mvbuf[:, 1:2], in_=statbuf[:, 1:2], func=ACT.Identity,
                        scale=invDc[:, 0:1], bias=statbuf[:, 3:4]))
                    tln1 = pr.emit(S, lambda s: s.activation(
                        out=rcol[:, 0:1], in_=mvbuf[:, 1:2], func=ACT.Ln, bias=epscol[:, 0:1]))
                    pr.emit(S, lambda s: s.activation(
                        out=rcol[:, 0:1], in_=rcol[:, 0:1], func=ACT.Exp, scale=neghalf[:, 0:1]))
                    tng = pr.emit(S, lambda s: s.activation(
                        out=rcol[:, 1:2], in_=mvbuf[:, 0:1], func=ACT.Identity, scale=-1.0))
                    setg(("mv",), tng)
                    tfl = pr.emit(S, lambda s: s.copy(out=sdmy[:, 0:2], in_=rcol[:, 0:2]))
                    pr.selfwait(S, tfl)
                    if dbg and h == 0:
                        pr.emit(S, lambda s, lt=lt: s.copy(
                            out=statbuf[:, 4:6], in_=rcol[:, 0:2]))
                        pr.emit(S, lambda s, lt=lt: s.copy(
                            out=statbuf[:, 6:8], in_=mvbuf[:, 0:2]))
                        pr.emit(S, lambda s, lt=lt: s.copy(
                            out=statbuf[:, 8:10], in_=statbuf[:, 1:3]))
                        pr.emit(Q0, lambda s, lt=lt: s.dma_start(
                            out=dbgscal_d[:, lt * 6:lt * 6 + 6], in_=statbuf[:, 4:10]),
                            waits=[pr.tok("S")])
                    if skip_ln:
                        txn = pr.emit(S, lambda s, ub=ub, xb=xb: s.copy(
                            out=xb[:], in_=ub[:]),
                            waits=[ltok[("red2", 0)]] + guard(("xb", lt % 2)))
                    else:
                        ta1 = pr.emit(S, lambda s, ub=ub: s.activation(
                            out=lntmp, in_=ub[:], func=ACT.Identity, bias=rcol[:, 1:2]),
                            waits=[ltok[("red2", 0)]] + guard(("lntmp",)) + guard(("lntmpR",)) + guard(("yst",)))
                        txn = pr.emit(S, lambda s, xb=xb: s.activation(
                            out=xb[:], in_=lntmp, func=ACT.Identity, scale=rcol[:, 0:1]),
                            waits=guard(("xb", lt % 2)))
                        setg(("lntmp",), txn)
                    setg(("ub", lt % 2), txn)
                    setg(("stat",), tvar)
                    tts = []
                    for half in range(2):
                        pb, pk = ((pt0, "pt0"), (pt1, "pt1"))[half]
                        tths = []
                        for q in range(4):
                            c = half * 4 + q
                            w = [txn, t_idr] + (guard((pk,)) if q == 0 else [])
                            tt = pr.emit(T, lambda t, xb=xb, c=c, q=q, pb=pb: t.transpose(
                                pb[:].bitcast(F32R)[:, q * P:(q + 1) * P],
                                in_=xb[:, c * P:(c + 1) * P],
                                identity=ident_r[:]), waits=w)
                            tths.append(tt)
                            tts.append(tt)
                        gw = []
                        for q in range(4):
                            gw += guard(("mini", half * 4 + q))
                        te = pr.emit(S, lambda s, half=half, i=i, pb=pb: s.copy(
                            out=mini[:, half * 4:(half + 1) * 4, i * P:(i + 1) * P],
                            in_=pb[:].bitcast(F32R)[:, 0:4 * P]
                            .rearrange("p (c q) -> p c q", q=P)),
                            waits=[tths[-1]] + gw)
                        setg((pk,), te)
                        for q in range(4):
                            ltok[("minie", half * 4 + q, i)] = te
                    setg(("xb", lt % 2), *tts)
                for c in range(NDT):
                    wts = [ltok[("minie", c, i)] for i in range(4)] + [t_wxT]
                    if c == 0:
                        wts += guard(("px",))
                    tm = pr.emit(T, lambda t, c=c: t.matmul(
                        px[:], lhsT=wx_lhsT[:, c, :], rhs=mini[:, c, :],
                        start=(c == 0), stop=(c == NDT - 1)), waits=wts)
                    tsp = pr.emit(Q2, lambda s, c=c, sub=sub, lbase=lbase: s.dma_start(
                        out=xnt_d[c, :, lbase + sub * 512:lbase + (sub + 1) * 512],
                        in_=mini[:, c, :].bitcast(F32)),
                        waits=[ltok[("minie", c, i)] for i in range(4)])
                    setg(("mini", c), tsp, tm)
                t_px = pr.tok(T)
                te1 = pr.emit(S, lambda s, sub=sub: s.copy(
                    out=dt_rawT_aug[0:64, sub * 512:(sub + 1) * 512],
                    in_=px[0:64, :]), waits=[t_px] + guard(("draw", sub)))
                te2 = pr.emit(S, lambda s: s.copy(out=bcstage[:], in_=px[64:96, :]),
                              waits=[t_px] + guard(("bcstage",)))
                pr.emit(Q2, lambda s, sub=sub, lbase=lbase: s.dma_start(
                    out=bc_d[0, :, lbase + sub * 512:lbase + (sub + 1) * 512],
                    in_=bcstage[0:16, :]), waits=[te2])
                tbc = pr.emit(Q2, lambda s, sub=sub, lbase=lbase: s.dma_start(
                    out=bc_d[1, :, lbase + sub * 512:lbase + (sub + 1) * 512],
                    in_=bcstage[16:32, :]))
                setg(("bcstage",), tbc)
                setg(("px",), te1, te2)
                ltok[("draw", sub)] = te1
                ltok[("bcd", sub)] = tbc

            def prepassA(dg):
                outa = []
                for k, dt in enumerate(dg):
                    xs = xstrip[k]
                    tx = pr.emit(Q0, lambda s, dt=dt, xs=xs, lbase=lbase: s.dma_start(
                        out=xs[:], in_=xnt_d[dt, :, lbase:lbase + SEG]),
                        waits=[ltok[("bcd", sub)] for sub in range(NSUB)]
                              + guard(("xs", k)))
                    for sub in range(NSUB):
                        bix = (k * NSUB + sub) % 2
                        pzb = (pdtz, pdtz2)[bix]
                        tm = pr.emit(T, lambda t, dt=dt, sub=sub, pzb=pzb: t.matmul(
                            pzb[:], lhsT=wdtT_aug[:, dt * P:(dt + 1) * P],
                            rhs=dt_rawT_aug[:, sub * 512:(sub + 1) * 512],
                            start=True, stop=True),
                            waits=[t_wdtT, ltok[("draw", sub)]] + guard(("pdtz", bix)))
                        ts1 = pr.emit(S, lambda s, k=k, sub=sub, pzb=pzb: s.activation(
                            out=dtT[:, k, sub * 512:(sub + 1) * 512], in_=pzb[:],
                            func=ACT.Exp), waits=[tm] + guard(("dtTr", k)))
                        pr.emit(S, lambda s, k=k, sub=sub: s.activation(
                            out=dtT[:, k, sub * 512:(sub + 1) * 512],
                            in_=dtT[:, k, sub * 512:(sub + 1) * 512],
                            func=ACT.Ln, bias=1.0))
                        setg(("pdtz", bix), ts1)
                        ltok[("drawuse", sub)] = tm
                    t_dtT = pr.tok(S)
                    tw = pr.emit(V, lambda v, k=k, xs=xs: v.tensor_tensor(
                        out=wT[:, k, :], in0=dtT[:, k, :], in1=xs[:], op=ALU.mult),
                        waits=[tx, t_dtT])
                    setg(("dtTr", k), tw)
                    ltok[("w", k)] = tw
                    outa.append((k, dt, xs, tx))
                setg(("draw", 0), *[ltok[("drawuse", s_)] for s_ in range(NSUB)])
                if NSUB > 1:
                    setg(("draw", 1), *[ltok[("drawuse", s_)] for s_ in range(NSUB)])
                return outa

            def prepassB(outa):
                for k, dt, xs, tx in outa:
                    ty0 = pr.emit(V, lambda v, k=k, dt=dt, xs=xs: v.tensor_scalar(
                        out=ybuf[:, k, :], in0=xs[:], scalar1=dcol[:, dt:dt + 1],
                        scalar2=None, op0=ALU.mult),
                        waits=[t_dpar, tx] + guard(("ybuf", k)))
                    setg(("xs", k), ty0)
                    ltok[("y0", k)] = ty0

            def nloop(gi, dg):
                nonlocal prev_carry_S
                prev_carry_local = None
                for n in range(N):
                    bb = bcbuf[n % 2]
                    tb1 = pr.emit(Q2, lambda s, n=n, bb=bb, lbase=lbase: s.dma_start(
                        out=bb[:], in_=bc_d[:, n:n + 1, lbase:lbase + SEG]
                        .rearrange("bc o c -> o bc c").to_broadcast([P, 2, SEG])),
                        waits=[ltok[("bcd", sub)] for sub in range(NSUB)]
                              + guard(("bc", n % 2)))
                    tb2 = tb1
                    ta = []
                    for k, dt in enumerate(dg):
                        ta.append(pr.emit(S, lambda s, k=k, dt=dt, n=n: s.activation(
                            out=abuf[:, k, :], in_=dtT[:, k, :], func=ACT.Exp,
                            scale=acol[:, dt * N + n:dt * N + n + 1]),
                            waits=[t_acol] + guard(("abuf", k))))
                    tbv = pr.emit(V, lambda v, bb=bb: v.tensor_tensor(
                        out=bybuf[:], in0=wT[:],
                        in1=bb[:, 0:1, :].to_broadcast([P, 4, SEG]), op=ALU.mult),
                        waits=[tb1, ltok[("w", 0)], ltok[("w", 3)]])
                    if n == 0 and h == 0 and gi == 0:
                        pr.selfwait(V, t_carry0)
                    if prev_carry_S is not None:
                        pr.selfwait(V, prev_carry_S)
                    tscan = []
                    for k, dt in enumerate(dg):
                        cix = n * NDT + dt
                        tscan.append(pr.emit(V, lambda v, k=k, cix=cix: v.tensor_tensor_scan(
                            out=abuf[:, k, :], data0=abuf[:, k, :], data1=bybuf[:, k, :],
                            initial=carry[:, cix:cix + 1], op0=ALU.mult, op1=ALU.add),
                            waits=[ta[-1]]))
                    t_scans = pr.tok(V)
                    c0 = n * NDT + dg[0]
                    tcs = pr.emit(S, lambda s, c0=c0: s.copy(
                        out=carry[:, c0:c0 + 4],
                        in_=abuf[:, :, SEG - 1:SEG].rearrange("p k o -> p (k o)")),
                        waits=tscan)
                    prev_carry_local = tcs
                    tyk = pr.emit(V, lambda v, bb=bb: v.tensor_tensor(
                        out=bybuf[:], in0=abuf[:],
                        in1=bb[:, 1:2, :].to_broadcast([P, 4, SEG]), op=ALU.mult),
                        waits=[tb2])
                    setg(("bc", n % 2), tyk)
                    for k in range(4):
                        setg(("abuf", k), tyk, tcs)
                    tya = pr.emit(V, lambda v: v.tensor_tensor(
                        out=ybuf[:], in0=ybuf[:], in1=bybuf[:], op=ALU.add),
                        waits=[ltok[("y0", 0)], ltok[("y0", 3)]])
                    j = gi * N + n
                    if h > 0 and j % 4 == 1:
                        post_state[0] = emit_post_A1(h - 1, j // 4)
                    elif h > 0 and j % 4 == 2:
                        post_state[1] = emit_post_A2(h - 1, j // 4, *post_state[0])
                    elif h > 0 and j % 4 == 3:
                        emit_post_B(h - 1, j // 4, *post_state[1])
                prev_carry_S = prev_carry_local
                for k in range(4):
                    setg(("dtTr", k), pr.tok(S), pr.tok(V))

            def yout(dg):
                t_y_done = pr.tok(V)
                for k, dt in enumerate(dg):
                    yes = []
                    tts_k = []
                    for half in range(2):
                        pb, pk = ((pt0, "pt0"), (pt1, "pt1"))[half]
                        tths = []
                        for q in range(4):
                            lt = half * 4 + q
                            w = [t_y_done, t_id] + (guard((pk,)) if q == 0 else [])
                            tt = pr.emit(T, lambda t, k=k, lt=lt, q=q, pb=pb: t.transpose(
                                pb[:, q * P:(q + 1) * P],
                                in_=ybuf[:, k, lt * P:(lt + 1) * P],
                                identity=ident_f[:]), waits=w)
                            tths.append(tt)
                            tts_k.append(tt)
                        te = pr.emit(S, lambda s, half=half, pb=pb: s.copy(
                            out=ystage[:, half * 4 * P:(half + 1) * 4 * P],
                            in_=pb[:, 0:4 * P]),
                            waits=[tths[-1]] + guard(("yst",)) + guard(("lntmp",)))
                        setg((pk,), te)
                        yes.append(te)
                    ty = pr.emit(Q2, lambda s, dt=dt, lbase=lbase: s.dma_start(
                        out=ylp_d[lbase:lbase + SEG, dt * P:(dt + 1) * P]
                        .rearrange("(lt p) c -> p lt c", p=P),
                        in_=ystage[:].rearrange("p (lt c) -> p lt c", c=P)),
                        waits=yes)
                    setg(("yst",), ty)
                    ltok[("ydma", h, dt)] = ty
                    setg(("ybuf", k), *tts_k)

            if not skip_scan:
                pa0 = prepassA(DGRPS[0])
                prepassB(pa0)
                nloop(0, DGRPS[0])
                pa1 = prepassA(DGRPS[1])
                yout(DGRPS[0])
                prepassB(pa1)
                nloop(1, DGRPS[1])
                yout(DGRPS[1])


        for lt in range(NLT):
            _p1 = emit_post_A1(NH - 1, lt)
            _p2 = emit_post_A2(NH - 1, lt, *_p1)
            emit_post_B(NH - 1, lt, *_p2)

        pr.selfwait(Q0)
        pr.selfwait(Q2)

        with nc.Block() as blk:
            def run_stream(name):
                def body(eng):
                    for wl, fn, inc, semn in pr.streams[name]:
                        for (pe, v) in wl:
                            eng.wait_ge(sems[pe], v)
                        if fn is not None:
                            inst = fn(eng)
                            if inc:
                                inst.then_inc(sems[semn], inc)
                return body
            blk.vector(run_stream("V"))
            blk.scalar(run_stream("S"))
            blk.tensor(run_stream("T"))
            blk.sync(run_stream("q0"))
            blk.gpsimd(run_stream("q2"))

    return nc


def prep_inputs(inp, L):
    right = np.ascontiguousarray(np.asarray(inp["L0_ffn_right"], np.float32)
                                 .reshape(E * R, D))
    base = {
        "A_log": np.ascontiguousarray(inp["L0_attn_A_log"], dtype=np.float32),
        "W_x": np.ascontiguousarray(inp["L0_attn_W_x"], dtype=np.float32),
        "W_dt": np.ascontiguousarray(inp["L0_attn_W_dt"], dtype=np.float32),
        "b_dt": np.ascontiguousarray(np.asarray(inp["L0_attn_b_dt"], np.float32)
                                     .reshape(1, D)),
        "D_param": np.ascontiguousarray(np.asarray(inp["L0_attn_D_param"], np.float32)
                                        .reshape(1, D)),
        "W_lin": np.ascontiguousarray(inp["L0_ffn_W_lin"], dtype=np.float32),
        "b_lin": np.ascontiguousarray(np.asarray(inp["L0_ffn_b_lin"], np.float32)
                                      .reshape(1, D)),
        "gate_w": np.ascontiguousarray(inp["L0_ffn_gate_w"], dtype=np.float32),
        "left": np.ascontiguousarray(inp["L0_ffn_left"], dtype=np.float32),
        "right": right,
    }
    u = np.asarray(inp["L0_attn_u"], np.float32)
    maps = []
    for b in range(u.shape[0]):
        m = dict(base)
        m["u"] = np.ascontiguousarray(u[b, :L, :])
        maps.append(m)
    return maps


def kernel(**inputs):
    from concourse.bass_utils import run_bass_kernel_spmd
    L = 4096
    nc = build_nc(L=L, SEG=1024)
    maps = prep_inputs(inputs, L)
    res = run_bass_kernel_spmd(nc, maps, core_ids=list(range(8)), trace=False)
    return np.stack([np.asarray(r["out"], np.float32) for r in res.results])


# version 05
# speedup vs baseline: 1.1322x; 1.0019x over previous
import numpy as np
from contextlib import ExitStack
import concourse.bass as bass
import concourse.mybir as mybir

F32 = mybir.dt.float32
F32R = mybir.dt.float32r
BF16 = mybir.dt.bfloat16
ALU = mybir.AluOpType
ACT = mybir.ActivationFunctionType
AXL = mybir.AxisListType

D = 1024
N = 16
RK = 64
NDT = 8
E = 8
R = 4
P = 128

ENGS = ("V", "S", "T", "q0", "q2")
SEMS = ("V", "S", "T", "q0", "q2", "G", "q1")
SEM2ENG = {"V": "V", "S": "S", "T": "T", "q0": "q0", "q2": "q2", "G": "q2", "q1": "S"}


class Prog:
    def __init__(self):
        self.streams = {e: [] for e in ENGS}
        self.cnt = {s: 0 for s in SEMS}
        self.waited = {c: {p: 0 for p in SEMS} for c in ENGS}

    def emit(self, eng, fn, waits=(), inc=None, sem=None):
        sem = sem or eng
        if inc is None:
            inc = 16 if eng in ("q0", "q2") else 1
        if eng in ("q0", "q2") and sem == eng and self.cnt[eng] > 0:
            self.selfwait(eng)
        if eng == "T" and self.cnt["T"] >= 2:
            self.selfwait("T", ("T", self.cnt["T"] - 2))
        wl = []
        for tok in waits:
            if tok is None:
                continue
            pe, v = tok
            if pe not in ("q0", "q1", "q2") and SEM2ENG[pe] == eng:
                continue
            if v > self.waited[eng][pe]:
                self.waited[eng][pe] = v
                wl.append((pe, v))
        self.cnt[sem] += inc
        self.streams[eng].append((wl, fn, inc, sem))
        return (sem, self.cnt[sem])

    def selfwait(self, eng, tok=None):
        if tok is None:
            tok = (eng, self.cnt[eng])
        pe, v = tok
        if v > self.waited[eng][pe]:
            self.waited[eng][pe] = v
            self.streams[eng].append(([(pe, v)], None, 0, eng))

    def tok(self, sem):
        return (sem, self.cnt[sem])


def build_nc(L=4096, SEG=1024, name="mk"):
    dbg = False
    skip_post = False
    skip_scan = False
    skip_ln = False
    post_level = 4
    assert L % SEG == 0 and SEG % 512 == 0
    NH = L // SEG
    NSUB = SEG // 512
    NLT = SEG // 128
    DGRPS = ((0, 1, 2, 3), (4, 5, 6, 7))

    nc = bass.Bass("TRN2", target_bir_lowering=False, name=name,
                   detect_race_conditions=False)

    u_d = nc.dram_tensor("u", [L, D], F32, kind="ExternalInput").ap()
    alog_d = nc.dram_tensor("A_log", [D, N], F32, kind="ExternalInput").ap()
    wx_d = nc.dram_tensor("W_x", [96, D], F32, kind="ExternalInput").ap()
    wdt_d = nc.dram_tensor("W_dt", [D, RK], F32, kind="ExternalInput").ap()
    bdt_d = nc.dram_tensor("b_dt", [1, D], F32, kind="ExternalInput").ap()
    dpar_d = nc.dram_tensor("D_param", [1, D], F32, kind="ExternalInput").ap()
    wlin_d = nc.dram_tensor("W_lin", [D, D], F32, kind="ExternalInput").ap()
    blin_d = nc.dram_tensor("b_lin", [1, D], F32, kind="ExternalInput").ap()
    gw_d = nc.dram_tensor("gate_w", [E, D], F32, kind="ExternalInput").ap()
    left_d = nc.dram_tensor("left", [E, D, R], F32, kind="ExternalInput").ap()
    right_d = nc.dram_tensor("right", [E * R, D], F32, kind="ExternalInput").ap()
    out_d = nc.dram_tensor("out", [L, D], F32, kind="ExternalOutput").ap()
    knd = "ExternalOutput" if dbg else "Internal"
    dbgscal_d = nc.dram_tensor("dbgscal", [P, 64], F32, kind=knd).ap()
    dbgsq_d = nc.dram_tensor("dbgsq", [P, D], F32, kind=knd).ap()
    bc_d = nc.dram_tensor("bc_scr", [2, 16, L], F32, kind=knd).ap()
    xnt_d = nc.dram_tensor("xnt_scr", [NDT, P, L], F32, kind=knd).ap()
    ylp_d = nc.dram_tensor("ylp_scr", [L, D], F32, kind=knd).ap()

    pr = Prog()
    st = ExitStack()

    def sb(nm, shape, dt):
        return st.enter_context(nc.sbuf_tensor(nm, shape, dt))

    def psum(nm, shape):
        return st.enter_context(nc.psum_tensor(nm, shape, F32))

    with st:
        ident_r = sb("ident_r", [P, P], F32R)
        ident_b = sb("ident_b", [P, P], BF16)
        ident_f = sb("ident_f", [P, P], F32)
        wx_lhsT = sb("wx_lhsT", [P, NDT, 96], F32R)
        wdtT_aug = sb("wdtT_aug", [96, NDT * P], F32R)
        wlinT = sb("wlinT", [P, NDT, D], BF16)
        glT = sb("glT", [P, NDT, 40], BF16)
        right_aug = sb("right_aug", [64, D], F32R)
        acol = sb("acol", [P, NDT * N], F32)
        dcol = sb("dcol", [P, NDT], F32)
        dt_rawT_aug = sb("dt_rawT_aug", [96, SEG], F32R)
        carry = sb("carry", [P, N * NDT], F32)
        dtT = sb("dtT", [P, 4, SEG], F32)
        wT = sb("wT", [P, 4, SEG], F32)
        ybuf = sb("ybuf", [P, 4, SEG], F32)
        abuf = sb("abuf", [P, 4, SEG], F32)
        bybuf = sb("bybuf", [P, 4, SEG], F32)
        bcbuf = [sb(f"bcbuf{i}", [P, 2, SEG], F32) for i in range(2)]
        xstrip = [sb(f"xstrip{i}", [P, SEG], F32) for i in range(4)]
        ubuf = [sb(f"ubuf{i}", [P, D], F32) for i in range(2)]
        xnbuf = [sb(f"xnbuf{i}", [P, D], F32R) for i in range(2)]
        mini = sb("mini", [P, NDT, 512], F32R)
        bcstage = sb("bcstage", [32, 512], F32)
        statbuf = sb("statbuf", [P, 12], F32)
        mvbuf = sb("mvbuf", [P, 2], F32)
        rcol = sb("rcol", [P, 2], F32)
        ystage = sb("ystage", [P, SEG], F32)
        ylt = [sb("ylt0", [P, D], F32)]
        u2buf = [sb("u2buf0", [P, D], F32)]
        x2buf = sb("x2buf", [P, D], F32)
        yln = sb("yln", [P, D], F32)
        xn3 = sb("xn3", [P, D], BF16)
        xn3T = sb("xn3T", [P, NDT * P], BF16)
        e8 = sb("e8", [P, 8], F32)
        g8 = sb("g8", [P, 8], F32)
        top8 = sb("top8", [P, 8], F32)
        sm_sc = sb("sm_sc", [P, 4], F32)
        tzb = sb("tzb", [P, 32], F32R)
        loraT = [sb(f"loraT{i}", [64, P], F32R) for i in range(2)]
        st4 = sb("st4", [P, 12], F32)
        mv4 = sb("mv4", [P, 2], F32)
        epscol = sb("epscol", [P, 1], F32)
        neghalf = sb("neghalf", [P, 1], F32)
        invDc = sb("invDc", [P, 1], F32)
        vdmy = sb("vdmy", [P, 4], F32)
        sdmy = sb("sdmy", [P, 4], F32)
        if SEG >= 1024:
            stg_a = ybuf[:].rearrange("p a b -> p (a b)")[:, 0:NDT * RK]
            stg_c = ybuf[:].rearrange("p a b -> p (a b)")[:, NDT * RK:NDT * RK + NDT * 32]
            stg_b = ybuf[0:96, 1, :]
            zr = ybuf[0:64, 2, :]
            aug32 = ybuf[0:32, 3, :]
            onesrow = bcbuf[0][0:1, 0, 0:max(SEG, D)]
            stg_row = bcbuf[0][0:1, 1, 0:D]
        else:
            stg_a = sb("stg_a", [P, NDT * RK], F32)[:]
            stg_b = sb("stg_b", [96, D], F32)[:]
            stg_c = sb("stg_c", [P, NDT * 32], F32)[:]
            stg_row = sb("stg_row", [1, D], F32)[:]
            onesrow = sb("onesrow", [1, max(SEG, D)], F32)[:]
            zr = sb("zr", [64, D], F32)[:]
            aug32 = sb("aug32", [32, D], F32)[:]

        px = psum("px", [96, 512])
        pt0 = psum("pt0", [P, 512])
        pt1 = psum("pt1", [P, 512])
        pdtz = psum("pdtz", [P, 512])
        pdtz2 = psum("pdtz2", [P, 512])
        pf0 = psum("pf0", [P, 512])
        pf1 = psum("pf1", [P, 512])
        psm = psum("psm", [P, 64])

        sems = {e: st.enter_context(nc.semaphore(f"sem_{e}")) for e in SEMS}
        V, S, T, Q0, Q2 = "V", "S", "T", "q0", "q2"

        if SEG >= 1024:
            lntmp = ystage[:][:, 0:D]
        else:
            lntmp = sb("lntmp", [P, D], F32)[:]
        last_use = {}
        ltok = {}

        def guard(key):
            t = last_use.get(key)
            if t is None:
                return []
            return list(t) if isinstance(t, (list, tuple)) else [t]

        def setg(key, *toks):
            last_use[key] = list(toks)

        pr.emit(Q2, lambda g: g.memset(ident_f[:], 0.0), inc=1, sem="G")
        t_id = pr.emit(Q2, lambda g: g.affine_select(
            out=ident_f[:], in_=ident_f[:], compare_op=ALU.not_equal,
            fill=1.0, base=0, pattern=[[-1, P]], channel_multiplier=1), inc=1, sem="G")
        t_idr = pr.emit(V, lambda v: v.tensor_copy(out=ident_r[:], in_=ident_f[:]),
                        waits=[t_id])
        t_idb = pr.emit(V, lambda v: v.tensor_copy(out=ident_b[:], in_=ident_f[:]),
                        waits=[t_id])

        t_wx = pr.emit(Q0, lambda s: s.dma_start(out=stg_b, in_=wx_d))
        t_wdt = pr.emit(Q0, lambda s: s.dma_start(
            out=stg_a.rearrange("p (c r) -> p c r", c=NDT),
            in_=wdt_d.rearrange("(c p) r -> p c r", p=P)))
        t_bdt = pr.emit(Q0, lambda s: s.dma_start(out=stg_row, in_=bdt_d))
        t_alog = pr.emit(Q0, lambda s: s.dma_start(
            out=acol[:].rearrange("p (c n) -> p c n", c=NDT),
            in_=alog_d.rearrange("(c p) n -> p c n", p=P)))
        t_dpar = pr.emit(Q0, lambda s: s.dma_start(
            out=dcol[:].rearrange("p (o c) -> p o c", o=1),
            in_=dpar_d.rearrange("o (c p) -> p o c", p=P),
            allow_slow_non_contiguous=True))

        t_zrm = pr.emit(V, lambda v: v.memset(zr, 0.0))
        pr.emit(V, lambda v: v.memset(epscol[:], 1e-5))
        pr.emit(V, lambda v: v.memset(neghalf[:], -0.5))
        pr.emit(V, lambda v: v.memset(invDc[:], 1.0 / D))
        t_zr = pr.emit(V, lambda v: v.memset(onesrow, 1.0))
        t_carry0 = pr.emit(V, lambda v: v.memset(carry[:], 0.0))
        t_ae = pr.emit(S, lambda s: s.activation(out=acol[:], in_=acol[:],
                                                 func=ACT.Exp), waits=[t_alog])
        t_acol = pr.emit(S, lambda s: s.activation(out=acol[:], in_=acol[:],
                                                  func=ACT.Identity, scale=-1.0))
        pr.emit(V, lambda v: v.memset(aug32, 0.0))
        t_aug1 = pr.emit(V, lambda v: v.memset(aug32[0:1, :], 1.0))
        pr.emit(S, lambda s: s.copy(out=dt_rawT_aug[64:96, :], in_=aug32[:, 0:SEG]),
                waits=[t_aug1])
        t_aug2 = pr.emit(S, lambda s: s.copy(out=aug32[0:1, :], in_=stg_row),
                         waits=[t_bdt])
        pr.emit(S, lambda s: s.copy(out=wdtT_aug[64:96, :], in_=aug32[:, 0:NDT * P]),
                waits=[t_aug2])
        for dt in range(NDT):
            tt = pr.emit(T, lambda t, dt=dt: t.transpose(
                pt0[0:RK, 0:P], in_=stg_a[:, dt * RK:(dt + 1) * RK],
                identity=ident_f[:]), waits=[t_wdt, t_id] + guard(("pt0",)))
            te = pr.emit(S, lambda s, dt=dt: s.copy(
                out=wdtT_aug[0:RK, dt * P:(dt + 1) * P], in_=pt0[0:RK, 0:P]), waits=[tt])
            setg(("pt0",), te)
        wx_tts = []
        for c in range(NDT):
            tt = pr.emit(T, lambda t, c=c: t.transpose(
                pt1[:, 0:96], in_=stg_b[0:96, c * P:(c + 1) * P],
                identity=ident_f[0:96, 0:96]), waits=[t_wx, t_id] + guard(("pt1",)))
            te = pr.emit(S, lambda s, c=c: s.copy(out=wx_lhsT[:, c, :],
                                                  in_=pt1[:, 0:96]), waits=[tt])
            setg(("pt1",), te)
            wx_tts.append(tt)
        t_wxT = pr.tok(S)
        setg(("stg_b",), *wx_tts)
        t_wdtT = t_wxT
        for c in range(NDT):
            tl = pr.emit(Q0, lambda s, c=c: s.dma_start(
                out=stg_b[0:96, :], in_=wlin_d[c * P:c * P + 96, :]),
                waits=guard(("stg_b",)))
            tl2 = pr.emit(Q0, lambda s, c=c: s.dma_start(
                out=zr[0:32, :], in_=wlin_d[c * P + 96:(c + 1) * P, :]),
                waits=[t_zrm] + guard(("zrw",)))
            es = []
            for b in range(NDT):
                tt = pr.emit(T, lambda t, b=b: t.transpose(
                    pt0[:, 0:96], in_=stg_b[0:96, b * P:(b + 1) * P],
                    identity=ident_f[0:96, 0:96]), waits=[tl, t_id] + guard(("pt0",)))
                tt2 = pr.emit(T, lambda t, b=b: t.transpose(
                    pt1[:, 0:32], in_=zr[0:32, b * P:(b + 1) * P],
                    identity=ident_f[0:32, 0:32]), waits=[tl2, t_id] + guard(("pt1",)))
                e1 = pr.emit(S, lambda s, c=c, b=b: s.copy(
                    out=wlinT[:, b, c * P:c * P + 96], in_=pt0[:, 0:96]), waits=[tt])
                e2 = pr.emit(S, lambda s, c=c, b=b: s.copy(
                    out=wlinT[:, b, c * P + 96:(c + 1) * P], in_=pt1[:, 0:32]),
                    waits=[tt2])
                setg(("pt0",), e1)
                setg(("pt1",), e2)
                es += [e1, e2]
            setg(("stg_b",), *es)
            setg(("zrw",), *es)
        t_gw = pr.emit(Q0, lambda s: s.dma_start(out=stg_b[0:8, :], in_=gw_d),
                       waits=guard(("stg_b",)))
        for c in range(NDT):
            tt = pr.emit(T, lambda t, c=c: t.transpose(
                pt0[:, 0:8], in_=stg_b[0:8, c * P:(c + 1) * P],
                identity=ident_f[0:8, 0:8]), waits=[t_gw, t_id] + guard(("pt0",)))
            te = pr.emit(S, lambda s, c=c: s.copy(out=glT[:, c, 0:8],
                                                  in_=pt0[:, 0:8]), waits=[tt])
            setg(("pt0",), te)
        t_lfs = []
        for e_ in range(E):
            t_lfs.append(pr.emit(Q0, lambda s, e_=e_: s.dma_start(
                out=stg_c.rearrange("p (c e r) -> p c e r", c=NDT, e=E)[:, :, e_, :],
                in_=left_d[e_].rearrange("(c p) r -> p c r", p=P))))
        t_lf = t_lfs[-1]
        t_glT = pr.emit(V, lambda v: v.tensor_copy(
            out=glT[:, :, 8:40],
            in_=stg_c.rearrange("p (c k) -> p c k", c=NDT)), waits=t_lfs)
        t_rg = pr.emit(Q0, lambda s: s.dma_start(out=stg_b[0:32, :], in_=right_d),
                       waits=[t_gw, pr.tok(T)])
        t_bl = pr.emit(Q0, lambda s: s.dma_start(out=stg_row, in_=blin_d),
                       waits=[pr.tok(S)])
        pr.emit(S, lambda s: s.copy(out=right_aug[0:32, :], in_=stg_b[0:32, :]),
                waits=[t_rg])
        t_aug3 = pr.emit(S, lambda s: s.copy(out=aug32[0:1, :], in_=stg_row),
                         waits=[t_bl])
        pr.emit(S, lambda s: s.copy(out=right_aug[32:64, :], in_=aug32[:, 0:D]),
                waits=[t_aug3])
        t_raug = pr.tok(S)
        t_aug4 = pr.emit(S, lambda s: s.copy(out=aug32[0:1, :], in_=onesrow[0:1, 0:D]),
                         waits=[t_zr])
        for i in range(2):
            pr.emit(S, lambda s, i=i: s.copy(out=loraT[i][32:64, :],
                                             in_=aug32[:, 0:P]), waits=[t_aug4])
        t_lorainit = pr.tok(S)

        t_initdone = [pr.tok("S"), pr.tok("T"), pr.tok("V"), pr.tok("q0"), pr.tok("q2"), pr.tok("G")]
        for k in range(4):
            setg(("ybuf", k), *t_initdone)
        setg(("bc", 0), *t_initdone)
        setg(("bc", 1), *t_initdone)
        prev_carry_S = None
        post_state = [None, None]

        def emit_post_A1(hp, lt):
            sl = hp * SEG + lt * P
            yb = ylt[0]
            ub2 = u2buf[0]
            tyl = pr.emit(Q0, lambda s, sl=sl, yb=yb: s.dma_start(
                out=yb[:], in_=ylp_d[sl:sl + P, :]),
                waits=[ltok[("ydma", hp, dt)] for dt in range(NDT)]
                      + guard(("ylt", 0)))
            tu2 = pr.emit(Q0, lambda s, sl=sl, ub2=ub2: s.dma_start(
                out=ub2[:], in_=u_d[sl:sl + P, :]), waits=guard(("u2", 0)))
            tsq2 = pr.emit(S, lambda s, yb=yb: s.activation(
                out=lntmp, in_=yb[:], func=ACT.Square,
                accum_out=statbuf[:, 1:2]),
                waits=[tyl] + guard(("lntmp",)) + guard(("lntmpR",)) + guard(("yst",)) + guard(("stat",)))
            tid2 = pr.emit(S, lambda s, yb=yb: s.activation(
                out=lntmp, in_=yb[:], func=ACT.Identity,
                accum_out=statbuf[:, 0:1]))
            ltok[("red2", 1)] = tid2
            setg(("lntmpR",), tid2)
            pr.emit(S, lambda s: s.activation(
                out=statbuf[:, 2:3], in_=statbuf[:, 0:1], func=ACT.Square,
                scale=invDc[:, 0:1]))
            pr.emit(S, lambda s: s.activation(
                out=statbuf[:, 3:4], in_=statbuf[:, 2:3], func=ACT.Identity,
                scale=-1.0))
            pr.emit(S, lambda s: s.activation(
                out=mvbuf[:, 0:1], in_=statbuf[:, 0:1], func=ACT.Identity,
                scale=invDc[:, 0:1]))
            tsf2 = pr.emit(S, lambda s: s.copy(out=sdmy[:, 2:4], in_=statbuf[:, 2:4]))
            pr.selfwait(S, tsf2)
            tvar2 = pr.emit(S, lambda s: s.activation(
                out=mvbuf[:, 1:2], in_=statbuf[:, 1:2], func=ACT.Identity,
                scale=invDc[:, 0:1], bias=statbuf[:, 3:4]))
            pr.emit(S, lambda s: s.activation(out=rcol[:, 0:1], in_=mvbuf[:, 1:2],
                                              func=ACT.Ln, bias=epscol[:, 0:1]))
            pr.emit(S, lambda s: s.activation(out=rcol[:, 0:1], in_=rcol[:, 0:1],
                                              func=ACT.Exp, scale=neghalf[:, 0:1]))
            tng2 = pr.emit(S, lambda s: s.activation(
                out=rcol[:, 1:2], in_=mvbuf[:, 0:1], func=ACT.Identity, scale=-1.0))
            setg(("mv",), tng2)
            tfl2 = pr.emit(S, lambda s: s.copy(out=sdmy[:, 0:2], in_=rcol[:, 0:2]))
            pr.selfwait(S, tfl2)
            ta2 = pr.emit(S, lambda s, yb=yb: s.activation(
                out=lntmp, in_=yb[:], func=ACT.Identity, bias=rcol[:, 1:2]),
                waits=[ltok[("red2", 1)]] + guard(("lntmp",)) + guard(("lntmpR",)) + guard(("yst",)))
            t6 = pr.emit(S, lambda s: s.activation(
                out=yln[:], in_=lntmp, func=ACT.Identity, scale=rcol[:, 0:1]),
                waits=guard(("yln",)))
            setg(("lntmp",), t6)
            setg(("ylt", 0), t6)
            return tu2, t6

        def emit_post_A2(hp, lt, tu2, t6):
            sl = hp * SEG + lt * P
            ub2 = u2buf[0]
            t7 = pr.emit(V, lambda v, ub2=ub2: v.tensor_tensor(
                out=x2buf[:], in0=ub2[:], in1=yln[:], op=ALU.add),
                waits=[tu2, t6] + guard(("x2",)))
            setg(("u2", 0), t7)
            setg(("yln",), t7)
            tsq3 = pr.emit(S, lambda s: s.activation(
                out=lntmp, in_=x2buf[:], func=ACT.Square,
                accum_out=statbuf[:, 1:2]),
                waits=[t7] + guard(("lntmp",)) + guard(("lntmpR",)) + guard(("yst",)))
            tid3 = pr.emit(S, lambda s: s.activation(
                out=lntmp, in_=x2buf[:], func=ACT.Identity,
                accum_out=statbuf[:, 0:1]))
            tr2c = tid3
            setg(("lntmpR",), tid3)
            pr.emit(S, lambda s: s.activation(
                out=statbuf[:, 2:3], in_=statbuf[:, 0:1], func=ACT.Square,
                scale=invDc[:, 0:1]))
            pr.emit(S, lambda s: s.activation(
                out=statbuf[:, 3:4], in_=statbuf[:, 2:3], func=ACT.Identity,
                scale=-1.0))
            pr.emit(S, lambda s: s.activation(
                out=mvbuf[:, 0:1], in_=statbuf[:, 0:1], func=ACT.Identity,
                scale=invDc[:, 0:1]))
            tsf3 = pr.emit(S, lambda s: s.copy(out=sdmy[:, 2:4], in_=statbuf[:, 2:4]))
            pr.selfwait(S, tsf3)
            tvar3 = pr.emit(S, lambda s: s.activation(
                out=mvbuf[:, 1:2], in_=statbuf[:, 1:2], func=ACT.Identity,
                scale=invDc[:, 0:1], bias=statbuf[:, 3:4]))
            setg(("stat",), tvar3)
            pr.emit(S, lambda s: s.activation(out=rcol[:, 0:1], in_=mvbuf[:, 1:2],
                                              func=ACT.Ln, bias=epscol[:, 0:1]))
            pr.emit(S, lambda s: s.activation(out=rcol[:, 0:1], in_=rcol[:, 0:1],
                                              func=ACT.Exp, scale=neghalf[:, 0:1]))
            tng3 = pr.emit(S, lambda s: s.activation(
                out=rcol[:, 1:2], in_=mvbuf[:, 0:1], func=ACT.Identity, scale=-1.0))
            setg(("mv",), tng3)
            tfl3 = pr.emit(S, lambda s: s.copy(out=sdmy[:, 0:2], in_=rcol[:, 0:2]))
            pr.selfwait(S, tfl3)
            ta3 = pr.emit(S, lambda s: s.activation(
                out=lntmp, in_=x2buf[:], func=ACT.Identity, bias=rcol[:, 1:2]),
                waits=[tr2c] + guard(("lntmp",)) + guard(("lntmpR",)) + guard(("yst",)))
            td_ = pr.emit(S, lambda s: s.activation(
                out=xn3[:], in_=lntmp, func=ACT.Identity, scale=rcol[:, 0:1]),
                waits=guard(("xn3",)))
            setg(("lntmp",), td_)
            tes = []
            tts = []
            for half in range(2):
                pb, pk = ((pt0, "pt0"), (pt1, "pt1"))[half]
                tths = []
                for q in range(4):
                    c = half * 4 + q
                    w = [td_, t_idb] + (guard((pk,)) if q == 0 else [])
                    tt = pr.emit(T, lambda t, c=c, q=q, pb=pb: t.transpose(
                        pb[:].bitcast(BF16)[:, q * P:(q + 1) * P],
                        in_=xn3[:, c * P:(c + 1) * P],
                        identity=ident_b[:]), waits=w)
                    tths.append(tt)
                    tts.append(tt)
                te = pr.emit(S, lambda s, half=half, pb=pb: s.copy(
                    out=xn3T[:, half * 4 * P:(half + 1) * 4 * P],
                    in_=pb[:].bitcast(BF16)[:, 0:4 * P]),
                    waits=[tths[-1]] + guard(("xn3T",)))
                setg((pk,), te)
                tes += [te] * 4
            setg(("xn3",), *tts)
            for c in range(NDT):
                w0 = [tes[c], t_wxT]
                if c == 0:
                    w0 += guard(("pf",)) + [t_glT]
                stp = False
                tm1 = pr.emit(T, lambda t, c=c, stp=stp: t.matmul(
                    pf0[:], lhsT=xn3T[:, c * P:(c + 1) * P], rhs=wlinT[:, c, 0:512],
                    start=(c == 0), stop=stp), waits=w0)
                pr.emit(T, lambda t, c=c, stp=stp: t.matmul(
                    pf1[:], lhsT=xn3T[:, c * P:(c + 1) * P],
                    rhs=wlinT[:, c, 512:1024],
                    start=(c == 0), stop=stp))
                if True:
                    w2 = guard(("psm",)) if c == 0 else []
                    pr.emit(T, lambda t, c=c: t.matmul(
                        psm[:, 0:40], lhsT=xn3T[:, c * P:(c + 1) * P], rhs=glT[:, c, :],
                        start=(c == 0), stop=(c == NDT - 1)), waits=w2)
            setg(("xn3T",), pr.tok(T))
            t_mm = pr.tok(T)
            return t7, t_mm


        def emit_post_B(hp, lt, t7, t_mm):
            sl = hp * SEG + lt * P
            tsm1 = pr.emit(V, lambda v: v.reduce_max(
                out=sm_sc[:, 0:1], in_=psm[:, 0:8], axis=AXL.X, negate=True),
                waits=[t_mm])
            for _pi in range(3):
                pr.emit(V, lambda v, _pi=_pi: v.memset(vdmy[:, _pi:_pi+1], 0.0))
            tsm1f = pr.emit(V, lambda v: v.tensor_copy(out=vdmy[:, 0:1],
                                                       in_=sm_sc[:, 0:1]))
            tsm2 = pr.emit(S, lambda s: s.activation(
                out=e8[:], in_=psm[:, 0:8], func=ACT.Exp, bias=sm_sc[:, 0:1]),
                waits=[tsm1f] + guard(("e8",)))
            tsm3 = pr.emit(V, lambda v: v.reduce_sum(
                out=sm_sc[:, 1:2], in_=e8[:], axis=AXL.X), waits=[tsm2])
            tsm5 = pr.emit(V, lambda v: v.max(out=top8[:], in_=e8[:]))
            for _pi in range(3):
                pr.emit(V, lambda v, _pi=_pi: v.memset(vdmy[:, _pi:_pi+1], 0.0))
            pr.emit(V, lambda v: v.reciprocal(out=sm_sc[:, 2:3], in_=sm_sc[:, 1:2]))
            pr.emit(V, lambda v: v.tensor_copy(out=vdmy[:, 0:2], in_=sm_sc[:, 1:3]))
            pr.emit(V, lambda v: v.tensor_copy(out=vdmy[:, 2:3], in_=top8[:, 1:2]))
            pr.selfwait(V)
            tsm6 = pr.emit(V, lambda v: v.scalar_tensor_tensor(
                out=g8[:], in0=e8[:], scalar=top8[:, 1:2], in1=e8[:],
                op0=ALU.is_ge, op1=ALU.mult))
            tz_t = pr.emit(V, lambda v: v.scalar_tensor_tensor(
                out=tzb[:], in0=psm[:, 8:40], scalar=sm_sc[:, 2:3],
                in1=g8[:].rearrange("p (e o) -> p e o", o=1).to_broadcast([P, 8, 4]),
                op0=ALU.mult, op1=ALU.mult), waits=guard(("tzb",)))
            setg(("e8",), tz_t)
            setg(("psm",), tz_t, tsm2)
            lb = loraT[lt % 2]
            tt = pr.emit(T, lambda t: t.transpose(
                pt0[:].bitcast(F32R)[0:32, 0:P], in_=tzb[:], identity=ident_r[:]),
                waits=[tz_t, t_idr] + guard(("pt0",)))
            te = pr.emit(S, lambda s, lb=lb: s.copy(
                out=lb[0:32, :], in_=pt0[:].bitcast(F32R)[0:32, 0:P]),
                waits=[tt, t_lorainit] + guard(("lora", lt % 2)))
            setg(("pt0",), te)
            setg(("tzb",), tt)
            tl1 = pr.emit(T, lambda t, lb=lb: t.matmul(
                pf0[:], lhsT=lb[:], rhs=right_aug[:, 0:512], start=False, stop=True),
                waits=[te, t_raug])
            tl2 = pr.emit(T, lambda t, lb=lb: t.matmul(
                pf1[:], lhsT=lb[:], rhs=right_aug[:, 512:1024], start=False,
                stop=True))
            setg(("lora", lt % 2), tl2)
            t_ffn = pr.tok(T)
            tsq4a = pr.emit(S, lambda s: s.activation(
                out=lntmp[:, 0:512], in_=pf0[:], func=ACT.Square,
                accum_out=st4[:, 0:1]),
                waits=[t_ffn] + guard(("lntmp",)) + guard(("lntmpR",)) + guard(("yst",)) + [t7])
            tsq4b = pr.emit(S, lambda s: s.activation(
                out=lntmp[:, 512:1024], in_=pf1[:], func=ACT.Square,
                accum_out=st4[:, 1:2]))
            tid4a = pr.emit(S, lambda s: s.activation(
                out=lntmp[:, 0:512], in_=pf0[:], func=ACT.Identity,
                accum_out=st4[:, 7:8]))
            tid4b = pr.emit(S, lambda s: s.activation(
                out=lntmp[:, 512:1024], in_=pf1[:], func=ACT.Identity,
                accum_out=st4[:, 8:9]))
            tr2d = tid4b
            setg(("lntmpR",), tid4b)
            tsfa = pr.emit(S, lambda s: s.copy(out=sdmy[:, 0:1], in_=st4[:, 8:9]))
            pr.selfwait(S, tsfa)
            pr.emit(S, lambda s: s.activation(
                out=st4[:, 2:3], in_=st4[:, 0:1], func=ACT.Identity,
                bias=st4[:, 1:2]))
            pr.emit(S, lambda s: s.activation(
                out=st4[:, 3:4], in_=st4[:, 7:8], func=ACT.Identity,
                bias=st4[:, 8:9]))
            pr.emit(S, lambda s: s.activation(
                out=st4[:, 4:5], in_=st4[:, 3:4], func=ACT.Square,
                scale=invDc[:, 0:1]))
            pr.emit(S, lambda s: s.activation(
                out=st4[:, 6:7], in_=st4[:, 4:5], func=ACT.Identity,
                scale=-1.0))
            pr.emit(S, lambda s: s.activation(
                out=mv4[:, 0:1], in_=st4[:, 3:4], func=ACT.Identity,
                scale=invDc[:, 0:1]))
            tsf4 = pr.emit(S, lambda s: s.copy(out=sdmy[:, 2:3], in_=st4[:, 6:7]))
            pr.selfwait(S, tsf4)
            tvar4 = pr.emit(S, lambda s: s.activation(
                out=mv4[:, 1:2], in_=st4[:, 2:3], func=ACT.Identity,
                scale=invDc[:, 0:1], bias=st4[:, 6:7]))
            pr.emit(S, lambda s: s.activation(out=rcol[:, 0:1], in_=mv4[:, 1:2],
                                              func=ACT.Ln, bias=epscol[:, 0:1]))
            pr.emit(S, lambda s: s.activation(out=rcol[:, 0:1], in_=rcol[:, 0:1],
                                              func=ACT.Exp, scale=neghalf[:, 0:1]))
            tng4 = pr.emit(S, lambda s: s.activation(
                out=rcol[:, 1:2], in_=mv4[:, 0:1], func=ACT.Identity, scale=-1.0))
            setg(("mv4",), tng4)
            tfl4 = pr.emit(S, lambda s: s.copy(out=sdmy[:, 0:2], in_=rcol[:, 0:2]))
            pr.selfwait(S, tfl4)
            ta4a = pr.emit(S, lambda s: s.activation(
                out=lntmp[:, 0:512], in_=pf0[:], func=ACT.Identity, bias=rcol[:, 1:2]),
                waits=[tr2d] + guard(("lntmp",)) + guard(("lntmpR",)) + guard(("yst",)))
            ta4b = pr.emit(S, lambda s: s.activation(
                out=lntmp[:, 512:1024], in_=pf1[:], func=ACT.Identity, bias=rcol[:, 1:2]))
            to1 = pr.emit(S, lambda s: s.activation(
                out=yln[:, 0:512], in_=lntmp[:, 0:512], func=ACT.Identity,
                scale=rcol[:, 0:1]), waits=guard(("obuf",)))
            to2 = pr.emit(S, lambda s: s.activation(
                out=yln[:, 512:1024], in_=lntmp[:, 512:1024], func=ACT.Identity,
                scale=rcol[:, 0:1]))
            setg(("lntmp",), to2)
            setg(("pf",), ta4b)
            tfin = pr.emit(V, lambda v: v.tensor_tensor(
                out=yln[:], in0=yln[:], in1=x2buf[:], op=ALU.add), waits=[to2])
            setg(("x2",), tfin)
            tod = pr.emit(Q0, lambda s, sl=sl: s.dma_start(
                out=out_d[sl:sl + P, :], in_=yln[:]), waits=[tfin])
            setg(("yln",), tod)
            setg(("obuf",), tod)

        for h in range(NH):
            lbase = h * SEG
            for sub in range(NSUB):
                for i in range(4):
                    lt = sub * 4 + i
                    sl = lbase + lt * P
                    ub = ubuf[lt % 2]
                    xb = xnbuf[lt % 2]
                    tu = pr.emit(Q0, lambda s, sl=sl, ub=ub: s.dma_start(
                        out=ub[:], in_=u_d[sl:sl + P, :]), waits=guard(("ub", lt % 2)))
                    tsq = pr.emit(S, lambda s, ub=ub: s.activation(
                        out=lntmp, in_=ub[:], func=ACT.Square,
                        accum_out=statbuf[:, 1:2]),
                        waits=[tu] + guard(("lntmp",)) + guard(("lntmpR",)) + guard(("yst",)) + guard(("stat",)))
                    tid = pr.emit(S, lambda s, ub=ub: s.activation(
                        out=lntmp, in_=ub[:], func=ACT.Identity,
                        accum_out=statbuf[:, 0:1]))
                    ltok[("red2", 0)] = tid
                    setg(("lntmpR",), tid)
                    pr.emit(S, lambda s: s.activation(
                        out=statbuf[:, 2:3], in_=statbuf[:, 0:1], func=ACT.Square,
                        scale=invDc[:, 0:1]))
                    pr.emit(S, lambda s: s.activation(
                        out=statbuf[:, 3:4], in_=statbuf[:, 2:3], func=ACT.Identity,
                        scale=-1.0))
                    pr.emit(S, lambda s: s.activation(
                        out=mvbuf[:, 0:1], in_=statbuf[:, 0:1], func=ACT.Identity,
                        scale=invDc[:, 0:1]))
                    tsf0 = pr.emit(S, lambda s: s.copy(out=sdmy[:, 2:4], in_=statbuf[:, 2:4]))
                    pr.selfwait(S, tsf0)
                    tvar = pr.emit(S, lambda s: s.activation(
                        out=mvbuf[:, 1:2], in_=statbuf[:, 1:2], func=ACT.Identity,
                        scale=invDc[:, 0:1], bias=statbuf[:, 3:4]))
                    tln1 = pr.emit(S, lambda s: s.activation(
                        out=rcol[:, 0:1], in_=mvbuf[:, 1:2], func=ACT.Ln, bias=epscol[:, 0:1]))
                    pr.emit(S, lambda s: s.activation(
                        out=rcol[:, 0:1], in_=rcol[:, 0:1], func=ACT.Exp, scale=neghalf[:, 0:1]))
                    tng = pr.emit(S, lambda s: s.activation(
                        out=rcol[:, 1:2], in_=mvbuf[:, 0:1], func=ACT.Identity, scale=-1.0))
                    setg(("mv",), tng)
                    tfl = pr.emit(S, lambda s: s.copy(out=sdmy[:, 0:2], in_=rcol[:, 0:2]))
                    pr.selfwait(S, tfl)
                    if dbg and h == 0:
                        pr.emit(S, lambda s, lt=lt: s.copy(
                            out=statbuf[:, 4:6], in_=rcol[:, 0:2]))
                        pr.emit(S, lambda s, lt=lt: s.copy(
                            out=statbuf[:, 6:8], in_=mvbuf[:, 0:2]))
                        pr.emit(S, lambda s, lt=lt: s.copy(
                            out=statbuf[:, 8:10], in_=statbuf[:, 1:3]))
                        pr.emit(Q0, lambda s, lt=lt: s.dma_start(
                            out=dbgscal_d[:, lt * 6:lt * 6 + 6], in_=statbuf[:, 4:10]),
                            waits=[pr.tok("S")])
                    if skip_ln:
                        txn = pr.emit(S, lambda s, ub=ub, xb=xb: s.copy(
                            out=xb[:], in_=ub[:]),
                            waits=[ltok[("red2", 0)]] + guard(("xb", lt % 2)))
                    else:
                        ta1 = pr.emit(S, lambda s, ub=ub: s.activation(
                            out=lntmp, in_=ub[:], func=ACT.Identity, bias=rcol[:, 1:2]),
                            waits=[ltok[("red2", 0)]] + guard(("lntmp",)) + guard(("lntmpR",)) + guard(("yst",)))
                        txn = pr.emit(S, lambda s, xb=xb: s.activation(
                            out=xb[:], in_=lntmp, func=ACT.Identity, scale=rcol[:, 0:1]),
                            waits=guard(("xb", lt % 2)))
                        setg(("lntmp",), txn)
                    setg(("ub", lt % 2), txn)
                    setg(("stat",), tvar)
                    tts = []
                    for half in range(2):
                        pb, pk = ((pt0, "pt0"), (pt1, "pt1"))[half]
                        tths = []
                        for q in range(4):
                            c = half * 4 + q
                            w = [txn, t_idr] + (guard((pk,)) if q == 0 else [])
                            tt = pr.emit(T, lambda t, xb=xb, c=c, q=q, pb=pb: t.transpose(
                                pb[:].bitcast(F32R)[:, q * P:(q + 1) * P],
                                in_=xb[:, c * P:(c + 1) * P],
                                identity=ident_r[:]), waits=w)
                            tths.append(tt)
                            tts.append(tt)
                        gw = []
                        for q in range(4):
                            gw += guard(("mini", half * 4 + q))
                        te = pr.emit(S, lambda s, half=half, i=i, pb=pb: s.copy(
                            out=mini[:, half * 4:(half + 1) * 4, i * P:(i + 1) * P],
                            in_=pb[:].bitcast(F32R)[:, 0:4 * P]
                            .rearrange("p (c q) -> p c q", q=P)),
                            waits=[tths[-1]] + gw)
                        setg((pk,), te)
                        for q in range(4):
                            ltok[("minie", half * 4 + q, i)] = te
                    setg(("xb", lt % 2), *tts)
                for c in range(NDT):
                    wts = [ltok[("minie", c, i)] for i in range(4)] + [t_wxT]
                    if c == 0:
                        wts += guard(("px",))
                    tm = pr.emit(T, lambda t, c=c: t.matmul(
                        px[:], lhsT=wx_lhsT[:, c, :], rhs=mini[:, c, :],
                        start=(c == 0), stop=(c == NDT - 1)), waits=wts)
                    tsp = pr.emit(Q2, lambda s, c=c, sub=sub, lbase=lbase: s.dma_start(
                        out=xnt_d[c, :, lbase + sub * 512:lbase + (sub + 1) * 512],
                        in_=mini[:, c, :].bitcast(F32)),
                        waits=[ltok[("minie", c, i)] for i in range(4)])
                    setg(("mini", c), tsp, tm)
                t_px = pr.tok(T)
                te1 = pr.emit(S, lambda s, sub=sub: s.copy(
                    out=dt_rawT_aug[0:64, sub * 512:(sub + 1) * 512],
                    in_=px[0:64, :]), waits=[t_px] + guard(("draw", sub)))
                te2 = pr.emit(S, lambda s: s.copy(out=bcstage[:], in_=px[64:96, :]),
                              waits=[t_px] + guard(("bcstage",)))
                pr.emit(Q2, lambda s, sub=sub, lbase=lbase: s.dma_start(
                    out=bc_d[0, :, lbase + sub * 512:lbase + (sub + 1) * 512],
                    in_=bcstage[0:16, :]), waits=[te2])
                tbc = pr.emit(Q2, lambda s, sub=sub, lbase=lbase: s.dma_start(
                    out=bc_d[1, :, lbase + sub * 512:lbase + (sub + 1) * 512],
                    in_=bcstage[16:32, :]))
                setg(("bcstage",), tbc)
                setg(("px",), te1, te2)
                ltok[("draw", sub)] = te1
                ltok[("bcd", sub)] = tbc

            def prepassA(dg):
                outa = []
                for k, dt in enumerate(dg):
                    xs = xstrip[k]
                    tx = pr.emit(Q0, lambda s, dt=dt, xs=xs, lbase=lbase: s.dma_start(
                        out=xs[:], in_=xnt_d[dt, :, lbase:lbase + SEG]),
                        waits=[ltok[("bcd", sub)] for sub in range(NSUB)]
                              + guard(("xs", k)))
                    for sub in range(NSUB):
                        bix = (k * NSUB + sub) % 2
                        pzb = (pdtz, pdtz2)[bix]
                        tm = pr.emit(T, lambda t, dt=dt, sub=sub, pzb=pzb: t.matmul(
                            pzb[:], lhsT=wdtT_aug[:, dt * P:(dt + 1) * P],
                            rhs=dt_rawT_aug[:, sub * 512:(sub + 1) * 512],
                            start=True, stop=True),
                            waits=[t_wdtT, ltok[("draw", sub)]] + guard(("pdtz", bix)))
                        ts1 = pr.emit(S, lambda s, k=k, sub=sub, pzb=pzb: s.activation(
                            out=dtT[:, k, sub * 512:(sub + 1) * 512], in_=pzb[:],
                            func=ACT.Exp), waits=[tm] + guard(("dtTr", k)))
                        pr.emit(S, lambda s, k=k, sub=sub: s.activation(
                            out=dtT[:, k, sub * 512:(sub + 1) * 512],
                            in_=dtT[:, k, sub * 512:(sub + 1) * 512],
                            func=ACT.Ln, bias=1.0))
                        setg(("pdtz", bix), ts1)
                        ltok[("drawuse", sub)] = tm
                    t_dtT = pr.tok(S)
                    tw = pr.emit(V, lambda v, k=k, xs=xs: v.tensor_tensor(
                        out=wT[:, k, :], in0=dtT[:, k, :], in1=xs[:], op=ALU.mult),
                        waits=[tx, t_dtT])
                    setg(("dtTr", k), tw)
                    ltok[("w", k)] = tw
                    outa.append((k, dt, xs, tx))
                setg(("draw", 0), *[ltok[("drawuse", s_)] for s_ in range(NSUB)])
                if NSUB > 1:
                    setg(("draw", 1), *[ltok[("drawuse", s_)] for s_ in range(NSUB)])
                return outa

            def prepassB(outa):
                for k, dt, xs, tx in outa:
                    ty0 = pr.emit(V, lambda v, k=k, dt=dt, xs=xs: v.tensor_scalar(
                        out=ybuf[:, k, :], in0=xs[:], scalar1=dcol[:, dt:dt + 1],
                        scalar2=None, op0=ALU.mult),
                        waits=[t_dpar, tx] + guard(("ybuf", k)))
                    setg(("xs", k), ty0)
                    ltok[("y0", k)] = ty0

            def nloop(gi, dg):
                nonlocal prev_carry_S
                prev_carry_local = None
                for n in range(N):
                    bb = bcbuf[n % 2]
                    tb1 = pr.emit(Q2 if n % 2 == 0 else Q0,
                        lambda s, n=n, bb=bb, lbase=lbase: s.dma_start(
                        out=bb[:], in_=bc_d[:, n:n + 1, lbase:lbase + SEG]
                        .rearrange("bc o c -> o bc c").to_broadcast([P, 2, SEG])),
                        waits=[ltok[("bcd", sub)] for sub in range(NSUB)]
                              + guard(("bc", n % 2)))
                    tb2 = tb1
                    ta = []
                    for k, dt in enumerate(dg):
                        ta.append(pr.emit(S, lambda s, k=k, dt=dt, n=n: s.activation(
                            out=abuf[:, k, :], in_=dtT[:, k, :], func=ACT.Exp,
                            scale=acol[:, dt * N + n:dt * N + n + 1]),
                            waits=[t_acol] + guard(("abuf", k))))
                    tbv = pr.emit(V, lambda v, bb=bb: v.tensor_tensor(
                        out=bybuf[:], in0=wT[:],
                        in1=bb[:, 0:1, :].to_broadcast([P, 4, SEG]), op=ALU.mult),
                        waits=[tb1, ltok[("w", 0)], ltok[("w", 3)]])
                    if n == 0 and h == 0 and gi == 0:
                        pr.selfwait(V, t_carry0)
                    if prev_carry_S is not None:
                        pr.selfwait(V, prev_carry_S)
                    tscan = []
                    for k, dt in enumerate(dg):
                        cix = n * NDT + dt
                        tscan.append(pr.emit(V, lambda v, k=k, cix=cix: v.tensor_tensor_scan(
                            out=abuf[:, k, :], data0=abuf[:, k, :], data1=bybuf[:, k, :],
                            initial=carry[:, cix:cix + 1], op0=ALU.mult, op1=ALU.add),
                            waits=[ta[-1]]))
                    t_scans = pr.tok(V)
                    c0 = n * NDT + dg[0]
                    tcs = pr.emit(S, lambda s, c0=c0: s.copy(
                        out=carry[:, c0:c0 + 4],
                        in_=abuf[:, :, SEG - 1:SEG].rearrange("p k o -> p (k o)")),
                        waits=tscan)
                    prev_carry_local = tcs
                    tyk = pr.emit(V, lambda v, bb=bb: v.tensor_tensor(
                        out=bybuf[:], in0=abuf[:],
                        in1=bb[:, 1:2, :].to_broadcast([P, 4, SEG]), op=ALU.mult),
                        waits=[tb2])
                    setg(("bc", n % 2), tyk)
                    for k in range(4):
                        setg(("abuf", k), tyk, tcs)
                    tya = pr.emit(V, lambda v: v.tensor_tensor(
                        out=ybuf[:], in0=ybuf[:], in1=bybuf[:], op=ALU.add),
                        waits=[ltok[("y0", 0)], ltok[("y0", 3)]])
                    j = gi * N + n
                    if h > 0 and j % 4 == 1:
                        post_state[0] = emit_post_A1(h - 1, j // 4)
                    elif h > 0 and j % 4 == 2:
                        post_state[1] = emit_post_A2(h - 1, j // 4, *post_state[0])
                    elif h > 0 and j % 4 == 3:
                        emit_post_B(h - 1, j // 4, *post_state[1])
                prev_carry_S = prev_carry_local
                for k in range(4):
                    setg(("dtTr", k), pr.tok(S), pr.tok(V))

            def yout(dg):
                t_y_done = pr.tok(V)
                for k, dt in enumerate(dg):
                    yes = []
                    tts_k = []
                    for half in range(2):
                        pb, pk = ((pt0, "pt0"), (pt1, "pt1"))[half]
                        tths = []
                        for q in range(4):
                            lt = half * 4 + q
                            w = [t_y_done, t_id] + (guard((pk,)) if q == 0 else [])
                            tt = pr.emit(T, lambda t, k=k, lt=lt, q=q, pb=pb: t.transpose(
                                pb[:, q * P:(q + 1) * P],
                                in_=ybuf[:, k, lt * P:(lt + 1) * P],
                                identity=ident_f[:]), waits=w)
                            tths.append(tt)
                            tts_k.append(tt)
                        te = pr.emit(S, lambda s, half=half, pb=pb: s.copy(
                            out=ystage[:, half * 4 * P:(half + 1) * 4 * P],
                            in_=pb[:, 0:4 * P]),
                            waits=[tths[-1]] + guard(("yst",)) + guard(("lntmp",)))
                        setg((pk,), te)
                        yes.append(te)
                    ty = pr.emit(Q2, lambda s, dt=dt, lbase=lbase: s.dma_start(
                        out=ylp_d[lbase:lbase + SEG, dt * P:(dt + 1) * P]
                        .rearrange("(lt p) c -> p lt c", p=P),
                        in_=ystage[:].rearrange("p (lt c) -> p lt c", c=P)),
                        waits=yes)
                    setg(("yst",), ty)
                    ltok[("ydma", h, dt)] = ty
                    setg(("ybuf", k), *tts_k)

            if not skip_scan:
                pa0 = prepassA(DGRPS[0])
                prepassB(pa0)
                nloop(0, DGRPS[0])
                pa1 = prepassA(DGRPS[1])
                yout(DGRPS[0])
                prepassB(pa1)
                nloop(1, DGRPS[1])
                yout(DGRPS[1])


        for lt in range(NLT):
            _p1 = emit_post_A1(NH - 1, lt)
            _p2 = emit_post_A2(NH - 1, lt, *_p1)
            emit_post_B(NH - 1, lt, *_p2)

        pr.selfwait(Q0)
        pr.selfwait(Q2)

        with nc.Block() as blk:
            def run_stream(name):
                def body(eng):
                    for wl, fn, inc, semn in pr.streams[name]:
                        for (pe, v) in wl:
                            eng.wait_ge(sems[pe], v)
                        if fn is not None:
                            inst = fn(eng)
                            if inc:
                                inst.then_inc(sems[semn], inc)
                return body
            blk.vector(run_stream("V"))
            blk.scalar(run_stream("S"))
            blk.tensor(run_stream("T"))
            blk.sync(run_stream("q0"))
            blk.gpsimd(run_stream("q2"))

    return nc


def prep_inputs(inp, L):
    right = np.ascontiguousarray(np.asarray(inp["L0_ffn_right"], np.float32)
                                 .reshape(E * R, D))
    base = {
        "A_log": np.ascontiguousarray(inp["L0_attn_A_log"], dtype=np.float32),
        "W_x": np.ascontiguousarray(inp["L0_attn_W_x"], dtype=np.float32),
        "W_dt": np.ascontiguousarray(inp["L0_attn_W_dt"], dtype=np.float32),
        "b_dt": np.ascontiguousarray(np.asarray(inp["L0_attn_b_dt"], np.float32)
                                     .reshape(1, D)),
        "D_param": np.ascontiguousarray(np.asarray(inp["L0_attn_D_param"], np.float32)
                                        .reshape(1, D)),
        "W_lin": np.ascontiguousarray(inp["L0_ffn_W_lin"], dtype=np.float32),
        "b_lin": np.ascontiguousarray(np.asarray(inp["L0_ffn_b_lin"], np.float32)
                                      .reshape(1, D)),
        "gate_w": np.ascontiguousarray(inp["L0_ffn_gate_w"], dtype=np.float32),
        "left": np.ascontiguousarray(inp["L0_ffn_left"], dtype=np.float32),
        "right": right,
    }
    u = np.asarray(inp["L0_attn_u"], np.float32)
    maps = []
    for b in range(u.shape[0]):
        m = dict(base)
        m["u"] = np.ascontiguousarray(u[b, :L, :])
        maps.append(m)
    return maps


def kernel(**inputs):
    from concourse.bass_utils import run_bass_kernel_spmd
    L = 4096
    nc = build_nc(L=L, SEG=1024)
    maps = prep_inputs(inputs, L)
    res = run_bass_kernel_spmd(nc, maps, core_ids=list(range(8)), trace=False)
    return np.stack([np.asarray(r["out"], np.float32) for r in res.results])


# version 06 (final)
# speedup vs baseline: 1.1415x; 1.0082x over previous
import numpy as np
from contextlib import ExitStack
import concourse.bass as bass
import concourse.mybir as mybir

F32 = mybir.dt.float32
F32R = mybir.dt.float32r
BF16 = mybir.dt.bfloat16
ALU = mybir.AluOpType
ACT = mybir.ActivationFunctionType
AXL = mybir.AxisListType

D = 1024
N = 16
RK = 64
NDT = 8
E = 8
R = 4
P = 128

ENGS = ("V", "S", "T", "q0", "q2")
SEMS = ("V", "S", "T", "q0", "q2", "G", "q1")
SEM2ENG = {"V": "V", "S": "S", "T": "T", "q0": "q0", "q2": "q2", "G": "q2", "q1": "S"}


class Prog:
    def __init__(self):
        self.streams = {e: [] for e in ENGS}
        self.cnt = {s: 0 for s in SEMS}
        self.waited = {c: {p: 0 for p in SEMS} for c in ENGS}

    def emit(self, eng, fn, waits=(), inc=None, sem=None):
        sem = sem or eng
        if inc is None:
            inc = 16 if eng in ("q0", "q2") else 1
        if eng in ("q0", "q2") and sem == eng and self.cnt[eng] > 0:
            self.selfwait(eng)
        if eng == "T" and self.cnt["T"] >= 2:
            self.selfwait("T", ("T", self.cnt["T"] - 2))
        wl = []
        for tok in waits:
            if tok is None:
                continue
            pe, v = tok
            if pe not in ("q0", "q1", "q2") and SEM2ENG[pe] == eng:
                continue
            if v > self.waited[eng][pe]:
                self.waited[eng][pe] = v
                wl.append((pe, v))
        self.cnt[sem] += inc
        self.streams[eng].append((wl, fn, inc, sem))
        return (sem, self.cnt[sem])

    def selfwait(self, eng, tok=None):
        if tok is None:
            tok = (eng, self.cnt[eng])
        pe, v = tok
        if v > self.waited[eng][pe]:
            self.waited[eng][pe] = v
            self.streams[eng].append(([(pe, v)], None, 0, eng))

    def tok(self, sem):
        return (sem, self.cnt[sem])


def build_nc(L=4096, SEG=1024, name="mk"):
    dbg = False
    skip_post = False
    skip_scan = False
    skip_ln = False
    post_level = 4
    assert L % SEG == 0 and SEG % 512 == 0
    NH = L // SEG
    NSUB = SEG // 512
    NLT = SEG // 128
    DGRPS = ((0, 1, 2, 3), (4, 5, 6, 7))

    nc = bass.Bass("TRN2", target_bir_lowering=False, name=name,
                   detect_race_conditions=False)

    u_d = nc.dram_tensor("u", [L, D], F32, kind="ExternalInput").ap()
    alog_d = nc.dram_tensor("A_log", [D, N], F32, kind="ExternalInput").ap()
    wx_d = nc.dram_tensor("W_x", [96, D], F32, kind="ExternalInput").ap()
    wdt_d = nc.dram_tensor("W_dt", [D, RK], F32, kind="ExternalInput").ap()
    bdt_d = nc.dram_tensor("b_dt", [1, D], F32, kind="ExternalInput").ap()
    dpar_d = nc.dram_tensor("D_param", [1, D], F32, kind="ExternalInput").ap()
    wlin_d = nc.dram_tensor("W_lin", [D, D], F32, kind="ExternalInput").ap()
    blin_d = nc.dram_tensor("b_lin", [1, D], F32, kind="ExternalInput").ap()
    gw_d = nc.dram_tensor("gate_w", [E, D], F32, kind="ExternalInput").ap()
    left_d = nc.dram_tensor("left", [E, D, R], F32, kind="ExternalInput").ap()
    right_d = nc.dram_tensor("right", [E * R, D], F32, kind="ExternalInput").ap()
    out_d = nc.dram_tensor("out", [L, D], F32, kind="ExternalOutput").ap()
    knd = "ExternalOutput" if dbg else "Internal"
    dbgscal_d = nc.dram_tensor("dbgscal", [P, 64], F32, kind=knd).ap()
    dbgsq_d = nc.dram_tensor("dbgsq", [P, D], F32, kind=knd).ap()
    bc_d = nc.dram_tensor("bc_scr", [2, 16, L], F32, kind=knd).ap()
    xnt_d = nc.dram_tensor("xnt_scr", [NDT, P, L], F32, kind=knd).ap()
    ylp_d = nc.dram_tensor("ylp_scr", [L, D], F32, kind=knd).ap()

    pr = Prog()
    st = ExitStack()

    def sb(nm, shape, dt):
        return st.enter_context(nc.sbuf_tensor(nm, shape, dt))

    def psum(nm, shape):
        return st.enter_context(nc.psum_tensor(nm, shape, F32))

    with st:
        ident_r = sb("ident_r", [P, P], F32R)
        ident_b = sb("ident_b", [P, P], BF16)
        ident_f = sb("ident_f", [P, P], F32)
        wx_lhsT = sb("wx_lhsT", [P, NDT, 96], F32R)
        wdtT_aug = sb("wdtT_aug", [96, NDT * P], F32R)
        wlinT = sb("wlinT", [P, NDT, D], BF16)
        glT = sb("glT", [P, NDT, 40], BF16)
        right_aug = sb("right_aug", [64, D], F32R)
        acol = sb("acol", [P, NDT * N], F32)
        dcol = sb("dcol", [P, NDT], F32)
        dt_rawT_aug = sb("dt_rawT_aug", [96, SEG], F32R)
        carry = sb("carry", [P, N * NDT], F32)
        dtT = sb("dtT", [P, 4, SEG], F32)
        wT = sb("wT", [P, 4, SEG], F32)
        ybuf = sb("ybuf", [P, 4, SEG], F32)
        abuf = sb("abuf", [P, 4, SEG], F32)
        bybuf = sb("bybuf", [P, 4, SEG], F32)
        bcbuf = [sb(f"bcbuf{i}", [P, 2, SEG], F32) for i in range(2)]
        xstrip = [sb(f"xstrip{i}", [P, SEG], F32) for i in range(4)]
        ubuf = [sb(f"ubuf{i}", [P, D], F32) for i in range(2)]
        xnbuf = [sb(f"xnbuf{i}", [P, D], F32R) for i in range(2)]
        mini = sb("mini", [P, NDT, 512], F32R)
        bcstage = sb("bcstage", [32, 512], F32)
        statbuf = sb("statbuf", [P, 12], F32)
        mvbuf = sb("mvbuf", [P, 2], F32)
        rcol = sb("rcol", [P, 2], F32)
        ystage = sb("ystage", [P, SEG], F32)
        ylt = [sb("ylt0", [P, D], F32)]
        u2buf = [sb("u2buf0", [P, D], F32)]
        x2buf = sb("x2buf", [P, D], F32)
        yln = sb("yln", [P, D], F32)
        xn3 = sb("xn3", [P, D], BF16)
        xn3T = sb("xn3T", [P, NDT * P], BF16)
        e8 = sb("e8", [P, 8], F32)
        g8 = sb("g8", [P, 8], F32)
        top8 = sb("top8", [P, 8], F32)
        sm_sc = sb("sm_sc", [P, 4], F32)
        tzb = sb("tzb", [P, 32], F32R)
        loraT = [sb(f"loraT{i}", [64, P], F32R) for i in range(2)]
        st4 = sb("st4", [P, 12], F32)
        mv4 = sb("mv4", [P, 2], F32)
        epscol = sb("epscol", [P, 1], F32)
        neghalf = sb("neghalf", [P, 1], F32)
        invDc = sb("invDc", [P, 1], F32)
        vdmy = sb("vdmy", [P, 4], F32)
        sdmy = sb("sdmy", [P, 4], F32)
        if SEG >= 1024:
            stg_a = ybuf[:].rearrange("p a b -> p (a b)")[:, 0:NDT * RK]
            stg_c = ybuf[:].rearrange("p a b -> p (a b)")[:, NDT * RK:NDT * RK + NDT * 32]
            stg_b = ybuf[0:96, 1, :]
            zr = ybuf[0:64, 2, :]
            aug32 = ybuf[0:32, 3, :]
            onesrow = bcbuf[0][0:1, 0, 0:max(SEG, D)]
            stg_row = bcbuf[0][0:1, 1, 0:D]
        else:
            stg_a = sb("stg_a", [P, NDT * RK], F32)[:]
            stg_b = sb("stg_b", [96, D], F32)[:]
            stg_c = sb("stg_c", [P, NDT * 32], F32)[:]
            stg_row = sb("stg_row", [1, D], F32)[:]
            onesrow = sb("onesrow", [1, max(SEG, D)], F32)[:]
            zr = sb("zr", [64, D], F32)[:]
            aug32 = sb("aug32", [32, D], F32)[:]

        px = psum("px", [96, 512])
        pt0 = psum("pt0", [P, 512])
        pt1 = psum("pt1", [P, 512])
        pdtz = psum("pdtz", [P, 512])
        pdtz2 = psum("pdtz2", [P, 512])
        pf0 = psum("pf0", [P, 512])
        pf1 = psum("pf1", [P, 512])
        psm = psum("psm", [P, 64])

        sems = {e: st.enter_context(nc.semaphore(f"sem_{e}")) for e in SEMS}
        V, S, T, Q0, Q2 = "V", "S", "T", "q0", "q2"

        if SEG >= 1024:
            lntmp = ystage[:][:, 0:D]
        else:
            lntmp = sb("lntmp", [P, D], F32)[:]
        last_use = {}
        ltok = {}

        def guard(key):
            t = last_use.get(key)
            if t is None:
                return []
            return list(t) if isinstance(t, (list, tuple)) else [t]

        def setg(key, *toks):
            last_use[key] = list(toks)

        pr.emit(Q2, lambda g: g.memset(ident_f[:], 0.0), inc=1, sem="G")
        t_id = pr.emit(Q2, lambda g: g.affine_select(
            out=ident_f[:], in_=ident_f[:], compare_op=ALU.not_equal,
            fill=1.0, base=0, pattern=[[-1, P]], channel_multiplier=1), inc=1, sem="G")
        t_idr = pr.emit(V, lambda v: v.tensor_copy(out=ident_r[:], in_=ident_f[:]),
                        waits=[t_id])
        t_idb = pr.emit(V, lambda v: v.tensor_copy(out=ident_b[:], in_=ident_f[:]),
                        waits=[t_id])

        t_wx = pr.emit(Q0, lambda s: s.dma_start(out=stg_b, in_=wx_d))
        t_wdt = pr.emit(Q0, lambda s: s.dma_start(
            out=stg_a.rearrange("p (c r) -> p c r", c=NDT),
            in_=wdt_d.rearrange("(c p) r -> p c r", p=P)))
        t_bdt = pr.emit(Q0, lambda s: s.dma_start(out=stg_row, in_=bdt_d))
        t_alog = pr.emit(Q0, lambda s: s.dma_start(
            out=acol[:].rearrange("p (c n) -> p c n", c=NDT),
            in_=alog_d.rearrange("(c p) n -> p c n", p=P)))
        t_dpar = pr.emit(Q0, lambda s: s.dma_start(
            out=dcol[:].rearrange("p (o c) -> p o c", o=1),
            in_=dpar_d.rearrange("o (c p) -> p o c", p=P),
            allow_slow_non_contiguous=True))

        t_zrm = pr.emit(V, lambda v: v.memset(zr, 0.0))
        pr.emit(V, lambda v: v.memset(epscol[:], 1e-5))
        pr.emit(V, lambda v: v.memset(neghalf[:], -0.5))
        pr.emit(V, lambda v: v.memset(invDc[:], 1.0 / D))
        t_zr = pr.emit(V, lambda v: v.memset(onesrow, 1.0))
        t_carry0 = pr.emit(V, lambda v: v.memset(carry[:], 0.0))
        t_ae = pr.emit(S, lambda s: s.activation(out=acol[:], in_=acol[:],
                                                 func=ACT.Exp), waits=[t_alog])
        t_acol = pr.emit(S, lambda s: s.activation(out=acol[:], in_=acol[:],
                                                  func=ACT.Identity, scale=-1.0))
        pr.emit(V, lambda v: v.memset(aug32, 0.0))
        t_aug1 = pr.emit(V, lambda v: v.memset(aug32[0:1, :], 1.0))
        pr.emit(S, lambda s: s.copy(out=dt_rawT_aug[64:96, :], in_=aug32[:, 0:SEG]),
                waits=[t_aug1])
        t_aug2 = pr.emit(S, lambda s: s.copy(out=aug32[0:1, :], in_=stg_row),
                         waits=[t_bdt])
        pr.emit(S, lambda s: s.copy(out=wdtT_aug[64:96, :], in_=aug32[:, 0:NDT * P]),
                waits=[t_aug2])
        for dt in range(NDT):
            tt = pr.emit(T, lambda t, dt=dt: t.transpose(
                pt0[0:RK, 0:P], in_=stg_a[:, dt * RK:(dt + 1) * RK],
                identity=ident_f[:]), waits=[t_wdt, t_id] + guard(("pt0",)))
            te = pr.emit(S, lambda s, dt=dt: s.copy(
                out=wdtT_aug[0:RK, dt * P:(dt + 1) * P], in_=pt0[0:RK, 0:P]), waits=[tt])
            setg(("pt0",), te)
        wx_tts = []
        for c in range(NDT):
            tt = pr.emit(T, lambda t, c=c: t.transpose(
                pt1[:, 0:96], in_=stg_b[0:96, c * P:(c + 1) * P],
                identity=ident_f[0:96, 0:96]), waits=[t_wx, t_id] + guard(("pt1",)))
            te = pr.emit(S, lambda s, c=c: s.copy(out=wx_lhsT[:, c, :],
                                                  in_=pt1[:, 0:96]), waits=[tt])
            setg(("pt1",), te)
            wx_tts.append(tt)
        t_wxT = pr.tok(S)
        setg(("stg_b",), *wx_tts)
        t_wdtT = t_wxT
        for c in range(NDT):
            tl = pr.emit(Q0, lambda s, c=c: s.dma_start(
                out=stg_b[0:96, :], in_=wlin_d[c * P:c * P + 96, :]),
                waits=guard(("stg_b",)))
            tl2 = pr.emit(Q0, lambda s, c=c: s.dma_start(
                out=zr[0:32, :], in_=wlin_d[c * P + 96:(c + 1) * P, :]),
                waits=[t_zrm] + guard(("zrw",)))
            es = []
            for b in range(NDT):
                tt = pr.emit(T, lambda t, b=b: t.transpose(
                    pt0[:, 0:96], in_=stg_b[0:96, b * P:(b + 1) * P],
                    identity=ident_f[0:96, 0:96]), waits=[tl, t_id] + guard(("pt0",)))
                tt2 = pr.emit(T, lambda t, b=b: t.transpose(
                    pt1[:, 0:32], in_=zr[0:32, b * P:(b + 1) * P],
                    identity=ident_f[0:32, 0:32]), waits=[tl2, t_id] + guard(("pt1",)))
                e1 = pr.emit(S, lambda s, c=c, b=b: s.copy(
                    out=wlinT[:, b, c * P:c * P + 96], in_=pt0[:, 0:96]), waits=[tt])
                e2 = pr.emit(S, lambda s, c=c, b=b: s.copy(
                    out=wlinT[:, b, c * P + 96:(c + 1) * P], in_=pt1[:, 0:32]),
                    waits=[tt2])
                setg(("pt0",), e1)
                setg(("pt1",), e2)
                es += [e1, e2]
            setg(("stg_b",), *es)
            setg(("zrw",), *es)
        t_gw = pr.emit(Q0, lambda s: s.dma_start(out=stg_b[0:8, :], in_=gw_d),
                       waits=guard(("stg_b",)))
        for c in range(NDT):
            tt = pr.emit(T, lambda t, c=c: t.transpose(
                pt0[:, 0:8], in_=stg_b[0:8, c * P:(c + 1) * P],
                identity=ident_f[0:8, 0:8]), waits=[t_gw, t_id] + guard(("pt0",)))
            te = pr.emit(S, lambda s, c=c: s.copy(out=glT[:, c, 0:8],
                                                  in_=pt0[:, 0:8]), waits=[tt])
            setg(("pt0",), te)
        t_lfs = []
        for e_ in range(E):
            t_lfs.append(pr.emit(Q0, lambda s, e_=e_: s.dma_start(
                out=stg_c.rearrange("p (c e r) -> p c e r", c=NDT, e=E)[:, :, e_, :],
                in_=left_d[e_].rearrange("(c p) r -> p c r", p=P))))
        t_lf = t_lfs[-1]
        t_glT = pr.emit(V, lambda v: v.tensor_copy(
            out=glT[:, :, 8:40],
            in_=stg_c.rearrange("p (c k) -> p c k", c=NDT)), waits=t_lfs)
        t_rg = pr.emit(Q0, lambda s: s.dma_start(out=stg_b[0:32, :], in_=right_d),
                       waits=[t_gw, pr.tok(T)])
        t_bl = pr.emit(Q0, lambda s: s.dma_start(out=stg_row, in_=blin_d),
                       waits=[pr.tok(S)])
        pr.emit(S, lambda s: s.copy(out=right_aug[0:32, :], in_=stg_b[0:32, :]),
                waits=[t_rg])
        t_aug3 = pr.emit(S, lambda s: s.copy(out=aug32[0:1, :], in_=stg_row),
                         waits=[t_bl])
        pr.emit(S, lambda s: s.copy(out=right_aug[32:64, :], in_=aug32[:, 0:D]),
                waits=[t_aug3])
        t_raug = pr.tok(S)
        t_aug4 = pr.emit(S, lambda s: s.copy(out=aug32[0:1, :], in_=onesrow[0:1, 0:D]),
                         waits=[t_zr])
        for i in range(2):
            pr.emit(S, lambda s, i=i: s.copy(out=loraT[i][32:64, :],
                                             in_=aug32[:, 0:P]), waits=[t_aug4])
        t_lorainit = pr.tok(S)

        t_initdone = [pr.tok("S"), pr.tok("T"), pr.tok("V"), pr.tok("q0"), pr.tok("q2"), pr.tok("G")]
        for k in range(4):
            setg(("ybuf", k), *t_initdone)
        setg(("bc", 0), *t_initdone)
        setg(("bc", 1), *t_initdone)
        prev_carry_S = None
        post_state = [None, None]

        def emit_post_A1(hp, lt):
            sl = hp * SEG + lt * P
            yb = ylt[0]
            ub2 = u2buf[0]
            tyl = pr.emit(Q0, lambda s, sl=sl, yb=yb: s.dma_start(
                out=yb[:], in_=ylp_d[sl:sl + P, :]),
                waits=[ltok[("ydma", hp, dt)] for dt in range(NDT)]
                      + guard(("ylt", 0)))
            tu2 = pr.emit(Q0, lambda s, sl=sl, ub2=ub2: s.dma_start(
                out=ub2[:], in_=u_d[sl:sl + P, :]), waits=guard(("u2", 0)))
            tsq2 = pr.emit(S, lambda s, yb=yb: s.activation(
                out=lntmp, in_=yb[:], func=ACT.Square,
                accum_out=statbuf[:, 1:2]),
                waits=[tyl] + guard(("lntmp",)) + guard(("lntmpR",)) + guard(("yst",)) + guard(("stat",)))
            tid2 = pr.emit(S, lambda s, yb=yb: s.activation(
                out=lntmp, in_=yb[:], func=ACT.Identity,
                accum_out=statbuf[:, 0:1]))
            ltok[("red2", 1)] = tid2
            setg(("lntmpR",), tid2)
            pr.emit(S, lambda s: s.activation(
                out=statbuf[:, 2:3], in_=statbuf[:, 0:1], func=ACT.Square,
                scale=invDc[:, 0:1]))
            pr.emit(S, lambda s: s.activation(
                out=statbuf[:, 3:4], in_=statbuf[:, 2:3], func=ACT.Identity,
                scale=-1.0))
            pr.emit(S, lambda s: s.activation(
                out=mvbuf[:, 0:1], in_=statbuf[:, 0:1], func=ACT.Identity,
                scale=invDc[:, 0:1]))
            tsf2 = pr.emit(S, lambda s: s.copy(out=sdmy[:, 2:4], in_=statbuf[:, 2:4]))
            pr.selfwait(S, tsf2)
            tvar2 = pr.emit(S, lambda s: s.activation(
                out=mvbuf[:, 1:2], in_=statbuf[:, 1:2], func=ACT.Identity,
                scale=invDc[:, 0:1], bias=statbuf[:, 3:4]))
            pr.emit(S, lambda s: s.activation(out=rcol[:, 0:1], in_=mvbuf[:, 1:2],
                                              func=ACT.Ln, bias=epscol[:, 0:1]))
            pr.emit(S, lambda s: s.activation(out=rcol[:, 0:1], in_=rcol[:, 0:1],
                                              func=ACT.Exp, scale=neghalf[:, 0:1]))
            tng2 = pr.emit(S, lambda s: s.activation(
                out=rcol[:, 1:2], in_=mvbuf[:, 0:1], func=ACT.Identity, scale=-1.0))
            setg(("mv",), tng2)
            tfl2 = pr.emit(S, lambda s: s.copy(out=sdmy[:, 0:2], in_=rcol[:, 0:2]))
            pr.selfwait(S, tfl2)
            ta2 = pr.emit(S, lambda s, yb=yb: s.activation(
                out=lntmp, in_=yb[:], func=ACT.Identity, bias=rcol[:, 1:2]),
                waits=[ltok[("red2", 1)]] + guard(("lntmp",)) + guard(("lntmpR",)) + guard(("yst",)))
            t6 = pr.emit(S, lambda s: s.activation(
                out=yln[:], in_=lntmp, func=ACT.Identity, scale=rcol[:, 0:1]),
                waits=guard(("yln",)))
            setg(("lntmp",), t6)
            setg(("ylt", 0), t6)
            return tu2, t6

        def emit_post_A2(hp, lt, tu2, t6):
            sl = hp * SEG + lt * P
            ub2 = u2buf[0]
            t7 = pr.emit(V, lambda v, ub2=ub2: v.tensor_tensor(
                out=x2buf[:], in0=ub2[:], in1=yln[:], op=ALU.add),
                waits=[tu2, t6] + guard(("x2",)))
            setg(("u2", 0), t7)
            setg(("yln",), t7)
            tsq3 = pr.emit(S, lambda s: s.activation(
                out=lntmp, in_=x2buf[:], func=ACT.Square,
                accum_out=statbuf[:, 1:2]),
                waits=[t7] + guard(("lntmp",)) + guard(("lntmpR",)) + guard(("yst",)))
            tid3 = pr.emit(S, lambda s: s.activation(
                out=lntmp, in_=x2buf[:], func=ACT.Identity,
                accum_out=statbuf[:, 0:1]))
            tr2c = tid3
            setg(("lntmpR",), tid3)
            pr.emit(S, lambda s: s.activation(
                out=statbuf[:, 2:3], in_=statbuf[:, 0:1], func=ACT.Square,
                scale=invDc[:, 0:1]))
            pr.emit(S, lambda s: s.activation(
                out=statbuf[:, 3:4], in_=statbuf[:, 2:3], func=ACT.Identity,
                scale=-1.0))
            pr.emit(S, lambda s: s.activation(
                out=mvbuf[:, 0:1], in_=statbuf[:, 0:1], func=ACT.Identity,
                scale=invDc[:, 0:1]))
            tsf3 = pr.emit(S, lambda s: s.copy(out=sdmy[:, 2:4], in_=statbuf[:, 2:4]))
            pr.selfwait(S, tsf3)
            tvar3 = pr.emit(S, lambda s: s.activation(
                out=mvbuf[:, 1:2], in_=statbuf[:, 1:2], func=ACT.Identity,
                scale=invDc[:, 0:1], bias=statbuf[:, 3:4]))
            setg(("stat",), tvar3)
            pr.emit(S, lambda s: s.activation(out=rcol[:, 0:1], in_=mvbuf[:, 1:2],
                                              func=ACT.Ln, bias=epscol[:, 0:1]))
            pr.emit(S, lambda s: s.activation(out=rcol[:, 0:1], in_=rcol[:, 0:1],
                                              func=ACT.Exp, scale=neghalf[:, 0:1]))
            tng3 = pr.emit(S, lambda s: s.activation(
                out=rcol[:, 1:2], in_=mvbuf[:, 0:1], func=ACT.Identity, scale=-1.0))
            setg(("mv",), tng3)
            tfl3 = pr.emit(S, lambda s: s.copy(out=sdmy[:, 0:2], in_=rcol[:, 0:2]))
            pr.selfwait(S, tfl3)
            ta3 = pr.emit(S, lambda s: s.activation(
                out=lntmp, in_=x2buf[:], func=ACT.Identity, bias=rcol[:, 1:2]),
                waits=[tr2c] + guard(("lntmp",)) + guard(("lntmpR",)) + guard(("yst",)))
            td_ = pr.emit(S, lambda s: s.activation(
                out=xn3[:], in_=lntmp, func=ACT.Identity, scale=rcol[:, 0:1]),
                waits=guard(("xn3",)))
            setg(("lntmp",), td_)
            tes = []
            tts = []
            for half in range(2):
                pb, pk = ((pt0, "pt0"), (pt1, "pt1"))[half]
                tths = []
                for q in range(4):
                    c = half * 4 + q
                    w = [td_, t_idb] + (guard((pk,)) if q == 0 else [])
                    tt = pr.emit(T, lambda t, c=c, q=q, pb=pb: t.transpose(
                        pb[:].bitcast(BF16)[:, q * P:(q + 1) * P],
                        in_=xn3[:, c * P:(c + 1) * P],
                        identity=ident_b[:]), waits=w)
                    tths.append(tt)
                    tts.append(tt)
                te = pr.emit(S, lambda s, half=half, pb=pb: s.copy(
                    out=xn3T[:, half * 4 * P:(half + 1) * 4 * P],
                    in_=pb[:].bitcast(BF16)[:, 0:4 * P]),
                    waits=[tths[-1]] + guard(("xn3T",)))
                setg((pk,), te)
                tes += [te] * 4
            setg(("xn3",), *tts)
            for c in range(NDT):
                w0 = [tes[c], t_wxT]
                if c == 0:
                    w0 += guard(("pf",)) + [t_glT]
                stp = False
                tm1 = pr.emit(T, lambda t, c=c, stp=stp: t.matmul(
                    pf0[:], lhsT=xn3T[:, c * P:(c + 1) * P], rhs=wlinT[:, c, 0:512],
                    start=(c == 0), stop=stp), waits=w0)
                pr.emit(T, lambda t, c=c, stp=stp: t.matmul(
                    pf1[:], lhsT=xn3T[:, c * P:(c + 1) * P],
                    rhs=wlinT[:, c, 512:1024],
                    start=(c == 0), stop=stp))
                if True:
                    w2 = guard(("psm",)) if c == 0 else []
                    pr.emit(T, lambda t, c=c: t.matmul(
                        psm[:, 0:40], lhsT=xn3T[:, c * P:(c + 1) * P], rhs=glT[:, c, :],
                        start=(c == 0), stop=(c == NDT - 1)), waits=w2)
            setg(("xn3T",), pr.tok(T))
            t_mm = pr.tok(T)
            return t7, t_mm


        def emit_post_B(hp, lt, t7, t_mm):
            sl = hp * SEG + lt * P
            tsm1 = pr.emit(V, lambda v: v.reduce_max(
                out=sm_sc[:, 0:1], in_=psm[:, 0:8], axis=AXL.X, negate=True),
                waits=[t_mm])
            for _pi in range(3):
                pr.emit(V, lambda v, _pi=_pi: v.memset(vdmy[:, _pi:_pi+1], 0.0))
            tsm1f = pr.emit(V, lambda v: v.tensor_copy(out=vdmy[:, 0:1],
                                                       in_=sm_sc[:, 0:1]))
            tsm2 = pr.emit(S, lambda s: s.activation(
                out=e8[:], in_=psm[:, 0:8], func=ACT.Exp, bias=sm_sc[:, 0:1]),
                waits=[tsm1f] + guard(("e8",)))
            tsm3 = pr.emit(V, lambda v: v.reduce_sum(
                out=sm_sc[:, 1:2], in_=e8[:], axis=AXL.X), waits=[tsm2])
            tsm5 = pr.emit(V, lambda v: v.max(out=top8[:], in_=e8[:]))
            for _pi in range(3):
                pr.emit(V, lambda v, _pi=_pi: v.memset(vdmy[:, _pi:_pi+1], 0.0))
            pr.emit(V, lambda v: v.reciprocal(out=sm_sc[:, 2:3], in_=sm_sc[:, 1:2]))
            pr.emit(V, lambda v: v.tensor_copy(out=vdmy[:, 0:2], in_=sm_sc[:, 1:3]))
            pr.emit(V, lambda v: v.tensor_copy(out=vdmy[:, 2:3], in_=top8[:, 1:2]))
            pr.selfwait(V)
            tsm6 = pr.emit(V, lambda v: v.scalar_tensor_tensor(
                out=g8[:], in0=e8[:], scalar=top8[:, 1:2], in1=e8[:],
                op0=ALU.is_ge, op1=ALU.mult))
            tz_t = pr.emit(V, lambda v: v.scalar_tensor_tensor(
                out=tzb[:], in0=psm[:, 8:40], scalar=sm_sc[:, 2:3],
                in1=g8[:].rearrange("p (e o) -> p e o", o=1).to_broadcast([P, 8, 4]),
                op0=ALU.mult, op1=ALU.mult), waits=guard(("tzb",)))
            setg(("e8",), tz_t)
            setg(("psm",), tz_t, tsm2)
            lb = loraT[lt % 2]
            tt = pr.emit(T, lambda t: t.transpose(
                pt0[:].bitcast(F32R)[0:32, 0:P], in_=tzb[:], identity=ident_r[:]),
                waits=[tz_t, t_idr] + guard(("pt0",)))
            te = pr.emit(S, lambda s, lb=lb: s.copy(
                out=lb[0:32, :], in_=pt0[:].bitcast(F32R)[0:32, 0:P]),
                waits=[tt, t_lorainit] + guard(("lora", lt % 2)))
            setg(("pt0",), te)
            setg(("tzb",), tt)
            tl1 = pr.emit(T, lambda t, lb=lb: t.matmul(
                pf0[:], lhsT=lb[:], rhs=right_aug[:, 0:512], start=False, stop=True),
                waits=[te, t_raug])
            tl2 = pr.emit(T, lambda t, lb=lb: t.matmul(
                pf1[:], lhsT=lb[:], rhs=right_aug[:, 512:1024], start=False,
                stop=True))
            setg(("lora", lt % 2), tl2)
            t_ffn = pr.tok(T)
            tsq4a = pr.emit(S, lambda s: s.activation(
                out=lntmp[:, 0:512], in_=pf0[:], func=ACT.Square,
                accum_out=st4[:, 0:1]),
                waits=[t_ffn] + guard(("lntmp",)) + guard(("lntmpR",)) + guard(("yst",)) + [t7])
            tsq4b = pr.emit(S, lambda s: s.activation(
                out=lntmp[:, 512:1024], in_=pf1[:], func=ACT.Square,
                accum_out=st4[:, 1:2]))
            tid4a = pr.emit(S, lambda s: s.activation(
                out=lntmp[:, 0:512], in_=pf0[:], func=ACT.Identity,
                accum_out=st4[:, 7:8]))
            tid4b = pr.emit(S, lambda s: s.activation(
                out=lntmp[:, 512:1024], in_=pf1[:], func=ACT.Identity,
                accum_out=st4[:, 8:9]))
            tr2d = tid4b
            setg(("lntmpR",), tid4b)
            tsfa = pr.emit(S, lambda s: s.copy(out=sdmy[:, 0:1], in_=st4[:, 8:9]))
            pr.selfwait(S, tsfa)
            pr.emit(S, lambda s: s.activation(
                out=st4[:, 2:3], in_=st4[:, 0:1], func=ACT.Identity,
                bias=st4[:, 1:2]))
            pr.emit(S, lambda s: s.activation(
                out=st4[:, 3:4], in_=st4[:, 7:8], func=ACT.Identity,
                bias=st4[:, 8:9]))
            pr.emit(S, lambda s: s.activation(
                out=st4[:, 4:5], in_=st4[:, 3:4], func=ACT.Square,
                scale=invDc[:, 0:1]))
            pr.emit(S, lambda s: s.activation(
                out=st4[:, 6:7], in_=st4[:, 4:5], func=ACT.Identity,
                scale=-1.0))
            pr.emit(S, lambda s: s.activation(
                out=mv4[:, 0:1], in_=st4[:, 3:4], func=ACT.Identity,
                scale=invDc[:, 0:1]))
            tsf4 = pr.emit(S, lambda s: s.copy(out=sdmy[:, 2:3], in_=st4[:, 6:7]))
            pr.selfwait(S, tsf4)
            tvar4 = pr.emit(S, lambda s: s.activation(
                out=mv4[:, 1:2], in_=st4[:, 2:3], func=ACT.Identity,
                scale=invDc[:, 0:1], bias=st4[:, 6:7]))
            pr.emit(S, lambda s: s.activation(out=rcol[:, 0:1], in_=mv4[:, 1:2],
                                              func=ACT.Ln, bias=epscol[:, 0:1]))
            pr.emit(S, lambda s: s.activation(out=rcol[:, 0:1], in_=rcol[:, 0:1],
                                              func=ACT.Exp, scale=neghalf[:, 0:1]))
            tng4 = pr.emit(S, lambda s: s.activation(
                out=rcol[:, 1:2], in_=mv4[:, 0:1], func=ACT.Identity, scale=-1.0))
            setg(("mv4",), tng4)
            tfl4 = pr.emit(S, lambda s: s.copy(out=sdmy[:, 0:2], in_=rcol[:, 0:2]))
            pr.selfwait(S, tfl4)
            ta4a = pr.emit(S, lambda s: s.activation(
                out=lntmp[:, 0:512], in_=pf0[:], func=ACT.Identity, bias=rcol[:, 1:2]),
                waits=[tr2d] + guard(("lntmp",)) + guard(("lntmpR",)) + guard(("yst",)))
            ta4b = pr.emit(S, lambda s: s.activation(
                out=lntmp[:, 512:1024], in_=pf1[:], func=ACT.Identity, bias=rcol[:, 1:2]))
            to1 = pr.emit(S, lambda s: s.activation(
                out=yln[:, 0:512], in_=lntmp[:, 0:512], func=ACT.Identity,
                scale=rcol[:, 0:1]), waits=guard(("obuf",)))
            to2 = pr.emit(S, lambda s: s.activation(
                out=yln[:, 512:1024], in_=lntmp[:, 512:1024], func=ACT.Identity,
                scale=rcol[:, 0:1]))
            setg(("lntmp",), to2)
            setg(("pf",), ta4b)
            tfin = pr.emit(V, lambda v: v.tensor_tensor(
                out=yln[:], in0=yln[:], in1=x2buf[:], op=ALU.add), waits=[to2])
            setg(("x2",), tfin)
            tod = pr.emit(Q0, lambda s, sl=sl: s.dma_start(
                out=out_d[sl:sl + P, :], in_=yln[:]), waits=[tfin])
            setg(("yln",), tod)
            setg(("obuf",), tod)

        for h in range(NH):
            lbase = h * SEG
            for sub in range(NSUB):
                for i in range(4):
                    lt = sub * 4 + i
                    sl = lbase + lt * P
                    ub = ubuf[lt % 2]
                    xb = xnbuf[lt % 2]
                    tu = pr.emit(Q0, lambda s, sl=sl, ub=ub: s.dma_start(
                        out=ub[:], in_=u_d[sl:sl + P, :]), waits=guard(("ub", lt % 2)))
                    tsq = pr.emit(S, lambda s, ub=ub: s.activation(
                        out=lntmp, in_=ub[:], func=ACT.Square,
                        accum_out=statbuf[:, 1:2]),
                        waits=[tu] + guard(("lntmp",)) + guard(("lntmpR",)) + guard(("yst",)) + guard(("stat",)))
                    tid = pr.emit(S, lambda s, ub=ub: s.activation(
                        out=lntmp, in_=ub[:], func=ACT.Identity,
                        accum_out=statbuf[:, 0:1]))
                    ltok[("red2", 0)] = tid
                    setg(("lntmpR",), tid)
                    pr.emit(S, lambda s: s.activation(
                        out=statbuf[:, 2:3], in_=statbuf[:, 0:1], func=ACT.Square,
                        scale=invDc[:, 0:1]))
                    pr.emit(S, lambda s: s.activation(
                        out=statbuf[:, 3:4], in_=statbuf[:, 2:3], func=ACT.Identity,
                        scale=-1.0))
                    pr.emit(S, lambda s: s.activation(
                        out=mvbuf[:, 0:1], in_=statbuf[:, 0:1], func=ACT.Identity,
                        scale=invDc[:, 0:1]))
                    tsf0 = pr.emit(S, lambda s: s.copy(out=sdmy[:, 2:4], in_=statbuf[:, 2:4]))
                    pr.selfwait(S, tsf0)
                    tvar = pr.emit(S, lambda s: s.activation(
                        out=mvbuf[:, 1:2], in_=statbuf[:, 1:2], func=ACT.Identity,
                        scale=invDc[:, 0:1], bias=statbuf[:, 3:4]))
                    tln1 = pr.emit(S, lambda s: s.activation(
                        out=rcol[:, 0:1], in_=mvbuf[:, 1:2], func=ACT.Ln, bias=epscol[:, 0:1]))
                    pr.emit(S, lambda s: s.activation(
                        out=rcol[:, 0:1], in_=rcol[:, 0:1], func=ACT.Exp, scale=neghalf[:, 0:1]))
                    tng = pr.emit(S, lambda s: s.activation(
                        out=rcol[:, 1:2], in_=mvbuf[:, 0:1], func=ACT.Identity, scale=-1.0))
                    setg(("mv",), tng)
                    tfl = pr.emit(S, lambda s: s.copy(out=sdmy[:, 0:2], in_=rcol[:, 0:2]))
                    pr.selfwait(S, tfl)
                    if dbg and h == 0:
                        pr.emit(S, lambda s, lt=lt: s.copy(
                            out=statbuf[:, 4:6], in_=rcol[:, 0:2]))
                        pr.emit(S, lambda s, lt=lt: s.copy(
                            out=statbuf[:, 6:8], in_=mvbuf[:, 0:2]))
                        pr.emit(S, lambda s, lt=lt: s.copy(
                            out=statbuf[:, 8:10], in_=statbuf[:, 1:3]))
                        pr.emit(Q0, lambda s, lt=lt: s.dma_start(
                            out=dbgscal_d[:, lt * 6:lt * 6 + 6], in_=statbuf[:, 4:10]),
                            waits=[pr.tok("S")])
                    if skip_ln:
                        txn = pr.emit(S, lambda s, ub=ub, xb=xb: s.copy(
                            out=xb[:], in_=ub[:]),
                            waits=[ltok[("red2", 0)]] + guard(("xb", lt % 2)))
                    else:
                        ta1 = pr.emit(S, lambda s, ub=ub: s.activation(
                            out=lntmp, in_=ub[:], func=ACT.Identity, bias=rcol[:, 1:2]),
                            waits=[ltok[("red2", 0)]] + guard(("lntmp",)) + guard(("lntmpR",)) + guard(("yst",)))
                        txn = pr.emit(S, lambda s, xb=xb: s.activation(
                            out=xb[:], in_=lntmp, func=ACT.Identity, scale=rcol[:, 0:1]),
                            waits=guard(("xb", lt % 2)))
                        setg(("lntmp",), txn)
                    setg(("ub", lt % 2), txn)
                    setg(("stat",), tvar)
                    tts = []
                    for half in range(2):
                        pb, pk = ((pt0, "pt0"), (pt1, "pt1"))[half]
                        tths = []
                        for q in range(4):
                            c = half * 4 + q
                            w = [txn, t_idr] + (guard((pk,)) if q == 0 else [])
                            tt = pr.emit(T, lambda t, xb=xb, c=c, q=q, pb=pb: t.transpose(
                                pb[:].bitcast(F32R)[:, q * P:(q + 1) * P],
                                in_=xb[:, c * P:(c + 1) * P],
                                identity=ident_r[:]), waits=w)
                            tths.append(tt)
                            tts.append(tt)
                        gw = []
                        for q in range(4):
                            gw += guard(("mini", half * 4 + q))
                        te = pr.emit(S, lambda s, half=half, i=i, pb=pb: s.copy(
                            out=mini[:, half * 4:(half + 1) * 4, i * P:(i + 1) * P],
                            in_=pb[:].bitcast(F32R)[:, 0:4 * P]
                            .rearrange("p (c q) -> p c q", q=P)),
                            waits=[tths[-1]] + gw)
                        setg((pk,), te)
                        for q in range(4):
                            ltok[("minie", half * 4 + q, i)] = te
                    setg(("xb", lt % 2), *tts)
                for c in range(NDT):
                    wts = [ltok[("minie", c, i)] for i in range(4)] + [t_wxT]
                    if c == 0:
                        wts += guard(("px",))
                    tm = pr.emit(T, lambda t, c=c: t.matmul(
                        px[:], lhsT=wx_lhsT[:, c, :], rhs=mini[:, c, :],
                        start=(c == 0), stop=(c == NDT - 1)), waits=wts)
                    tsp = pr.emit(Q2, lambda s, c=c, sub=sub, lbase=lbase: s.dma_start(
                        out=xnt_d[c, :, lbase + sub * 512:lbase + (sub + 1) * 512],
                        in_=mini[:, c, :].bitcast(F32)),
                        waits=[ltok[("minie", c, i)] for i in range(4)])
                    setg(("mini", c), tsp, tm)
                t_px = pr.tok(T)
                te1 = pr.emit(S, lambda s, sub=sub: s.copy(
                    out=dt_rawT_aug[0:64, sub * 512:(sub + 1) * 512],
                    in_=px[0:64, :]), waits=[t_px] + guard(("draw", sub)))
                te2 = pr.emit(S, lambda s: s.copy(out=bcstage[:], in_=px[64:96, :]),
                              waits=[t_px] + guard(("bcstage",)))
                pr.emit(Q2, lambda s, sub=sub, lbase=lbase: s.dma_start(
                    out=bc_d[0, :, lbase + sub * 512:lbase + (sub + 1) * 512],
                    in_=bcstage[0:16, :]), waits=[te2])
                tbc = pr.emit(Q2, lambda s, sub=sub, lbase=lbase: s.dma_start(
                    out=bc_d[1, :, lbase + sub * 512:lbase + (sub + 1) * 512],
                    in_=bcstage[16:32, :]))
                setg(("bcstage",), tbc)
                setg(("px",), te1, te2)
                ltok[("draw", sub)] = te1
                ltok[("bcd", sub)] = tbc

            def prepassA(dg):
                outa = []
                for k, dt in enumerate(dg):
                    xs = xstrip[k]
                    tx = pr.emit(Q0, lambda s, dt=dt, xs=xs, lbase=lbase: s.dma_start(
                        out=xs[:], in_=xnt_d[dt, :, lbase:lbase + SEG]),
                        waits=[ltok[("bcd", sub)] for sub in range(NSUB)]
                              + guard(("xs", k)))
                    for sub in range(NSUB):
                        bix = (k * NSUB + sub) % 2
                        pzb = (pdtz, pdtz2)[bix]
                        tm = pr.emit(T, lambda t, dt=dt, sub=sub, pzb=pzb: t.matmul(
                            pzb[:], lhsT=wdtT_aug[:, dt * P:(dt + 1) * P],
                            rhs=dt_rawT_aug[:, sub * 512:(sub + 1) * 512],
                            start=True, stop=True),
                            waits=[t_wdtT, ltok[("draw", sub)]] + guard(("pdtz", bix)))
                        ts1 = pr.emit(S, lambda s, k=k, sub=sub, pzb=pzb: s.activation(
                            out=dtT[:, k, sub * 512:(sub + 1) * 512], in_=pzb[:],
                            func=ACT.Exp), waits=[tm] + guard(("dtTr", k)))
                        pr.emit(S, lambda s, k=k, sub=sub: s.activation(
                            out=dtT[:, k, sub * 512:(sub + 1) * 512],
                            in_=dtT[:, k, sub * 512:(sub + 1) * 512],
                            func=ACT.Ln, bias=1.0))
                        setg(("pdtz", bix), ts1)
                        ltok[("drawuse", sub)] = tm
                    t_dtT = pr.tok(S)
                    tw = pr.emit(V, lambda v, k=k, xs=xs: v.tensor_tensor(
                        out=wT[:, k, :], in0=dtT[:, k, :], in1=xs[:], op=ALU.mult),
                        waits=[tx, t_dtT])
                    setg(("dtTr", k), tw)
                    ltok[("w", k)] = tw
                    outa.append((k, dt, xs, tx))
                setg(("draw", 0), *[ltok[("drawuse", s_)] for s_ in range(NSUB)])
                if NSUB > 1:
                    setg(("draw", 1), *[ltok[("drawuse", s_)] for s_ in range(NSUB)])
                return outa

            def prepassB(outa):
                for k, dt, xs, tx in outa:
                    ty0 = pr.emit(V, lambda v, k=k, dt=dt, xs=xs: v.tensor_scalar(
                        out=ybuf[:, k, :], in0=xs[:], scalar1=dcol[:, dt:dt + 1],
                        scalar2=None, op0=ALU.mult),
                        waits=[t_dpar, tx] + guard(("ybuf", k)))
                    setg(("xs", k), ty0)
                    ltok[("y0", k)] = ty0

            def emit_bc(n):
                bb = bcbuf[n % 2]
                return pr.emit(Q2 if n % 2 == 0 else Q0,
                    lambda s, n=n, bb=bb, lbase=lbase: s.dma_start(
                    out=bb[:], in_=bc_d[:, n:n + 1, lbase:lbase + SEG]
                    .rearrange("bc o c -> o bc c").to_broadcast([P, 2, SEG])),
                    waits=[ltok[("bcd", sub)] for sub in range(NSUB)]
                          + guard(("bc", n % 2)))

            def nloop(gi, dg, pre):
                nonlocal prev_carry_S
                prev_carry_local = None
                for n in range(N):
                    bb = bcbuf[n % 2]
                    tb1 = pre[n] if n in pre else emit_bc(n)
                    tb2 = tb1
                    ta = []
                    for k, dt in enumerate(dg):
                        ta.append(pr.emit(S, lambda s, k=k, dt=dt, n=n: s.activation(
                            out=abuf[:, k, :], in_=dtT[:, k, :], func=ACT.Exp,
                            scale=acol[:, dt * N + n:dt * N + n + 1]),
                            waits=[t_acol] + guard(("abuf", k))))
                    tbv = pr.emit(V, lambda v, bb=bb: v.tensor_tensor(
                        out=bybuf[:], in0=wT[:],
                        in1=bb[:, 0:1, :].to_broadcast([P, 4, SEG]), op=ALU.mult),
                        waits=[tb1, ltok[("w", 0)], ltok[("w", 3)]])
                    if n == 0 and h == 0 and gi == 0:
                        pr.selfwait(V, t_carry0)
                    if prev_carry_S is not None:
                        pr.selfwait(V, prev_carry_S)
                    tscan = []
                    for k, dt in enumerate(dg):
                        cix = n * NDT + dt
                        tscan.append(pr.emit(V, lambda v, k=k, cix=cix: v.tensor_tensor_scan(
                            out=abuf[:, k, :], data0=abuf[:, k, :], data1=bybuf[:, k, :],
                            initial=carry[:, cix:cix + 1], op0=ALU.mult, op1=ALU.add),
                            waits=[ta[-1]]))
                    t_scans = pr.tok(V)
                    c0 = n * NDT + dg[0]
                    tcs = pr.emit(S, lambda s, c0=c0: s.copy(
                        out=carry[:, c0:c0 + 4],
                        in_=abuf[:, :, SEG - 1:SEG].rearrange("p k o -> p (k o)")),
                        waits=tscan)
                    prev_carry_local = tcs
                    tyk = pr.emit(V, lambda v, bb=bb: v.tensor_tensor(
                        out=bybuf[:], in0=abuf[:],
                        in1=bb[:, 1:2, :].to_broadcast([P, 4, SEG]), op=ALU.mult),
                        waits=[tb2])
                    setg(("bc", n % 2), tyk)
                    for k in range(4):
                        setg(("abuf", k), tyk, tcs)
                    tya = pr.emit(V, lambda v: v.tensor_tensor(
                        out=ybuf[:], in0=ybuf[:], in1=bybuf[:], op=ALU.add),
                        waits=[ltok[("y0", 0)], ltok[("y0", 3)]])
                    j = gi * N + n
                    if h > 0 and j % 4 == 1:
                        post_state[0] = emit_post_A1(h - 1, j // 4)
                    elif h > 0 and j % 4 == 2:
                        post_state[1] = emit_post_A2(h - 1, j // 4, *post_state[0])
                    elif h > 0 and j % 4 == 3:
                        emit_post_B(h - 1, j // 4, *post_state[1])
                prev_carry_S = prev_carry_local
                for k in range(4):
                    setg(("dtTr", k), pr.tok(S), pr.tok(V))

            def yout(dg):
                t_y_done = pr.tok(V)
                for k, dt in enumerate(dg):
                    yes = []
                    tts_k = []
                    for half in range(2):
                        pb, pk = ((pt0, "pt0"), (pt1, "pt1"))[half]
                        tths = []
                        for q in range(4):
                            lt = half * 4 + q
                            w = [t_y_done, t_id] + (guard((pk,)) if q == 0 else [])
                            tt = pr.emit(T, lambda t, k=k, lt=lt, q=q, pb=pb: t.transpose(
                                pb[:, q * P:(q + 1) * P],
                                in_=ybuf[:, k, lt * P:(lt + 1) * P],
                                identity=ident_f[:]), waits=w)
                            tths.append(tt)
                            tts_k.append(tt)
                        te = pr.emit(S, lambda s, half=half, pb=pb: s.copy(
                            out=ystage[:, half * 4 * P:(half + 1) * 4 * P],
                            in_=pb[:, 0:4 * P]),
                            waits=[tths[-1]] + guard(("yst",)) + guard(("lntmp",)))
                        setg((pk,), te)
                        yes.append(te)
                    ty = pr.emit(Q2, lambda s, dt=dt, lbase=lbase: s.dma_start(
                        out=ylp_d[lbase:lbase + SEG, dt * P:(dt + 1) * P]
                        .rearrange("(lt p) c -> p lt c", p=P),
                        in_=ystage[:].rearrange("p (lt c) -> p lt c", c=P)),
                        waits=yes)
                    setg(("yst",), ty)
                    ltok[("ydma", h, dt)] = ty
                    setg(("ybuf", k), *tts_k)

            if not skip_scan:
                pa0 = prepassA(DGRPS[0])
                pre0 = {0: emit_bc(0), 1: emit_bc(1)}
                prepassB(pa0)
                nloop(0, DGRPS[0], pre0)
                pa1 = prepassA(DGRPS[1])
                pre1 = {0: emit_bc(0), 1: emit_bc(1)}
                yout(DGRPS[0])
                prepassB(pa1)
                nloop(1, DGRPS[1], pre1)
                yout(DGRPS[1])


        for lt in range(NLT):
            _p1 = emit_post_A1(NH - 1, lt)
            _p2 = emit_post_A2(NH - 1, lt, *_p1)
            emit_post_B(NH - 1, lt, *_p2)

        pr.selfwait(Q0)
        pr.selfwait(Q2)

        with nc.Block() as blk:
            def run_stream(name):
                def body(eng):
                    for wl, fn, inc, semn in pr.streams[name]:
                        for (pe, v) in wl:
                            eng.wait_ge(sems[pe], v)
                        if fn is not None:
                            inst = fn(eng)
                            if inc:
                                inst.then_inc(sems[semn], inc)
                return body
            blk.vector(run_stream("V"))
            blk.scalar(run_stream("S"))
            blk.tensor(run_stream("T"))
            blk.sync(run_stream("q0"))
            blk.gpsimd(run_stream("q2"))

    return nc


def prep_inputs(inp, L):
    right = np.ascontiguousarray(np.asarray(inp["L0_ffn_right"], np.float32)
                                 .reshape(E * R, D))
    base = {
        "A_log": np.ascontiguousarray(inp["L0_attn_A_log"], dtype=np.float32),
        "W_x": np.ascontiguousarray(inp["L0_attn_W_x"], dtype=np.float32),
        "W_dt": np.ascontiguousarray(inp["L0_attn_W_dt"], dtype=np.float32),
        "b_dt": np.ascontiguousarray(np.asarray(inp["L0_attn_b_dt"], np.float32)
                                     .reshape(1, D)),
        "D_param": np.ascontiguousarray(np.asarray(inp["L0_attn_D_param"], np.float32)
                                        .reshape(1, D)),
        "W_lin": np.ascontiguousarray(inp["L0_ffn_W_lin"], dtype=np.float32),
        "b_lin": np.ascontiguousarray(np.asarray(inp["L0_ffn_b_lin"], np.float32)
                                      .reshape(1, D)),
        "gate_w": np.ascontiguousarray(inp["L0_ffn_gate_w"], dtype=np.float32),
        "left": np.ascontiguousarray(inp["L0_ffn_left"], dtype=np.float32),
        "right": right,
    }
    u = np.asarray(inp["L0_attn_u"], np.float32)
    maps = []
    for b in range(u.shape[0]):
        m = dict(base)
        m["u"] = np.ascontiguousarray(u[b, :L, :])
        maps.append(m)
    return maps


def kernel(**inputs):
    from concourse.bass_utils import run_bass_kernel_spmd
    L = 4096
    nc = build_nc(L=L, SEG=1024)
    maps = prep_inputs(inputs, L)
    res = run_bass_kernel_spmd(nc, maps, core_ids=list(range(8)), trace=False)
    return np.stack([np.asarray(r["out"], np.float32) for r in res.results])
